# version 16
# baseline (speedup 1.0000x reference)
import math
from contextlib import ExitStack

import numpy as np
import ml_dtypes

import concourse.bass as bass
import concourse.mybir as mybir
import concourse.library_config as libcfg
from concourse.bass_utils import run_bass_kernel_spmd
from concourse.bass_core_rust import IndirectOffsetOnAxis

F32 = mybir.dt.float32
BF16 = mybir.dt.bfloat16
FP16 = mybir.dt.float16
I16 = mybir.dt.int16
I32 = mybir.dt.int32
AF = mybir.ActivationFunctionType
ALU = mybir.AluOpType
AX = mybir.AxisListType
BF16NP = ml_dtypes.bfloat16
ts = bass.ts


class CFG:
    def __init__(self, T=4096):
        self.B = 4
        self.T = T
        self.TO = T // 2
        self.D = 1024
        self.H = 16
        self.HD = 64
        self.RD = 32
        self.KVR = 256
        self.QR = 512
        self.E = 8
        self.I = 4096
        self.TT = self.B * T
        self.CAP = math.ceil(self.TT / self.E * 1.25)
        assert self.CAP % 128 == 0
        self.DCH = self.D // 128
        self.KVCH = self.KVR // 128
        self.QRCH = self.QR // 128
        self.SPB = T // 512
        self.SPO = self.SPB // 2
        self.NT_B = T // 128
        self.NT_O = self.TO // 128
        self.CAPT = self.CAP // 128
        self.ICH = self.I // 128
        self.ISL = self.I // 512
        self.SCH = []
        r = self.CAP
        while r > 0:
            c = min(512, r)
            self.SCH.append(c)
            r -= c
        self.QD = self.HD + self.RD
        self.MARGIN = 30.0
        self.KB = [8 * j + 8 for j in range(self.SPO)]


def _rope_tables(T, dim):
    inv_freq = 1.0 / 10000.0 ** (np.arange(0, dim, 2, dtype=np.float32) / dim)
    t = np.arange(T, dtype=np.float32)
    freqs = np.outer(t, inv_freq)
    emb = np.concatenate([freqs, freqs], -1)
    return np.cos(emb), np.sin(emb)


def host_consts(cfg, core):
    b, qh = core // 2, core % 2
    c = {}
    c["ident"] = np.eye(128, dtype=np.float32)
    c["identb"] = np.eye(128, dtype=np.float32).astype(BF16NP)
    tri = (np.arange(128)[:, None] <= np.arange(128)[None, :]).astype(np.float32)
    c["tri"] = tri.astype(BF16NP)
    mt = np.zeros((8, 128, 512), np.float32)
    for mi in range(8):
        if qh == 0:
            if mi < 4:
                mt[mi] = (np.arange(128)[:, None] <= (np.arange(512)[None, :] - 128 * mi))
        else:
            if mi < 4:
                mt[mi] = 1.0
            else:
                mt[mi] = (np.arange(128)[:, None] <= (np.arange(512)[None, :] - 128 * (mi - 4)))
    c["masktbl"] = np.ascontiguousarray(mt.transpose(1, 0, 2)).astype(BF16NP)
    cos, sin = _rope_tables(cfg.T, cfg.RD)
    c["cos4"] = np.ascontiguousarray(np.tile(cos.T, (4, 1))).astype(BF16NP)
    c["sin4"] = np.ascontiguousarray(np.tile(sin.T, (4, 1))).astype(BF16NP)
    h2 = cfg.RD // 2
    R = np.zeros((cfg.RD, cfg.RD), np.float32)
    for dd in range(h2):
        R[dd, dd + h2] = -1.0
        R[dd + h2, dd] = 1.0
    R4T = np.zeros((128, 128), np.float32)
    for g in range(4):
        R4T[g * 32:(g + 1) * 32, g * 32:(g + 1) * 32] = R.T
    c["r4t"] = R4T.astype(BF16NP)
    c["ones_colb"] = np.ones((128, 1), np.float32).astype(BF16NP)
    c["ones_rowb"] = np.ones((1, 128), np.float32).astype(BF16NP)
    c["ones_col"] = np.ones((128, 1), np.float32)
    c["ones_row"] = np.ones((1, 128), np.float32)
    e_i = np.tile(np.arange(cfg.E, dtype=np.float32), (128, 1))
    c["eidx1000"] = (e_i + 1000.0).astype(np.float32)
    c["eidxp1"] = (e_i + 1.0).astype(np.float32)
    pm = np.zeros((4 * cfg.SPB, cfg.SPO), np.float32)
    for j in range(cfg.SPO):
        g = b * cfg.SPB + 2 * j + qh
        pm[:g, j] = 1.0
    c["presum"] = pm
    pl = np.zeros((cfg.SPO, 4 * cfg.SPB), np.float32)
    for j in range(cfg.SPO):
        pl[j, b * cfg.SPB + 2 * j + qh] = 1.0
    c["placemat"] = pl
    agp1 = (cfg.TO * core + 1 + np.arange(cfg.TO, dtype=np.int32)).astype(np.int16)
    c["agp1"] = np.tile(agp1, (16, 1))
    hs = np.zeros((16, cfg.H * cfg.HD), np.float32)
    for h in range(cfg.H):
        hs[h, h * cfg.HD:(h + 1) * cfg.HD] = 1.0
    c["headsel"] = hs
    c["cmeta"] = np.array([[qh, cfg.SPB * b + qh, core, 0, 0, 0, 0, 0]], np.int32)
    nt_b, nt_o = cfg.NT_B, cfg.NT_O
    oid = np.zeros((1, nt_b), np.int32)
    for bt in range(nt_b):
        sp, tl = bt // 4, bt % 4
        oid[0, bt] = ((sp - qh) // 2) * 4 + tl if sp % 2 == qh else nt_o
    c["oid_tbl"] = oid
    osp = np.zeros((1, cfg.SPB), np.int32)
    for sp in range(cfg.SPB):
        osp[0, sp] = (sp - qh) // 2 if sp % 2 == qh else cfg.SPO
    c["osp_tbl"] = osp
    return c


class Sem:
    def __init__(self, stack, nc, name):
        self.h = stack.enter_context(nc.semaphore(name))
        self.v = 0

    def t(self, inst, n=1):
        inst.then_inc(self.h, n)
        self.v += n
        return self.v


class K:
    def __init__(self, cfg):
        self.cfg = cfg
        self.nc = bass.Bass("TRN2", target_bir_lowering=False)
        self.st = ExitStack()
        self.d = {}
        self.last_wait = {}
        self._freeq = []

    def DI(self, name, shape, dt=F32):
        t = self.nc.dram_tensor(name, list(shape), dt, kind="ExternalInput")
        self.d[name] = t
        return t

    def snap(self):
        return {id(s): s.v for s in self.all_sems}

    TSP_OPS = frozenset({
        "tensor_scalar", "tensor_single_scalar", "tensor_scalar_add",
        "tensor_scalar_sub", "tensor_scalar_mul", "tensor_scalar_max",
        "tensor_scalar_min", "scalar_tensor_tensor", "activation",
    })
    NOEMIT = frozenset({
        "wait_ge", "sem_inc", "value_load", "snap", "free_register",
        "alloc_register", "load_library", "maybe_wait_ge",
    })

    MANUAL_SEM = frozenset({
        "dma_start", "indirect_dma_start", "dma_start_transpose",
        "collective_compute", "local_scatter", "dma_reset", "trigger_dma",
    })

    class _RawProxy:

        def __init__(self, outer, eng, own):
            self._o, self._e, self._own = outer, eng, own
            self._lastv = own.v

        def __getattr__(self, name):
            attr = getattr(self._e, name)
            if not callable(attr) or name in K.NOEMIT:
                return attr

            def call(*a, **kw):
                if self._own.v != self._lastv:
                    self._e.wait_ge(self._own.h, self._own.v)
                    self._lastv = self._own.v
                r = attr(*a, **kw)
                return r
            return call

    class _FreeProxy:

        def __init__(self, outer, eng, own):
            self._o, self._e, self._own = outer, eng, own

        def __getattr__(self, name):
            return getattr(self._e, name)

    class _Proxy:

        def __init__(self, outer, eng, own):
            self._o, self._e, self._own = outer, eng, own
            self._after_tsp = False

        def __getattr__(self, name):
            attr = getattr(self._e, name)
            if not callable(attr):
                return attr
            if name in K.NOEMIT or name in K.MANUAL_SEM:
                return attr

            def call(*a, **kw):
                self._e.wait_ge(self._own.h, self._own.v)
                r = attr(*a, **kw)
                if hasattr(r, "then_inc"):
                    r.then_inc(self._own.h, 1)
                    self._own.v += 1
                return r
            return call

    def sec(self, eng_name, fn, snap=None, raw=False):
        own = self.eng_sem[eng_name]
        lw = self.last_wait.setdefault(eng_name, {})
        own_target = own.v

        def wrapped(eng):
            if own_target > 0:
                eng.wait_ge(own.h, own_target)
            for s in self.all_sems:
                if s is own:
                    continue
                tgt = s.v if snap is None else snap[id(s)]
                if tgt > 0 and lw.get(id(s), -1) < tgt:
                    eng.wait_ge(s.h, tgt)
                    lw[id(s)] = tgt
            if raw == "free" or (raw and eng_name == "tensor"):
                px = K._FreeProxy(self, eng, own)
            elif raw:
                px = K._RawProxy(self, eng, own)
            else:
                px = K._Proxy(self, eng, own)
            fn(px)
            for h in self._freeq:
                try:
                    eng.free_register(h)
                except Exception:
                    pass
            self._freeq = []
            eng.wait_ge(own.h, own.v)
            eng.sem_inc(own.h, 1)
        getattr(self.blk, eng_name)(wrapped)
        own.v += 1
        for k2 in self.last_wait:
            if k2 != eng_name:
                self.last_wait[k2].pop(id(own), None)

    def build(self):
        cfg, nc = self.cfg, self.nc
        T, TO, D, E, I, CAP = cfg.T, cfg.TO, cfg.D, cfg.E, cfg.I, cfg.CAP

        self.DI("x_b", (T, D))
        self.DI("x_own", (TO, D))
        self.DI("W_kv", (D, cfg.KVR)); self.DI("g_kv", (cfg.KVR,))
        self.DI("W_kdec", (cfg.KVR, D)); self.DI("W_vdec", (cfg.KVR, D))
        self.DI("W_q", (D, cfg.QR)); self.DI("W_qdec", (cfg.QR, D))
        self.DI("W_krope", (D, cfg.H * cfg.RD)); self.DI("W_qrope", (cfg.QR, cfg.H * cfg.RD))
        self.DI("W_o", (D, D))
        self.DI("Wr_T", (D, E)); self.DI("b_r", (E,))
        self.DI("Wi", (D, I)); self.DI("Wo", (I, D))
        for nm, sh, dt in [
            ("ident", (128, 128), F32), ("identb", (128, 128), BF16),
            ("tri", (128, 128), BF16),
            ("masktbl", (128, 8, 512), BF16),
            ("cos4", (128, T), BF16), ("sin4", (128, T), BF16),
            ("r4t", (128, 128), BF16), ("ones_colb", (128, 1), BF16),
            ("ones_rowb", (1, 128), BF16), ("ones_col", (128, 1), F32),
            ("ones_row", (1, 128), F32),
            ("eidx1000", (128, E), F32), ("eidxp1", (128, E), F32),
            ("presum", (4 * cfg.SPB, cfg.SPO), F32),
            ("placemat", (cfg.SPO, 4 * cfg.SPB), F32), ("agp1", (16, TO), I16),
            ("headsel", (16, cfg.H * cfg.HD), F32), ("cmeta", (1, 8), I32),
            ("oid_tbl", (1, cfg.NT_B), I32), ("osp_tbl", (1, cfg.SPB), I32),
        ]:
            self.DI(nm, sh, dt)
        self.y_out = nc.dram_tensor("y", [TO, D], F32, kind="ExternalOutput")

        self.xn_spill = nc.dram_tensor("xn_spill", [TO + 128, D], BF16)
        self.xn_bounce = nc.dram_tensor("xn_bounce", [T, D], BF16)
        self.q_dram = nc.dram_tensor("q_dram", [cfg.SPB, 128, cfg.QRCH * 512], BF16)
        self.qr_dram = nc.dram_tensor("qr_dram", [cfg.SPB, 128, 4 * 512], BF16)
        self.kr_dram = nc.dram_tensor("kr_dram", [128, 4, T], BF16)
        self.qro_dram = nc.dram_tensor("qro_dram", [128, 4, TO], BF16)
        self.lg_dram = nc.dram_tensor("lg_dram", [cfg.SPB, 128, 4 * E], F32)
        self.ag1_out = nc.dram_tensor("ag1_out", [8 * TO, D], BF16, addr_space="Shared")
        self.ar1_in = nc.dram_tensor("ar1_in", [4 * cfg.SPB, E], F32)
        self.ar1_out = nc.dram_tensor("ar1_out", [4 * cfg.SPB, E], F32)
        self.tbi_in = nc.dram_tensor("tbi_in", [16, CAP], I16)
        self.tbi_out = nc.dram_tensor("tbi_out", [128, CAP], I16, addr_space="Shared")
        self.tbg_in = nc.dram_tensor("tbg_in", [16, CAP], FP16)
        self.tbg_out = nc.dram_tensor("tbg_out", [128, CAP], FP16, addr_space="Shared")
        self.aoT_sp = nc.dram_tensor("aoT_sp", [D, TO], BF16)
        self.y_mid = nc.dram_tensor("y_mid", [TO, D], F32)
        self.wi_t = nc.dram_tensor("wi_t", [cfg.ISL, 128, cfg.DCH, 512], BF16)
        self.wo_t = nc.dram_tensor("wo_t", [2 * cfg.ISL, 128, 4, 512], BF16)
        self.ag2_in = nc.dram_tensor("ag2_in", [CAP, D], F32)
        self.ag2_out = nc.dram_tensor("ag2_out", [8 * CAP, D], F32, addr_space="Shared")

        S0 = self.st
        SB = lambda n, sh, dt: S0.enter_context(nc.sbuf_tensor(n, list(sh), dt))
        self.b_ident = SB("b_ident", (128, 128), F32)
        self.b_identb = SB("b_identb", (128, 128), BF16)
        self.b_tri = SB("b_tri", (128, 128), BF16)
        self.b_mask = SB("b_mask", (128, 8, 512), BF16)
        self.b_r4t = SB("b_r4t", (128, 128), BF16)
        self.b_ocb = SB("b_ocb", (128, 1), BF16)
        self.b_orb = SB("b_orb", (1, 128), BF16)
        self.b_oc = SB("b_oc", (128, 1), F32)
        self.b_or = SB("b_or", (1, 128), F32)
        self.b_e1000 = SB("b_e1000", (128, E), F32)
        self.b_ep1 = SB("b_ep1", (128, E), F32)
        self.b_presum = SB("b_presum", (4 * cfg.SPB, cfg.SPO), F32)
        self.b_place = SB("b_place", (cfg.SPO, 4 * cfg.SPB), F32)
        self.b_agp1 = SB("b_agp1", (16, TO), I16)
        self.b_hsel = SB("b_hsel", (16, cfg.H * cfg.HD), F32)
        self.b_cmeta = {e: SB(f"b_cmeta_{e}", (1, 8), I32)
                        for e in ("sync", "gpsimd", "vector", "tensor", "scalar")}

        self.b_gkv = SB("b_gkv", (128, cfg.KVCH), F32)
        self.b_br = SB("b_br", (E, 1), F32)

        self.r_lg = SB("r_lg", (128, cfg.NT_O, E), F32)
        self.r_w = SB("r_w", (128, cfg.NT_O, E), F32)
        self.r_tk = SB("r_tk", (128, cfg.NT_O, E), F32)
        self.r_M = SB("r_M", (128, cfg.NT_O, E), BF16)
        self.r_C = SB("r_C", (128, cfg.NT_O, E), F32)
        self.g_row = SB("g_row", (128, cfg.NT_O, 2), F32)
        self.g_val = SB("g_val", (128, cfg.NT_O, 2), F32)
        self.e_idx = SB("e_idx", (128, cfg.CAPT), I32)
        self.e_gate = SB("e_gate", (128, cfg.CAPT), F32)
        self.s_recip = SB("s_recip", (16, TO), F32)
        self.scr = SB("scr", (128, 32), F32)
        self.rrow = SB("rrow", (1, 4 * E), F32)
        self.rrowb = SB("rrowb", (1, 2 * E), BF16)
        self.spsum = SB("spsum", (1, cfg.SPO * E), F32)
        self.spsum2 = SB("spsum2", (cfg.SPO, E), F32)
        self.ar_sb = SB("ar_sb", (4 * cfg.SPB, E), F32)
        self.sbase = SB("sbase", (1, cfg.SPO * E), F32)

        PS = lambda n, sh: self.st.enter_context(nc.psum_tensor(n, list(sh), F32))
        self.psG3a = PS("psG3a", (128, 1536))
        self.psG3b = PS("psG3b", (128, 1536))
        self.psC1 = PS("psC1", (128, 512))
        self.psT = PS("psT", (128, 128))
        self.ps = [self.psG3a[:, 0:512], self.psG3a[:, 512:1024], self.psG3a[:, 1024:1536],
                   self.psG3b[:, 0:512], self.psG3b[:, 512:1024], self.psG3b[:, 1024:1536],
                   self.psC1[:, 0:512]]

        st = self.st
        self.sD = Sem(st, nc, "sD")
        self.sG = Sem(st, nc, "sG")
        self.sP = Sem(st, nc, "sP")
        self.sV = Sem(st, nc, "sV")
        self.sS = Sem(st, nc, "sS")
        self.sQ = Sem(st, nc, "sQ")
        self.sC = Sem(st, nc, "sC")
        self.sX = Sem(st, nc, "sX")
        self.all_sems = [self.sD, self.sG, self.sP, self.sV, self.sS, self.sQ, self.sC, self.sX]
        self.eng_sem = {"sync": self.sD, "gpsimd": self.sQ, "tensor": self.sP,
                        "vector": self.sV, "scalar": self.sS}

        with nc.Block() as blk:
            self.blk = blk
            with nc.named_scope("phL"):
                self.ph_load()
            with ExitStack() as RA:
                SB2 = lambda n, sh, dt: RA.enter_context(nc.sbuf_tensor(n, list(sh), dt))
                self.kv_nT = SB2("kv_nT", (128, cfg.KVCH, T), BF16)
                self.q_cT = SB2("q_cT", (128, cfg.QRCH, TO), BF16)
                with nc.named_scope("phR"):
                    self.ph_R()
                with nc.named_scope("phRoute"):
                    self.ph_route()
                with nc.named_scope("phAttn"):
                    self.ph_attn()
            with nc.named_scope("phWo"):
                self.ph_wo()
            with nc.named_scope("phMoe"):
                self.ph_moe()
            with nc.named_scope("phFinal"):
                self.ph_final()
        self.st.close()
        mybir.codegen_inst_isa_subclasses(nc)
        return nc

    def _vload(self, eng, ap, lo, hi):
        val = eng.value_load(ap, min_val=lo, max_val=hi)
        try:
            self._freeq.append(val.val)
        except Exception:
            pass
        return val

    def ph_load(self):
        d = self.d

        def f(s):
            for nm, dst in [
                ("ident", self.b_ident), ("identb", self.b_identb),
                ("tri", self.b_tri), ("masktbl", self.b_mask),
                ("r4t", self.b_r4t), ("ones_colb", self.b_ocb), ("ones_rowb", self.b_orb),
                ("ones_col", self.b_oc), ("ones_row", self.b_or),
                ("eidx1000", self.b_e1000), ("eidxp1", self.b_ep1),
                ("presum", self.b_presum), ("placemat", self.b_place),
                ("agp1", self.b_agp1),
                ("headsel", self.b_hsel),
            ]:
                self.sD.t(s.dma_start(out=dst[:], in_=d[nm].ap()), 16)
            self.sD.t(s.dma_start(out=self.b_br[:, :],
                                  in_=d["b_r"].ap().rearrange("(e o) -> e o", o=1)), 16)
            for dd in self.b_cmeta.values():
                self.sD.t(s.dma_start(out=dd[:, :], in_=d["cmeta"].ap()), 16)

            self.sD.t(s.dma_start(out=self.b_gkv[:, :],
                                  in_=d["g_kv"].ap().rearrange("(c p) -> p c", p=128),
                                  allow_slow_non_contiguous=True), 16)
        self.sec("sync", f)

    def ph_R(self):
        cfg, d = self.cfg, self.d
        T, TO, D, E = cfg.T, cfg.TO, cfg.D, cfg.E
        DCH, KVCH, QRCH, SPB = cfg.DCH, cfg.KVCH, cfg.QRCH, cfg.SPB
        sP, sV = self.sP, self.sV
        ps = self.ps

        with ExitStack() as RS:
            SB = lambda n, sh, dt: RS.enter_context(self.nc.sbuf_tensor(n, list(sh), dt))
            w_kv = SB("w_kv", (128, DCH, cfg.KVR), BF16)
            w_q = SB("w_q", (128, DCH, cfg.QR), BF16)
            w_kro = SB("w_kro", (128, DCH, cfg.H * cfg.RD), BF16)
            w_qro = SB("w_qro", (128, QRCH, cfg.H * cfg.RD), BF16)
            w_rt = SB("w_rt", (128, DCH, E), BF16)
            x_sp = SB("x_sp", (128, 2, 4, D), F32)
            sqs = SB("sqs", (128, D), F32)
            xn_bf = SB("xn_bf", (128, 2, 4, D), BF16)
            lnsq = SB("lnsq", (128, 2, 4), F32)
            lnm = SB("lnm", (128, 2, 4), F32)
            m_col = SB("m_col", (128, 2, 4), F32)
            r_col = SB("r_col", (128, 2, 4), F32)
            vtmp = SB("vtmp", (128, 4), F32)
            xfm = SB("xfm", (128, DCH, 512), BF16)
            kvsp = SB("kvsp", (128, KVCH, 512), BF16)
            sqv = SB("sqv", (128, KVCH, 512), F32)
            rmsrow = SB("rmsrow", (1, 512), F32)
            q_sp = SB("q_sp", (128, 2, QRCH, 512), BF16)
            cr_t = SB("cr_t", (128, 2, 512), BF16)
            sr_t = SB("sr_t", (128, 2, 512), BF16)
            rp_a = SB("rp_a", (128, 2, 512), BF16)
            rp_b = SB("rp_b", (128, 512), F32)
            rp_c = SB("rp_c", (128, 512), F32)
            lgev = SB("lgev", (E, 512), F32)
            qr_sp = SB("qr_sp", (128, 2, 4, 512), BF16)
            kr_sp = SB("kr_sp", (128, 2, 4, 512), BF16)
            lg_sp = SB("lg_sp", (128, 2, 4, E), F32)

            def f(g):
                for nm, dst in [("W_kv", w_kv), ("W_q", w_q), ("W_krope", w_kro),
                                ("W_qrope", w_qro), ("Wr_T", w_rt)]:
                    self.sG.t(g.dma_start(out=dst[:],
                                          in_=d[nm].ap().rearrange("(c p) o -> p c o", p=128)), 16)
            self.sec("gpsimd", f)

            def f(v):
                v.tensor_scalar(w_qro[:, :, :], w_qro[:, :, :],
                                1.0 / math.sqrt(float(cfg.QD)), None, ALU.mult)
            self.sec("vector", f)

            sG, sD, sS = self.sG, self.sD, self.sS
            gsn0 = self.snap()
            norm_done = {}
            x_dma = {}
            cs_dma = {}
            spill_done = {}
            evac = {}

            def emit_x(sp, wait_v):
                def f(s, sp=sp, wait_v=wait_v):
                    if wait_v:
                        s.wait_ge(sV.h, wait_v)
                    self.sD.t(s.dma_start(
                        out=x_sp[:, sp % 2, :, :],
                        in_=d["x_b"].ap()[sp*512:(sp+1)*512, :]
                        .rearrange("(a p) b -> p a b", p=128)), 16)
                    x_dma[sp] = sD.v
                    self.sD.t(s.dma_start(out=cr_t[:, sp % 2, :],
                                          in_=d["cos4"].ap()[:, sp*512:(sp+1)*512]), 16)
                    self.sD.t(s.dma_start(out=sr_t[:, sp % 2, :],
                                          in_=d["sin4"].ap()[:, sp*512:(sp+1)*512]), 16)
                    cs_dma[sp] = sD.v
                self.sec("sync", f, snap=gsn0, raw="free")

            def emit_stats_sq(sp):
                def f(sc, sp=sp):
                    sc.wait_ge(sD.h, x_dma[sp])
                    for tl in range(4):
                        self.sS.t(sc.activation(out=sqs[:, :], in_=x_sp[:, sp % 2, tl, :],
                                                func=AF.Square,
                                                accum_out=lnsq[:, sp % 2, tl:tl+1]))
                self.sec("scalar", f, snap=gsn0, raw="free")
                return sS.v

            def emit_stats_vec(sp, sq_val):
                def f(v, sp=sp, sq_val=sq_val):
                    v.wait_ge(sD.h, x_dma[sp])
                    par = sp % 2
                    for tl in range(4):
                        sV.t(v.tensor_reduce(lnm[:, par, tl:tl+1], x_sp[:, par, tl, :],
                                             AX.X, ALU.add))
                    v.wait_ge(sS.h, sq_val)
                    sV.t(v.tensor_scalar(m_col[:, par, :], lnm[:, par, :], 1.0 / D,
                                         None, ALU.mult))
                    sV.t(v.tensor_scalar(vtmp[:, :], lnsq[:, par, :], 1.0 / D,
                                         None, ALU.mult))
                    v.wait_ge(sV.h, sV.v)
                    sV.t(v.tensor_mul(r_col[:, par, :], m_col[:, par, :], m_col[:, par, :]))
                    v.wait_ge(sV.h, sV.v)
                    sV.t(v.tensor_sub(vtmp[:, :], vtmp[:, :], r_col[:, par, :]))
                    v.wait_ge(sV.h, sV.v)
                    sV.t(v.tensor_scalar(vtmp[:, :], vtmp[:, :], 1e-5, None, ALU.add))
                    v.wait_ge(sV.h, sV.v)
                    sV.t(v.reciprocal(r_col[:, par, :], vtmp[:, :]))
                self.sec("vector", f, snap=gsn0, raw="free")
                return sV.v

            def emit_stats_rst(sp, var_val):
                def f(sc, sp=sp, var_val=var_val):
                    sc.wait_ge(sV.h, var_val)
                    self.sS.t(sc.activation(out=r_col[:, sp % 2, :], in_=r_col[:, sp % 2, :],
                                            func=AF.Sqrt))
                self.sec("scalar", f, snap=gsn0, raw="free")
                return sS.v

            def emit_norm(sp, rst_val, xn_spill_val):
                vals = []

                def f(v, sp=sp, rst_val=rst_val):
                    par = sp % 2
                    v.wait_ge(sS.h, rst_val)
                    if xn_spill_val:
                        v.wait_ge(sG.h, xn_spill_val)
                    for tl in range(4):
                        sV.t(v.tensor_scalar(xn_bf[:, par, tl, :], x_sp[:, par, tl, :],
                                             m_col[:, par, tl:tl+1], r_col[:, par, tl:tl+1],
                                             ALU.subtract, ALU.mult))
                        vals.append(sV.v)
                self.sec("vector", f, snap=gsn0, raw="free")
                norm_done[sp] = list(vals)

            emit_x(0, 0)
            sq0 = emit_stats_sq(0)
            var0 = emit_stats_vec(0, sq0)
            rst0 = emit_stats_rst(0, var0)
            emit_norm(0, rst0, 0)
            emit_x(1, 0)

            for sp in range(SPB):
                par = sp % 2
                p0, v0 = sP.v, sV.v

                def f(t, sp=sp, par=par, v0=v0):
                    for i in range(32):
                        tl, fc = divmod(i, 8)
                        if fc == 0:
                            t.wait_ge(sV.h, norm_done[sp][tl])
                        if i >= 2:
                            t.wait_ge(sV.h, v0 + i - 1)
                        dst = (self.psT[:, :] if i % 2 == 0
                               else ps[2]).bitcast(BF16)[0:128, 0:128]
                        sP.t(t.transpose(dst, xn_bf[:, par, tl, fc*128:(fc+1)*128],
                                         self.b_identb[:, :]))
                self.sec("tensor", f, snap=gsn0, raw="free")

                def f(v, p0=p0):
                    for i in range(32):
                        tl, fc = divmod(i, 8)
                        v.wait_ge(sP.h, p0 + i + 1)
                        src = (self.psT[:, :] if i % 2 == 0
                               else ps[2]).bitcast(BF16)[0:128, 0:128]
                        sV.t(v.tensor_copy(xfm[:, fc, tl*128:(tl+1)*128], src))
                self.sec("vector", f, snap=gsn0, raw="free")
                te_last = sV.v

                p0, v0 = sP.v, sV.v

                def f(t, v0=v0, te_last=te_last):
                    t.wait_ge(sV.h, te_last)
                    if ("kvn", 1) in evac:
                        t.wait_ge(sV.h, evac[("kvn", 1)])
                    for kvc in range(KVCH):
                        for fc in range(DCH):
                            inst = t.matmul(ps[kvc][:, :], w_kv[:, fc, kvc*128:(kvc+1)*128],
                                            xfm[:, fc, :], start=(fc == 0), stop=(fc == DCH - 1))
                        sP.t(inst)
                self.sec("tensor", f, snap=gsn0, raw="free")

                def f(v, p0=p0):
                    for kvc in range(KVCH):
                        v.wait_ge(sP.h, p0 + kvc + 1)
                        sV.t(v.tensor_copy(kvsp[:, kvc, :], ps[kvc][:, :]))
                self.sec("vector", f, snap=gsn0, raw="free")
                kvsp_done = sV.v

                def f(sc, sp=sp, kvsp_done=kvsp_done):
                    sc.wait_ge(sV.h, kvsp_done)
                    for kvc in range(KVCH):
                        self.sS.t(sc.activation(out=sqv[:, kvc, :], in_=kvsp[:, kvc, :],
                                                func=AF.Square))
                self.sec("scalar", f, snap=gsn0, raw="free")
                sqv_done = sS.v
                sqn = emit_stats_sq(sp + 1) if sp + 1 < SPB else 0

                p0, v0 = sP.v, sV.v

                def f(t, sp=sp, kvsp_done=kvsp_done):
                    if ("qev", 3) in evac:
                        t.wait_ge(sV.h, evac[("qev", 3)])
                    for qc in range(QRCH):
                        if qc < 2:
                            t.wait_ge(sV.h, kvsp_done)
                        for fc in range(DCH):
                            inst = t.matmul(ps[qc % 2][:, :], w_q[:, fc, qc*128:(qc+1)*128],
                                            xfm[:, fc, :], start=(fc == 0), stop=(fc == DCH - 1))
                        sP.t(inst)
                self.sec("tensor", f, snap=gsn0, raw="free")

                def f(v, sp=sp, par=par, p0=p0):
                    if ("qspill", sp - 2) in spill_done:
                        v.wait_ge(sG.h, spill_done[("qspill", sp - 2)])
                    for qc in range(QRCH):
                        v.wait_ge(sP.h, p0 + qc + 1)
                        sV.t(v.tensor_copy(q_sp[:, par, qc, :], ps[qc % 2][:, :]))
                        evac[("qev", qc)] = sV.v
                self.sec("vector", f, snap=gsn0, raw="free")

                if sp + 1 < SPB:
                    varn = emit_stats_vec(sp + 1, sqn)

                p0 = sP.v

                def f(t, sp=sp, sqv_done=sqv_done):
                    t.wait_ge(sS.h, sqv_done)
                    if ("kvn", 1) in evac:
                        t.wait_ge(sV.h, evac[("kvn", 1)])
                    for kvc in range(KVCH):
                        inst = t.matmul(ps[5][0:1, :], self.b_oc[:, 0:1], sqv[:, kvc, :],
                                        start=(kvc == 0), stop=(kvc == KVCH - 1))
                    sP.t(inst)
                self.sec("tensor", f, snap=gsn0, raw="free")

                def f(v, p0=p0):
                    v.wait_ge(sP.h, p0 + 1)
                    sV.t(v.tensor_scalar(rmsrow[:, :], ps[5][0:1, :], 1.0 / cfg.KVR, 1e-6,
                                         ALU.mult, ALU.add))
                    v.wait_ge(sV.h, sV.v)
                    sV.t(v.reciprocal(rmsrow[:, :], rmsrow[:, :]))
                self.sec("vector", f, snap=gsn0, raw="free")
                rms_scaled = sV.v

                def f(sc, rms_scaled=rms_scaled):
                    sc.wait_ge(sV.h, rms_scaled)
                    self.sS.t(sc.activation(out=rmsrow[:, :], in_=rmsrow[:, :], func=AF.Sqrt))
                self.sec("scalar", f, snap=gsn0, raw="free")
                rmsr_done = sS.v
                rstn = emit_stats_rst(sp + 1, varn) if sp + 1 < SPB else 0

                pe_at = {}
                c = 0
                for g4 in range(4):
                    c += 1
                    pe_at[("proj", g4)] = c
                    if g4 >= 1:
                        c += 1
                        pe_at[("rot", g4 - 1)] = c
                c += 1
                pe_at[("rot", 3)] = c

                def emit_rope_pair(wsrc, nw, dstbuf, kind):
                    p0, v0 = sP.v, sV.v
                    rpa_at = lambda g4: v0 + 4 * g4 + 1
                    comb_at = lambda g4: v0 + 4 * g4 + 4

                    def f(t, sp=sp, par=par, wsrc=wsrc, nw=nw):
                        for g4 in range(4):
                            if g4 >= 2:
                                t.wait_ge(sV.h, rpa_at(g4 - 2))
                            for w in range(nw):
                                rhs = xfm[:, w, :] if nw == DCH else q_sp[:, par, w, :]
                                inst = t.matmul(ps[g4 % 2][:, :], wsrc[:, w, g4*128:(g4+1)*128],
                                                rhs, start=(w == 0), stop=(w == nw - 1))
                            sP.t(inst)
                            if g4 >= 1:
                                t.wait_ge(sV.h, rpa_at(g4 - 1))
                                if g4 >= 3:
                                    t.wait_ge(sV.h, comb_at(g4 - 3))
                                sP.t(t.matmul(ps[3 + ((g4 - 1) % 2)][:, :], self.b_r4t[:, :],
                                              rp_a[:, (g4 - 1) % 2, :], start=True, stop=True))
                        t.wait_ge(sV.h, rpa_at(3))
                        t.wait_ge(sV.h, comb_at(1))
                        sP.t(t.matmul(ps[3 + (3 % 2)][:, :], self.b_r4t[:, :],
                                      rp_a[:, 3 % 2, :], start=True, stop=True))
                    self.sec("tensor", f, snap=gsn0, raw="free")

                    def f(v, sp=sp, par=par, p0=p0, dstbuf=dstbuf, kind=kind):
                        if (kind, sp - 2) in spill_done:
                            v.wait_ge(sG.h, spill_done[(kind, sp - 2)])
                        for g4 in range(4):
                            v.wait_ge(sP.h, p0 + pe_at[("proj", g4)])
                            sV.t(v.tensor_copy(rp_a[:, g4 % 2, :], ps[g4 % 2][:, :]))
                            v.wait_ge(sP.h, p0 + pe_at[("rot", g4)])
                            sV.t(v.tensor_mul(rp_b[:, :], ps[3 + (g4 % 2)][:, :],
                                              sr_t[:, par, :]))
                            sV.t(v.tensor_mul(rp_c[:, :], rp_a[:, g4 % 2, :], cr_t[:, par, :]))
                            v.wait_ge(sV.h, sV.v)
                            sV.t(v.tensor_add(dstbuf[:, par, g4, :], rp_c[:, :], rp_b[:, :]))
                    self.sec("vector", f, snap=gsn0, raw="free")
                    return comb_at(3)

                combk = emit_rope_pair(w_kro, DCH, kr_sp, "krspill")

                p0 = sP.v

                def f(t, rmsr_done=rmsr_done):
                    t.wait_ge(sS.h, rmsr_done)
                    sP.t(t.matmul(ps[5][:, :], self.b_or[:, :], rmsrow[:, :],
                                  start=True, stop=True))
                self.sec("tensor", f, snap=gsn0, raw="free")

                def f(v, sp=sp, p0=p0):
                    v.wait_ge(sP.h, p0 + 1)
                    for kvc in range(KVCH):
                        sV.t(v.scalar_tensor_tensor(self.kv_nT[:, kvc, sp*512:(sp+1)*512],
                                                    kvsp[:, kvc, :], self.b_gkv[:, kvc:kvc+1],
                                                    ps[5][:, :], ALU.mult, ALU.mult))
                        evac[("kvn", kvc)] = sV.v
                self.sec("vector", f, snap=gsn0, raw="free")

                combq = emit_rope_pair(w_qro, QRCH, qr_sp, "qrspill")

                p0 = sP.v

                def f(t):
                    for fc in range(DCH):
                        inst = t.matmul(ps[0][0:E, :], w_rt[:, fc, :], xfm[:, fc, :],
                                        start=(fc == 0), stop=(fc == DCH - 1))
                    sP.t(inst)
                self.sec("tensor", f, snap=gsn0, raw="free")

                def f(v, p0=p0):
                    v.wait_ge(sP.h, p0 + 1)
                    sV.t(v.tensor_scalar(lgev[:, :], ps[0][0:E, :], self.b_br[:, 0:1],
                                         None, ALU.add))
                self.sec("vector", f, snap=gsn0, raw="free")
                lgev_done = sV.v

                p0, v0 = sP.v, sV.v

                def f(t, te_last=te_last, lgev_done=lgev_done, v0=v0):
                    t.wait_ge(sV.h, lgev_done)
                    for tl in range(4):
                        if tl >= 2:
                            t.wait_ge(sV.h, v0 + tl - 1)
                        pp = self.psT[0:128, 0:E] if tl % 2 == 0 else ps[2][0:128, 0:E]
                        sP.t(t.transpose(pp, lgev[:, tl*128:(tl+1)*128],
                                         self.b_ident[0:E, 0:E]))
                self.sec("tensor", f, snap=gsn0, raw="free")

                def f(v, sp=sp, par=par, p0=p0):
                    if ("lgspill", sp - 2) in spill_done:
                        v.wait_ge(sG.h, spill_done[("lgspill", sp - 2)])
                    for tl in range(4):
                        v.wait_ge(sP.h, p0 + tl + 1)
                        pp = self.psT[0:128, 0:E] if tl % 2 == 0 else ps[2][0:128, 0:E]
                        sV.t(v.tensor_copy(lg_sp[:, par, tl, :], pp))
                self.sec("vector", f, snap=gsn0, raw="free")
                lg_done = sV.v

                if sp + 1 < SPB:
                    emit_norm(sp + 1, rstn, spill_done.get(("xnspill", sp - 1), 0))

                qev3 = evac[("qev", 3)]

                def f(g, sp=sp, par=par, qev3=qev3, combk=combk, combq=combq,
                      lg_done=lg_done):
                    g.wait_ge(sV.h, norm_done[sp][3])
                    self.sG.t(g.dma_start(
                        out=self.xn_bounce.ap()[sp*512:(sp+1)*512, :]
                        .rearrange("(a p) b -> p a b", p=128),
                        in_=xn_bf[:, par, :, :]), 16)
                    spill_done[("xnspill", sp)] = sG.v
                    g.wait_ge(sV.h, qev3)
                    self.sG.t(g.dma_start(
                        out=self.q_dram.ap()[sp, :, :],
                        in_=q_sp[:, par, :, :].rearrange("p a b -> p (a b)")), 16)
                    spill_done[("qspill", sp)] = sG.v
                    g.wait_ge(sV.h, combk)
                    self.sG.t(g.dma_start(
                        out=self.kr_dram.ap()[:, :, sp*512:(sp+1)*512],
                        in_=kr_sp[:, par, :, :]), 16)
                    spill_done[("krspill", sp)] = sG.v
                    g.wait_ge(sV.h, combq)
                    self.sG.t(g.dma_start(
                        out=self.qr_dram.ap()[sp, :, :],
                        in_=qr_sp[:, par, :, :].rearrange("p a b -> p (a b)")), 16)
                    spill_done[("qrspill", sp)] = sG.v
                    g.wait_ge(sV.h, lg_done)
                    self.sG.t(g.dma_start(
                        out=self.lg_dram.ap()[sp, :, :],
                        in_=lg_sp[:, par, :, :].rearrange("p a b -> p (a b)")), 16)
                    spill_done[("lgspill", sp)] = sG.v
                self.sec("gpsimd", f, snap=gsn0, raw="free")

                if sp + 2 < SPB:
                    emit_x(sp + 2, combq)

        cfg2 = self.cfg

        def f(g):
            qh = self._vload(g, self.b_cmeta['gpsimd'][0:1, 0:1], 0, 1)
            for j in range(cfg2.SPO):
                gsp = g.snap(qh + 2 * j, min_val=0, max_val=cfg2.SPB - 1)
                self.sG.t(g.dma_start(
                    out=self.q_cT[:, :, j*512:(j+1)*512],
                    in_=self.q_dram.ap()[ts(gsp, 1), :, :]
                    .rearrange("a p (c b) -> p (a c) b", c=cfg2.QRCH)), 16)
                self.sG.t(g.dma_start(
                    out=self.xn_spill.ap()[j*512:(j+1)*512, :],
                    in_=self.xn_bounce.ap()[ts(gsp, 512), :]), 16)
        self.sec("gpsimd", f)

        def f(sc):
            qh = self._vload(sc, self.b_cmeta['scalar'][0:1, 0:1], 0, 1)
            for j in range(cfg2.SPO):
                gsp = sc.snap(qh + 2 * j, min_val=0, max_val=cfg2.SPB - 1)
                self.sD.t(sc.dma_start(
                    out=self.qro_dram.ap()[:, :, j*512:(j+1)*512],
                    in_=self.qr_dram.ap()[ts(gsp, 1), :, :]
                    .rearrange("a p (c b) -> p (a c) b", c=4)), 16)
                self.sD.t(sc.dma_start(
                    out=self.r_lg[:, 4*j:4*(j+1), :],
                    in_=self.lg_dram.ap()[ts(gsp, 1), :, :]
                    .rearrange("a p (c b) -> p (a c) b", c=4)), 16)
        self.sec("scalar", f, raw=True)

    def ph_route(self):
        cfg = self.cfg
        E, CAP, TO = cfg.E, cfg.CAP, cfg.TO
        sV, sP = self.sV, self.sP
        ps = self.ps

        def f(g):
            for isl in range(self.cfg.ISL):
                self.sG.t(g.dma_start(
                    out=self.wi_t.ap()[isl],
                    in_=self.d["Wi"].ap()[:, isl*512:(isl+1)*512]
                    .rearrange("(c p) o -> p c o", p=128)), 16)
            for k2 in range(2 * self.cfg.ISL):
                Fh, isl = k2 // self.cfg.ISL, k2 % self.cfg.ISL
                self.sG.t(g.dma_start(
                    out=self.wo_t.ap()[k2],
                    in_=self.d["Wo"].ap()[isl*512:(isl+1)*512, Fh*512:(Fh+1)*512]
                    .rearrange("(c p) o -> p c o", p=128)), 16)
        self.sec("gpsimd", f)

        with ExitStack() as XR:
            cntrow = XR.enter_context(self.nc.sbuf_tensor("cntrow", [1, cfg.NT_O * E], F32))
            rbrow = XR.enter_context(self.nc.sbuf_tensor("rbrow", [1, cfg.NT_O * E], F32))
            rsn = self.snap()
            v0, s0, p0 = sV.v, self.sS.v, self.sP.v
            S1 = lambda tt: s0 + tt + 1
            VA1, VB1 = {}, {}
            PB = lambda tt: (0, 8) if tt % 2 == 0 else (16, 24)

            def f(v):
                def A(tt):
                    b, m8 = PB(tt)
                    if tt >= 2:
                        v.wait_ge(self.sS.h, S1(tt - 2))
                    sV.t(v.reduce_max(self.scr[:, b:b+1], self.r_lg[:, tt, :], AX.X))
                    v.wait_ge(sV.h, sV.v)
                    sV.t(v.tensor_scalar(self.scr[:, b+1:b+2], self.scr[:, b:b+1], -1.0,
                                         None, ALU.mult))
                    VA1[tt] = sV.v
                A(0); A(1)
                for tt in range(cfg.NT_O):
                    b, m8 = PB(tt)
                    v.wait_ge(self.sS.h, S1(tt))
                    sV.t(v.reciprocal(self.scr[:, b+3:b+4], self.scr[:, b+2:b+3]))
                    v.wait_ge(sV.h, sV.v)
                    sV.t(v.tensor_scalar(self.r_w[:, tt, :], self.r_w[:, tt, :],
                                         self.scr[:, b+3:b+4], None, ALU.mult))
                    v.wait_ge(sV.h, sV.v)
                    sV.t(v.max(out=self.scr[:, m8:m8+8], in_=self.r_w[:, tt, :]))
                    v.wait_ge(sV.h, sV.v)
                    sV.t(v.scalar_tensor_tensor(self.r_tk[:, tt, :], self.r_w[:, tt, :],
                                                self.scr[:, m8+1:m8+2], self.r_w[:, tt, :],
                                                ALU.is_ge, ALU.mult))
                    v.wait_ge(sV.h, sV.v)
                    sV.t(v.tensor_scalar(self.r_M[:, tt, :], self.r_tk[:, tt, :], 0.0,
                                         None, ALU.is_gt))
                    VB1[tt] = sV.v
                    if tt + 2 < cfg.NT_O:
                        A(tt + 2)
                v.wait_ge(self.sP.h, p0 + cfg.NT_O)
                sV.t(v.tensor_copy(cntrow[:, :], ps[3][0:1, 0:cfg.NT_O * E]))
                for j in range(cfg.SPO):
                    v.wait_ge(sV.h, sV.v)
                    sV.t(v.tensor_add(self.spsum[:, j*E:(j+1)*E],
                                      cntrow[:, (4*j)*E:(4*j+1)*E],
                                      cntrow[:, (4*j+1)*E:(4*j+2)*E]))
                    v.wait_ge(sV.h, sV.v)
                    sV.t(v.tensor_add(self.spsum[:, j*E:(j+1)*E],
                                      self.spsum[:, j*E:(j+1)*E],
                                      cntrow[:, (4*j+2)*E:(4*j+3)*E]))
                    v.wait_ge(sV.h, sV.v)
                    sV.t(v.tensor_add(self.spsum[:, j*E:(j+1)*E],
                                      self.spsum[:, j*E:(j+1)*E],
                                      cntrow[:, (4*j+3)*E:(4*j+4)*E]))
            self.sec("vector", f, snap=rsn, raw="free")

            def f(sc):
                for tt in range(cfg.NT_O):
                    b, m8 = PB(tt)
                    sc.wait_ge(sV.h, VA1[tt])
                    sc.wait_ge(self.sS.h, self.sS.v)
                    self.sS.t(sc.activation(out=self.r_w[:, tt, :], in_=self.r_lg[:, tt, :],
                                            func=AF.Exp, bias=self.scr[:, b+1:b+2],
                                            accum_out=self.scr[:, b+2:b+3]))
            self.sec("scalar", f, snap=rsn, raw="free")

            def f(t):
                for tt in range(cfg.NT_O):
                    t.wait_ge(sV.h, VB1[tt])
                    sP.t(t.matmul(ps[3][0:1, tt*E:(tt+1)*E], self.b_ocb[:, 0:1],
                                  self.r_M[:, tt, :], start=True, stop=True))
            self.sec("tensor", f, snap=rsn, raw="free")

            with ExitStack() as XS:
                r_ar = XS.enter_context(self.nc.sbuf_tensor("r_ar", [4 * cfg.SPB, E], F32))

                def f(g):
                    for j in range(cfg.SPO):
                        self.sG.t(g.dma_start(out=self.spsum2[j:j+1, :],
                                              in_=self.spsum[0:1, j*E:(j+1)*E]), 16)
                self.sec("gpsimd", f)

                def f(t):
                    t.matmul(ps[2][0:4*cfg.SPB, 0:E], self.b_place[:, :], self.spsum2[:, :],
                             start=True, stop=True)
                self.sec("tensor", f)

                def f(v):
                    v.tensor_copy(self.ar_sb[:, :], ps[2][0:4*cfg.SPB, 0:E])
                self.sec("vector", f)

                def f(g):
                    self.sG.t(g.dma_start(out=self.ar1_in.ap(), in_=self.ar_sb[:, :]), 16)
                self.sec("gpsimd", f)

                def f(g):
                    self.sC.t(g.collective_compute(
                        "AllReduce", ALU.add, replica_groups=[list(range(8))],
                        ins=[self.ar1_in.ap().opt()], outs=[self.ar1_out.ap().opt()]), 1)
                    self.sC.t(g.collective_compute(
                        "AllGather", ALU.bypass, replica_groups=[list(range(8))],
                        ins=[self.xn_spill.ap()[0:TO, :].opt()],
                        outs=[self.ag1_out.ap().opt()]), 1)
                self.sec("gpsimd", f)

                def f(s):
                    self.sD.t(s.dma_start(out=r_ar[:, :], in_=self.ar1_out.ap()), 16)
                self.sec("sync", f)

                for j in range(cfg.SPO):
                    def f(t, j=j):
                        t.matmul(ps[2][0:1, 0:E], self.b_presum[:, j:j+1], r_ar[:, :],
                                 start=True, stop=True)
                    self.sec("tensor", f)

                    def f(v, j=j):
                        v.tensor_copy(self.sbase[:, j*E:(j+1)*E], ps[2][0:1, 0:E])
                    self.sec("vector", f)

            def f(v):
                for j in range(cfg.SPO):
                    sV.t(v.tensor_copy(rbrow[:, (4*j)*E:(4*j+1)*E],
                                       self.sbase[:, j*E:(j+1)*E]))
                    for k in range(1, 4):
                        v.wait_ge(sV.h, sV.v)
                        sV.t(v.tensor_add(rbrow[:, (4*j+k)*E:(4*j+k+1)*E],
                                          rbrow[:, (4*j+k-1)*E:(4*j+k)*E],
                                          cntrow[:, (4*j+k-1)*E:(4*j+k)*E]))
            self.sec("vector", f, raw="free")

            rsn2 = self.snap()
            p0, v0 = self.sP.v, sV.v
            RC = lambda tt: v0 + 2 * (tt + 1)

            def f(t):
                for tt in range(cfg.NT_O):
                    pA, pB = (ps[2], ps[3]) if tt % 2 == 0 else (ps[4], ps[5])
                    if tt >= 2:
                        t.wait_ge(sV.h, RC(tt - 2))
                    sP.t(t.matmul(pA[:, 0:E], self.b_or[:, :], rbrow[0:1, tt*E:(tt+1)*E],
                                  start=True, stop=True))
                    sP.t(t.matmul(pB[:, 0:E], self.b_tri[:, :], self.r_M[:, tt, :],
                                  start=True, stop=True))
            self.sec("tensor", f, snap=rsn2, raw="free")

            def f(v):
                for tt in range(cfg.NT_O):
                    pA, pB = (ps[2], ps[3]) if tt % 2 == 0 else (ps[4], ps[5])
                    v.wait_ge(self.sP.h, p0 + 2 * (tt + 1))
                    sV.t(v.tensor_copy(self.r_C[:, tt, :], pB[:, 0:E]))
                    v.wait_ge(sV.h, sV.v)
                    sV.t(v.tensor_add(self.r_C[:, tt, :], self.r_C[:, tt, :], pA[:, 0:E]))
            self.sec("vector", f, snap=rsn2, raw="free")

        with ExitStack() as XS:
            SB = lambda n, sh, dt: XS.enter_context(self.nc.sbuf_tensor(n, list(sh), dt))
            selb = SB("selb", (128, E), F32)
            tmp1 = SB("tmp1", (128, E), F32)
            tmp2 = SB("tmp2", (128, E), F32)
            gsl = SB("gsl", (128, E), F32)
            r_i16 = SB("r_i16", (16, TO), I16)
            r_i16b = SB("r_i16b", (16, TO), I16)
            r_g16 = SB("r_g16", (16, TO), FP16)
            r_gslf = SB("r_gslf", (8, TO), F32)
            r_gtmp = SB("r_gtmp", (8, TO), F32)

            def f(v):
                v.memset(r_i16[:, :], -1)
                v.memset(r_i16b[:, :], -1)
                v.memset(r_g16[:, :], 0.0)
            self.sec("vector", f)

            for tt in range(cfg.NT_O):
                def f(v, tt=tt):
                    v.tensor_scalar(selb[:, :], self.r_C[:, tt, :], float(CAP), None, ALU.is_le)
                    v.tensor_scalar(tmp1[:, :], self.r_tk[:, tt, :], 0.0, None, ALU.is_gt)
                    v.tensor_mul(selb[:, :], selb[:, :], tmp1[:, :])
                    v.tensor_mul(gsl[:, :], self.r_C[:, tt, :], selb[:, :])
                    v.tensor_scalar(gsl[:, :], gsl[:, :], 1.0, None, ALU.subtract)
                    v.scalar_tensor_tensor(tmp1[:, :], selb[:, :], -1000.0,
                                           self.b_e1000[:, :], ALU.mult, ALU.add)
                    v.tensor_reduce(self.scr[:, 0:1], tmp1[:, :], AX.X, ALU.min)
                    v.tensor_scalar(tmp2[:, :], tmp1[:, :], self.scr[:, 0:1], None,
                                    ALU.is_equal)
                    v.tensor_mul(tmp2[:, :], tmp2[:, :], self.r_C[:, tt, :])
                    v.tensor_reduce(self.scr[:, 1:2], tmp2[:, :], AX.X, ALU.add)
                    v.tensor_scalar(self.scr[:, 2:3], self.scr[:, 0:1], float(CAP), None, ALU.mult)
                    v.tensor_add(self.scr[:, 2:3], self.scr[:, 2:3], self.scr[:, 1:2])
                    v.tensor_scalar(self.scr[:, 2:3], self.scr[:, 2:3], 1.0, None, ALU.subtract)
                    v.tensor_scalar(self.scr[:, 3:4], self.scr[:, 0:1], 999.0, None, ALU.is_le)
                    v.tensor_mul(self.g_row[:, tt, 0:1], self.scr[:, 2:3], self.scr[:, 3:4])
                    v.tensor_copy(self.g_val[:, tt, 0:1], self.scr[:, 3:4])
                    v.tensor_mul(tmp2[:, :], selb[:, :], self.b_ep1[:, :])
                    v.tensor_scalar(tmp2[:, :], tmp2[:, :], 1.0, None, ALU.subtract)
                    v.tensor_reduce(self.scr[:, 4:5], tmp2[:, :], AX.X, ALU.max)
                    v.tensor_scalar(tmp1[:, :], tmp2[:, :], self.scr[:, 4:5], None,
                                    ALU.is_equal)
                    v.tensor_mul(tmp1[:, :], tmp1[:, :], self.r_C[:, tt, :])
                    v.tensor_reduce(self.scr[:, 5:6], tmp1[:, :], AX.X, ALU.add)
                    v.tensor_scalar(self.scr[:, 6:7], self.scr[:, 4:5], float(CAP), None, ALU.mult)
                    v.tensor_add(self.scr[:, 6:7], self.scr[:, 6:7], self.scr[:, 5:6])
                    v.tensor_scalar(self.scr[:, 6:7], self.scr[:, 6:7], 1.0, None, ALU.subtract)
                    v.tensor_tensor(self.scr[:, 7:8], self.scr[:, 4:5], self.scr[:, 0:1], ALU.is_gt)
                    v.tensor_mul(self.g_row[:, tt, 1:2], self.scr[:, 6:7], self.scr[:, 7:8])
                    v.tensor_copy(self.g_val[:, tt, 1:2], self.scr[:, 7:8])
                self.sec("vector", f)

                def f(t, tt=tt):
                    t.transpose(self.psT[0:E, 0:128], gsl[:, :], self.b_ident[:, :])
                    t.transpose(self.psC1[0:E, 0:128], self.r_tk[:, tt, :], self.b_ident[:, :])
                self.sec("tensor", f)

                def f(v, tt=tt):
                    v.tensor_copy(r_gslf[0:E, tt*128:(tt+1)*128], self.psT[0:E, 0:128])
                    v.tensor_copy(r_g16[0:E, tt*128:(tt+1)*128], self.psC1[0:E, 0:128])
                self.sec("vector", f)

            halfc = CAP // 2
            with ExitStack() as YS:
                t_i = YS.enter_context(self.nc.sbuf_tensor("t_i", [16, CAP], I16))
                t_g = YS.enter_context(self.nc.sbuf_tensor("t_g", [16, CAP], FP16))

                def f(v):
                    v.tensor_scalar(r_gtmp[:, :], r_gslf[:, :], float(halfc), None, ALU.is_lt)
                    v.tensor_scalar(r_gslf[:, :].copy() if False else r_gslf[:, :],
                                    r_gslf[:, :], 0.0, None, ALU.add)
                    v.scalar_tensor_tensor(r_gtmp[:, :], r_gslf[:, :], 1.0, r_gtmp[:, :],
                                           ALU.add, ALU.mult)
                    v.tensor_scalar(r_gtmp[:, :], r_gtmp[:, :], 1.0, None, ALU.subtract)
                    v.tensor_copy(r_i16[0:E, :], r_gtmp[:, :])
                    v.tensor_scalar(r_gtmp[:, :], r_gslf[:, :], float(halfc), None, ALU.is_ge)
                    v.scalar_tensor_tensor(r_gtmp[:, :], r_gslf[:, :], float(1 - halfc),
                                           r_gtmp[:, :], ALU.add, ALU.mult)
                    v.tensor_scalar(r_gtmp[:, :], r_gtmp[:, :], 1.0, None, ALU.subtract)
                    v.tensor_copy(r_i16b[0:E, :], r_gtmp[:, :])
                self.sec("vector", f)

                def f(g):
                    g.load_library(libcfg.local_scatter)
                    self.sQ.t(g.local_scatter(t_i[:, 0:halfc], self.b_agp1[:, :], r_i16[:, :],
                                              channels=16, num_elems=halfc, num_idxs=TO))
                    self.sQ.t(g.local_scatter(t_i[:, halfc:CAP], self.b_agp1[:, :], r_i16b[:, :],
                                              channels=16, num_elems=CAP - halfc, num_idxs=TO))
                    self.sQ.t(g.local_scatter(t_g[:, 0:halfc], r_g16[:, :], r_i16[:, :],
                                              channels=16, num_elems=halfc, num_idxs=TO))
                    self.sQ.t(g.local_scatter(t_g[:, halfc:CAP], r_g16[:, :], r_i16b[:, :],
                                              channels=16, num_elems=CAP - halfc, num_idxs=TO))
                self.sec("gpsimd", f)

                def f(g):
                    self.sG.t(g.dma_start(out=self.tbi_in.ap(), in_=t_i[:, :]), 16)
                    self.sG.t(g.dma_start(out=self.tbg_in.ap(), in_=t_g[:, :]), 16)
                self.sec("gpsimd", f)

                def f(g):
                    self.sC.t(g.collective_compute(
                        "AllGather", ALU.bypass, replica_groups=[list(range(8))],
                        ins=[self.tbi_in.ap().opt()], outs=[self.tbi_out.ap().opt()]), 1)
                    self.sC.t(g.collective_compute(
                        "AllGather", ALU.bypass, replica_groups=[list(range(8))],
                        ins=[self.tbg_in.ap().opt()], outs=[self.tbg_out.ap().opt()]), 1)
                self.sec("gpsimd", f)

    def ph_attn(self):
        cfg, d = self.cfg, self.d
        T, TO = cfg.T, cfg.TO
        H, HD, RD = cfg.H, cfg.HD, cfg.RD
        KVCH, QRCH, SPO = cfg.KVCH, cfg.QRCH, cfg.SPO
        sP, sV, sS, sX = self.sP, self.sV, self.sS, self.sX
        sG, sD = self.sG, self.sD
        ps = self.ps
        qsc = 1.0 / math.sqrt(float(cfg.QD))

        with ExitStack() as AS:
            SB = lambda n, sh, dt: AS.enter_context(self.nc.sbuf_tensor(n, list(sh), dt))
            w_kd = SB("w_kd", (128, KVCH, cfg.D), BF16)
            w_vd = SB("w_vd", (128, KVCH, cfg.D), BF16)
            w_qd = SB("w_qd", (128, QRCH, cfg.D), BF16)
            v_g = SB("v_g", (128, T // 128, 4 * (HD + 1)), BF16)
            kt2 = SB("kt2", (128, 2, T), BF16)
            qt2 = SB("qt2", (128, 2, TO), BF16)
            sqk = SB("sqk", (128, T), BF16)
            sqq = SB("sqq", (128, TO), BF16)
            probs = SB("probs", (128, 2, 1536), BF16)
            att_acc = SB("att_acc", (128, SPO, 512), F32)
            brow = SB("brow", (1, T), F32)
            bqn = SB("bqn", (1, TO), F32)
            bqnb = SB("bqnb", (1, TO), BF16)
            att_bf = SB("att_bf", (64, 2, 512), BF16)
            att_f = SB("att_f", (64, 512), F32)

            def f(g):
                for nm, dst in [("W_kdec", w_kd), ("W_vdec", w_vd), ("W_qdec", w_qd)]:
                    self.sG.t(g.dma_start(out=dst[:],
                                          in_=d[nm].ap().rearrange("(c p) o -> p c o", p=128)), 16)
            self.sec("gpsimd", f)

            def f(v):
                v.memset(self.s_recip[:, :], 0.0)
                v.memset(kt2[96:97, 0, :], 1.0)
                v.memset(kt2[96:97, 1, :], 1.0)
                v.memset(sqk[96:97, :], 1.0)
                for l in range(4):
                    v.memset(v_g[:, :, l*(HD+1)+HD : l*(HD+1)+HD+1], 1.0)
            self.sec("vector", f)

            def emit_build(h, snap0):
                buf = h % 2
                p0, v0 = sP.v, sV.v
                items = [("k", sp8) for sp8 in range(T // 512)] + \
                        [("q", so) for so in range(SPO)]

                def f(t, h=h, v0=v0, items=items):
                    for idx, (kind, sp) in enumerate(items):
                        if idx >= 2:
                            t.wait_ge(sV.h, v0 + 2 * idx - 3)
                        pp = ps[idx % 2]
                        if kind == "k":
                            for kvc in range(KVCH):
                                inst = t.matmul(pp[0:HD, :], w_kd[:, kvc, h*HD:(h+1)*HD],
                                                self.kv_nT[:, kvc, sp*512:(sp+1)*512],
                                                start=(kvc == 0), stop=(kvc == KVCH - 1))
                        else:
                            for qc in range(QRCH):
                                inst = t.matmul(pp[0:HD, :], w_qd[:, qc, h*HD:(h+1)*HD],
                                                self.q_cT[:, qc, sp*512:(sp+1)*512],
                                                start=(qc == 0), stop=(qc == QRCH - 1))
                        sP.t(inst)
                self.sec("tensor", f, snap=snap0, raw="free")

                def f(v, buf=buf, p0=p0, items=items):
                    for idx, (kind, sp) in enumerate(items):
                        v.wait_ge(sP.h, p0 + idx + 1)
                        pp = ps[idx % 2]
                        if kind == "k":
                            sV.t(v.tensor_copy(kt2[0:HD, buf, sp*512:(sp+1)*512], pp[0:HD, :]))
                            sV.t(v.tensor_mul(sqk[0:HD, sp*512:(sp+1)*512],
                                              kt2[0:HD, buf, sp*512:(sp+1)*512],
                                              kt2[0:HD, buf, sp*512:(sp+1)*512]))
                        else:
                            sV.t(v.tensor_scalar(qt2[0:HD, buf, sp*512:(sp+1)*512], pp[0:HD, :],
                                                 qsc, None, ALU.mult))
                            sV.t(v.tensor_mul(sqq[0:HD, sp*512:(sp+1)*512],
                                              qt2[0:HD, buf, sp*512:(sp+1)*512],
                                              qt2[0:HD, buf, sp*512:(sp+1)*512]))
                self.sec("vector", f, snap=snap0, raw=True)

            def emit_rope_dma(h):
                buf, g4, l = h % 2, h // 4, h % 4

                def f(g, buf=buf, g4=g4, l=l):
                    self.sG.t(g.dma_start(
                        out=kt2[HD:HD+RD, buf, :],
                        in_=self.kr_dram.ap()[l*RD:(l+1)*RD, g4:g4+1, :]
                        .rearrange("p a b -> p (a b)")), 16)
                    self.sG.t(g.dma_start(
                        out=qt2[HD:HD+RD, buf, :],
                        in_=self.qro_dram.ap()[l*RD:(l+1)*RD, g4:g4+1, :]
                        .rearrange("p a b -> p (a b)")), 16)
                self.sec("gpsimd", f)

            def emit_rope_sq(h):
                buf = h % 2

                def f(v, buf=buf):
                    v.tensor_mul(sqk[HD:HD+RD, :], kt2[HD:HD+RD, buf, :],
                                 kt2[HD:HD+RD, buf, :])
                    v.tensor_mul(sqq[HD:HD+RD, :], qt2[HD:HD+RD, buf, :],
                                 qt2[HD:HD+RD, buf, :])
                self.sec("vector", f)

            def emit_vg(g4):
                gsn = self.snap()
                p0, v0 = sP.v, sV.v
                ntc = T // 128

                def f(t, g4=g4, v0=v0, ntc=ntc):
                    for tc in range(ntc):
                        if tc >= 2:
                            t.wait_ge(sV.h, v0 + tc - 1)
                        pp = ps[tc % 2]
                        for kvc in range(KVCH):
                            inst = t.matmul(pp[:, 0:4*HD], self.kv_nT[:, kvc, tc*128:(tc+1)*128],
                                            w_vd[:, kvc, g4*4*HD:(g4+1)*4*HD],
                                            start=(kvc == 0), stop=(kvc == KVCH - 1))
                        sP.t(inst)
                self.sec("tensor", f, snap=gsn, raw="free")

                def f(v, p0=p0, ntc=ntc):
                    for tc in range(ntc):
                        v.wait_ge(sP.h, p0 + tc + 1)
                        pp = ps[tc % 2]
                        dst = v_g[:, tc, :].rearrange("p (l d) -> p l d", l=4)[:, :, 0:HD]
                        src = pp[:, 0:4*HD].rearrange("p (l d) -> p l d", l=4)
                        sV.t(v.tensor_copy(dst, src))
                self.sec("vector", f, snap=gsn, raw=True)

            def emit_bound(h):
                gsn = self.snap()
                p0, v0 = sP.v, sV.v
                nsq = T // 512 + SPO

                def f(t, v0=v0, nsq=nsq):
                    for i in range(nsq):
                        if i >= 2:
                            t.wait_ge(sV.h, v0 + i - 1)
                        pp = self.psG3a[0:1, 0:512] if i % 2 == 0 else self.psG3b[0:1, 0:512]
                        if i < T // 512:
                            sP.t(t.matmul(pp, self.b_ocb[0:97, 0:1],
                                          sqk[0:97, i*512:(i+1)*512], start=True, stop=True))
                        else:
                            so = i - T // 512
                            sP.t(t.matmul(pp, self.b_ocb[0:96, 0:1],
                                          sqq[0:96, so*512:(so+1)*512], start=True, stop=True))
                self.sec("tensor", f, snap=gsn, raw="free")

                def f(v, p0=p0, nsq=nsq):
                    for i in range(nsq):
                        v.wait_ge(sP.h, p0 + i + 1)
                        pp = self.psG3a[0:1, 0:512] if i % 2 == 0 else self.psG3b[0:1, 0:512]
                        if i < T // 512:
                            if i == 0:
                                sV.t(v.tensor_copy(brow[:, 0:512], pp))
                            else:
                                sV.t(v.tensor_tensor(brow[:, 0:512], brow[:, 0:512], pp,
                                                     ALU.max))
                        else:
                            so = i - T // 512
                            sV.t(v.tensor_copy(bqn[:, so*512:(so+1)*512], pp))
                self.sec("vector", f, snap=gsn, raw=True)

            def emit_chain(h):
                buf = h % 2

                def f(v):
                    v.tensor_reduce(self.scr[0:1, 0:1], brow[:, 0:512], AX.X, ALU.max)
                    v.tensor_scalar(self.scr[0:1, 0:1], self.scr[0:1, 0:1], 1e-30, None,
                                    ALU.add)
                    v.tensor_scalar(bqn[:, :], bqn[:, :], 1e-30, None, ALU.add)
                    v.tensor_scalar(bqn[:, :], bqn[:, :], self.scr[0:1, 0:1], None,
                                    ALU.mult)
                self.sec("vector", f)

                def f(sc):
                    sc.activation(out=bqn[:, :], in_=bqn[:, :], func=AF.Sqrt)
                self.sec("scalar", f)

                def f(v):
                    v.tensor_scalar(bqn[:, :], bqn[:, :], -1.0, cfg.MARGIN,
                                    ALU.mult, ALU.add)
                    v.tensor_copy(bqnb[:, :], bqn[:, :])
                self.sec("vector", f)

                def f(g, buf=buf):
                    self.sG.t(g.dma_start(out=qt2[96:97, buf, :], in_=bqnb[0:1, :]), 16)
                self.sec("gpsimd", f)

            def emit_norm_g0(h):
                def f(g, h=h):
                    self.sG.t(g.dma_start(out=self.s_recip[h:h+1, :],
                                          in_=att_acc[HD:HD+1, :, :]
                                          .rearrange("p a b -> p (a b)")), 16)
                self.sec("gpsimd", f)
                return sG.v

            def emit_norm_g3(h):
                gsn = self.snap()
                v0, g0 = sV.v, sG.v

                def f(v, g0=g0):
                    for j in range(SPO):
                        if j >= 2:
                            v.wait_ge(sG.h, g0 + 16 * (j - 1))
                        sV.t(v.tensor_copy(att_bf[:, j % 2, :], att_acc[0:HD, j, :]))
                self.sec("vector", f, snap=gsn, raw=True)

                def f(g, h=h, v0=v0):
                    for j in range(SPO):
                        g.wait_ge(sV.h, v0 + j + 1)
                        self.sG.t(g.dma_start(out=self.aoT_sp.ap()[h*HD:(h+1)*HD,
                                                                   j*512:(j+1)*512],
                                              in_=att_bf[:, j % 2, :]), 16)
                self.sec("gpsimd", f, snap=gsn, raw="free")

            def emit_inner(h, snap0, v_guard, s4_guard):
                buf, l = h % 2, h % 4
                for j in range(SPO):
                    nk = cfg.KB[j]
                    ngr = (nk + 2) // 3
                    grp = [list(range(3*g, min(3*g+3, nk))) for g in range(ngr)]

                    def g_masked(g, grp=grp, nk=nk):
                        return [kc for kc in grp[g] if kc >= nk - 8]
                    scores_done = [0] * ngr
                    av_done = [0] * ngr
                    exp_done = [0] * ngr
                    mask_grp_done = [0] * ngr
                    pe_c = 0
                    for g in range(ngr):
                        pe_c += 1
                        scores_done[g] = pe_c
                        if g >= 1:
                            pe_c += 1
                            av_done[g - 1] = pe_c
                    pe_c += 1
                    av_done[ngr - 1] = pe_c
                    for g in range(ngr):
                        exp_done[g] = g + 1
                    mk_c = 0
                    for g in range(ngr):
                        if g_masked(g):
                            mk_c += 1
                        mask_grp_done[g] = mk_c

                    gsn = snap0 if j == 0 else self.snap()
                    p0, s0, x0 = sP.v, sS.v, sX.v
                    psets = [self.psG3a, self.psG3b]

                    def f(t, j=j, nk=nk, ngr=ngr, grp=grp, s0=s0, x0=x0, l=l, buf=buf,
                          av_done=av_done, exp_done=exp_done,
                          mask_grp_done=mask_grp_done, psets=psets, g_masked=g_masked):
                        if j == 0:
                            t.wait_ge(sV.h, v_guard)

                        def av_group(g):
                            t.wait_ge(sS.h, s0 + exp_done[g])
                            if g_masked(g):
                                t.wait_ge(sX.h, x0 + mask_grp_done[g])
                            for i, kc in enumerate(grp[g]):
                                inst = t.matmul(ps[6][0:HD+1, :],
                                                v_g[:, kc, l*(HD+1):(l+1)*(HD+1)],
                                                probs[:, g % 2, i*512:(i+1)*512],
                                                start=(kc == 0), stop=(kc == nk - 1))
                                if i == len(grp[g]) - 1:
                                    sP.t(inst)
                        for g in range(ngr):
                            if g >= 2:
                                t.wait_ge(sS.h, s0 + exp_done[g - 2])
                            pset = psets[g % 2]
                            for i, kc in enumerate(grp[g]):
                                inst = t.matmul(pset[:, i*512:(i+1)*512],
                                                kt2[0:97, buf, kc*128:(kc+1)*128],
                                                qt2[0:97, buf, j*512:(j+1)*512],
                                                start=True, stop=True)
                                if i == len(grp[g]) - 1:
                                    sP.t(inst)
                            if g >= 1:
                                av_group(g - 1)
                        av_group(ngr - 1)
                    self.sec("tensor", f, snap=gsn, raw="free")

                    def f(sc, ngr=ngr, grp=grp, p0=p0,
                          scores_done=scores_done, av_done=av_done, psets=psets):
                        for g in range(ngr):
                            sc.wait_ge(sP.h, p0 + scores_done[g])
                            if g >= 2:
                                sc.wait_ge(sP.h, p0 + av_done[g - 2])
                            w = len(grp[g])
                            self.sS.t(sc.activation(out=probs[:, g % 2, 0:w*512],
                                                    in_=psets[g % 2][:, 0:w*512], func=AF.Exp))
                    self.sec("scalar", f, snap=gsn, raw="free")

                    def f(v, nk=nk, ngr=ngr, grp=grp, s0=s0,
                          exp_done=exp_done, g_masked=g_masked):
                        for g in range(ngr):
                            mk = g_masked(g)
                            if not mk:
                                continue
                            v.wait_ge(sS.h, s0 + exp_done[g])
                            for kc in mk:
                                i = kc - grp[g][0]
                                mi = kc - (nk - 8)
                                inst = v.tensor_mul(probs[:, g % 2, i*512:(i+1)*512],
                                                    probs[:, g % 2, i*512:(i+1)*512],
                                                    self.b_mask[:, mi, :])
                                if kc == mk[-1]:
                                    self.sX.t(inst, 1)
                    self.sec("vector", f, snap=gsn, raw="free")

                    def f(v, j=j):
                        if j == 0 and s4_guard > 0:
                            v.wait_ge(sG.h, s4_guard)
                        v.tensor_copy(att_acc[0:HD+1, j, :], ps[6][0:HD+1, :])
                    self.sec("vector", f)

            emit_rope_dma(0)
            emit_build(0, self.snap())
            emit_rope_sq(0)
            emit_vg(0)
            emit_bound(0)
            emit_chain(0)
            s4_guard = 0
            for h in range(H):
                snap0 = self.snap()
                if h + 1 < H:
                    emit_rope_dma(h + 1)
                if h >= 1:
                    s4_guard = emit_norm_g0(h - 1)
                if h + 1 < H:
                    emit_build(h + 1, snap0)
                    emit_rope_sq(h + 1)
                    emit_bound(h + 1)
                if h >= 1:
                    emit_norm_g3(h - 1)
                v_guard = sV.v
                emit_inner(h, snap0, v_guard, s4_guard)
                if h + 1 < H:
                    emit_chain(h + 1)
                if (h + 1) % 4 == 0 and h + 1 < H:
                    emit_vg((h + 1) // 4)
            s4_guard = emit_norm_g0(H - 1)
            emit_norm_g3(H - 1)

            def f(v):
                v.reciprocal(self.s_recip[:, :], self.s_recip[:, :])
            self.sec("vector", f)

    def ph_wo(self):
        cfg, d = self.cfg, self.d
        TO, D, DCH, SPO = cfg.TO, cfg.D, cfg.DCH, cfg.SPO
        ps = self.ps
        sP, sV = self.sP, self.sV

        with ExitStack() as WS:
            SB = lambda n, sh, dt: WS.enter_context(self.nc.sbuf_tensor(n, list(sh), dt))
            w_o = SB("w_o", (128, DCH, D), BF16)
            ao = SB("ao", (128, DCH, 512), BF16)
            xx = SB("xx", (128, 4, D), F32)
            yt = SB("yt", (128, D), F32)

            def f(g):
                self.sG.t(g.dma_start(out=w_o[:],
                                      in_=d["W_o"].ap().rearrange("(c p) o -> p c o", p=128)), 16)
            self.sec("gpsimd", f)

            for j in range(SPO):
                def f(s, j=j):
                    for fc in range(DCH):
                        self.sD.t(s.dma_start(out=ao[:, fc, :],
                                              in_=self.aoT_sp.ap()[fc*128:(fc+1)*128,
                                                                   j*512:(j+1)*512]), 16)
                self.sec("sync", f)

                def f(sc, j=j):
                    self.sD.t(sc.dma_start(out=xx[:, :, :],
                                          in_=d["x_own"].ap()[j*512:(j+1)*512, :]
                                          .rearrange("(a p) b -> p a b", p=128)), 16)
                self.sec("scalar", f, raw=True)

                gsn = self.snap()
                p0n, v0n = sP.v, sV.v

                def f(t, j=j, v0n=v0n):
                    for fc in range(DCH):
                        if fc >= 2:
                            t.wait_ge(sV.h, v0n + fc - 1)
                        pp = ps[3] if fc % 2 == 0 else ps[4]
                        sP.t(t.matmul(pp[:, :], self.b_hsel[:, fc*128:(fc+1)*128],
                                      self.s_recip[:, j*512:(j+1)*512],
                                      start=True, stop=True))
                self.sec("tensor", f, snap=gsn, raw="free")

                def f(v, j=j, p0n=p0n):
                    for fc in range(DCH):
                        v.wait_ge(sP.h, p0n + fc + 1)
                        pp = ps[3] if fc % 2 == 0 else ps[4]
                        sV.t(v.tensor_mul(ao[:, fc, :], ao[:, fc, :], pp[:, :]))
                self.sec("vector", f, snap=gsn, raw=True)

                for tl in range(4):
                    def f(t, j=j, tl=tl):
                        for Fh in range(2):
                            pp = ps[Fh]
                            for fc in range(DCH):
                                t.matmul(pp[:, :], ao[:, fc, tl*128:(tl+1)*128],
                                         w_o[:, fc, Fh*512:(Fh+1)*512],
                                         start=(fc == 0), stop=(fc == DCH - 1))
                    self.sec("tensor", f)

                    def f(v, j=j, tl=tl):
                        for Fh in range(2):
                            v.tensor_add(yt[:, Fh*512:(Fh+1)*512], ps[Fh][:, :],
                                         xx[:, tl, Fh*512:(Fh+1)*512])
                    self.sec("vector", f)

                    def f(s, j=j, tl=tl):
                        r0 = (j * 4 + tl) * 128
                        self.sD.t(s.dma_start(out=self.y_mid.ap()[r0:r0+128, :], in_=yt[:, :]), 16)
                    self.sec("sync", f)

    def ph_moe(self):
        cfg, d = self.cfg, self.d
        D, I, CAP, E = cfg.D, cfg.I, cfg.CAP, cfg.E
        DCH, ICH, ISL, CAPT = cfg.DCH, cfg.ICH, cfg.ISL, cfg.CAPT
        ps = self.ps
        sP, sV, sS = self.sP, self.sV, self.sS

        with ExitStack() as MS:
            SB = lambda n, sh, dt: MS.enter_context(self.nc.sbuf_tensor(n, list(sh), dt))
            cmb = SB("cmb", (8, CAP), F32)
            cmb16 = SB("cmb16", (8, CAP), I16)
            cmg16 = SB("cmg16", (8, CAP), FP16)
            rowf = SB("rowf", (1, CAP), F32)
            rowg = SB("rowg", (1, CAP), F32)
            m_xg = SB("m_xg", (128, 2, 4, D), BF16)
            m_xgf = SB("m_xgf", (128, 2, DCH, 512), BF16)
            m_wi = SB("m_wi", (128, 4, DCH, 512), BF16)
            m_wo = SB("m_wo", (128, 4, 4, 512), BF16)
            m_h = SB("m_h", (128, ICH, 512), BF16)
            m_out = SB("m_out", (128, 2, 512), F32)

            _ns_mc = self.nc.named_scope("mComb"); _ns_mc.__enter__()
            def f(s):
                cid = self._vload(s, self.b_cmeta['sync'][0:1, 2:3], 0, 7)
                for r in range(8):
                    rr = s.snap(cid + 16 * r, min_val=0, max_val=127)
                    self.sD.t(s.dma_start(out=cmb16[r:r+1, :],
                                          in_=self.tbi_out.ap()[ts(rr, 1), :]), 16)
                    self.sD.t(s.dma_start(out=cmg16[r:r+1, :],
                                          in_=self.tbg_out.ap()[ts(rr, 1), :]), 16)
            self.sec("sync", f)

            def f(v):
                v.tensor_copy(cmb[:, :], cmb16[:, :])
            self.sec("vector", f)

            spans5 = [(o, min(512, CAP - o)) for o in range(0, CAP, 512)]
            for o5, w5 in spans5:
                def f(t, o5=o5, w5=w5):
                    t.matmul(ps[0][0:1, 0:w5], self.b_oc[0:8, 0:1], cmb[:, o5:o5+w5],
                             start=True, stop=True)
                self.sec("tensor", f)

                def f(v, o5=o5, w5=w5):
                    v.tensor_copy(rowf[:, o5:o5+w5], ps[0][0:1, 0:w5])
                self.sec("vector", f)

            def f(v):
                v.tensor_copy(cmb[:, :], cmg16[:, :])
            self.sec("vector", f)

            for o5, w5 in spans5:
                def f(t, o5=o5, w5=w5):
                    t.matmul(ps[0][0:1, 0:w5], self.b_oc[0:8, 0:1], cmb[:, o5:o5+w5],
                             start=True, stop=True)
                self.sec("tensor", f)

                def f(v, o5=o5, w5=w5):
                    v.tensor_copy(rowg[:, o5:o5+w5], ps[0][0:1, 0:w5])
                self.sec("vector", f)

            def f(v):
                v.tensor_scalar(rowf[:, :], rowf[:, :], 1.0, None, ALU.subtract)
                v.tensor_scalar(rowf[:, :], rowf[:, :], 0.0, None, ALU.max)
            self.sec("vector", f)

            gsn = self.snap()
            p0, v0 = self.sP.v, self.sV.v

            def f(t, p0=p0, v0=v0):
                for st in range(CAPT):
                    if st >= 2:
                        t.wait_ge(self.sV.h, v0 + 2 * (st - 1))
                    pp = self.psT if st % 2 == 0 else self.psC1
                    self.sP.t(t.transpose(pp[0:128, 0:1], rowf[:, st*128:(st+1)*128],
                                          self.b_ident[0:1, 0:1]))
                    self.sP.t(t.transpose(pp[0:128, 1:2], rowg[:, st*128:(st+1)*128],
                                          self.b_ident[0:1, 0:1]))
            self.sec("tensor", f, snap=gsn, raw=True)

            def f(v, p0=p0):
                for st in range(CAPT):
                    v.wait_ge(self.sP.h, p0 + 2 * (st + 1))
                    pp = self.psT if st % 2 == 0 else self.psC1
                    self.sV.t(v.tensor_copy(self.e_idx[:, st:st+1], pp[0:128, 0:1]))
                    self.sV.t(v.tensor_copy(self.e_gate[:, st:st+1], pp[0:128, 1:2]))
            self.sec("vector", f, snap=gsn, raw=True)

            _ns_mc.__exit__(None, None, None)
            NCh = len(cfg.SCH)
            assert all(c == 512 for c in cfg.SCH)
            sV, sG, sD = self.sV, self.sG, self.sD

            def f(g):
                for st in range(4):
                    self.sG.t(g.indirect_dma_start(
                        m_xg[:, 0, st, :], None, self.ag1_out.ap(),
                        IndirectOffsetOnAxis(self.e_idx[:, st:st+1], 0)), 16)
            self.sec("gpsimd", f)

            gsn = self.snap()
            p0, s0, v0, d0, g0 = sP.v, sS.v, sV.v, sD.v, sG.v
            P, S, V, G, D = {}, {}, {}, {}, {}
            pl = p0
            for ci in range(NCh):
                for i in range(32):
                    pl += 1; P[("tr", ci, i)] = pl
                for k in range(32):
                    pl += 1; P[("m1", ci, k)] = pl
                for Fh in range(2):
                    for isl in range(8):
                        pl += 1; P[("m2", ci, Fh, isl)] = pl
            sl = s0
            for ci in range(NCh):
                for k in range(32):
                    sl += 1; S[("relu", ci, k)] = sl
            vl = v0
            for ci in range(NCh):
                for i in range(32):
                    vl += 1; V[("te", ci, i)] = vl
                for Fh in range(2):
                    for st in range(4):
                        vl += 1; V[("gate", ci, Fh, st)] = vl
            gl = g0
            for ci in range(NCh):
                if ci + 1 < NCh:
                    gl += 64; G[("gath", ci + 1)] = gl
                for Fh in range(2):
                    for st in range(4):
                        gl += 16; G[("out", (ci * 2 + Fh) * 4 + st)] = gl
            dl = d0
            for ci in range(NCh):
                for isl in range(8):
                    dl += 16; D[("wi", ci, isl)] = dl
                for Fh in range(2):
                    for isl in range(8):
                        dl += 16; D[("wo", ci, Fh, isl)] = dl

            def f(t):
                for ci in range(NCh):
                    for i in range(32):
                        st, fc = divmod(i, DCH)
                        if i == 0 and ci >= 1:
                            t.wait_ge(sG.h, G[("gath", ci)])
                        if i >= 2:
                            t.wait_ge(sV.h, V[("te", ci, i - 2)])
                        dst = (self.psT[:, :] if i % 2 == 0
                               else ps[2]).bitcast(BF16)[0:128, 0:128]
                        sP.t(t.transpose(dst, m_xg[:, ci % 2, st, fc*128:(fc+1)*128],
                                         self.b_identb[:, :]))
                    for isl in range(8):
                        for ic2 in range(4):
                            k = isl * 4 + ic2
                            if k == 0:
                                t.wait_ge(sV.h, V[("te", ci, 31)])
                            if ic2 == 0:
                                t.wait_ge(sD.h, D[("wi", ci, isl)])
                            if k >= 2:
                                t.wait_ge(sS.h, S[("relu", ci, k - 2)])
                            elif ci > 0:
                                t.wait_ge(sS.h, S[("relu", ci - 1, 30 + k)])
                            pp = ps[k % 2]
                            for fc in range(DCH):
                                inst = t.matmul(pp[:, :],
                                                m_wi[:, (ci*8+isl) % 4, fc, ic2*128:(ic2+1)*128],
                                                m_xgf[:, ci % 2, fc, :],
                                                start=(fc == 0), stop=(fc == DCH - 1))
                            sP.t(inst)
                    for Fh in range(2):
                        if Fh == 1:
                            t.wait_ge(sV.h, V[("gate", ci, 0, 3)])
                        elif ci > 0:
                            t.wait_ge(sV.h, V[("gate", ci - 1, 1, 3)])
                        for isl in range(8):
                            t.wait_ge(sD.h, D[("wo", ci, Fh, isl)])
                            t.wait_ge(sS.h, S[("relu", ci, isl * 4 + 3)])
                            buf = (ci * 16 + Fh * 8 + isl) % 4
                            for st in range(4):
                                for ic2 in range(4):
                                    inst = t.matmul(ps[3 + st][:, :],
                                                    m_h[:, isl*4+ic2, st*128:(st+1)*128],
                                                    m_wo[:, buf, ic2, :],
                                                    start=(isl == 0 and ic2 == 0),
                                                    stop=(isl == 7 and ic2 == 3))
                            sP.t(inst)
            self.sec("tensor", f, snap=gsn, raw="free")

            def f(sc):
                for ci in range(NCh):
                    for k in range(32):
                        sc.wait_ge(sP.h, P[("m1", ci, k)])
                        self.sS.t(sc.activation(out=m_h[:, k, :], in_=ps[k % 2][:, :],
                                                func=AF.Relu))
            self.sec("scalar", f, snap=gsn, raw="free")

            def f(v):
                for ci in range(NCh):
                    for i in range(32):
                        st, fc = divmod(i, DCH)
                        v.wait_ge(sP.h, P[("tr", ci, i)])
                        src = (self.psT[:, :] if i % 2 == 0
                               else ps[2]).bitcast(BF16)[0:128, 0:128]
                        self.sV.t(v.tensor_copy(m_xgf[:, ci % 2, fc, st*128:(st+1)*128], src))
                    for Fh in range(2):
                        v.wait_ge(sP.h, P[("m2", ci, Fh, 7)])
                        for st in range(4):
                            q = (ci * 2 + Fh) * 4 + st
                            if q >= 2:
                                v.wait_ge(sG.h, G[("out", q - 2)])
                            self.sV.t(v.tensor_scalar(m_out[:, q % 2, :], ps[3 + st][:, :],
                                                      self.e_gate[:, ci*4+st:ci*4+st+1],
                                                      None, ALU.mult))
            self.sec("vector", f, snap=gsn, raw="free")

            def f(g):
                for ci in range(NCh):
                    if ci + 1 < NCh:
                        if ci >= 1:
                            g.wait_ge(sP.h, P[("tr", ci - 1, 31)])
                        for st in range(4):
                            c = (ci + 1) * 4 + st
                            self.sG.t(g.indirect_dma_start(
                                m_xg[:, (ci + 1) % 2, st, :], None, self.ag1_out.ap(),
                                IndirectOffsetOnAxis(self.e_idx[:, c:c+1], 0)), 16)
                    for Fh in range(2):
                        for st in range(4):
                            q = (ci * 2 + Fh) * 4 + st
                            g.wait_ge(sV.h, V[("gate", ci, Fh, st)])
                            r0 = (ci * 4 + st) * 128
                            self.sG.t(g.dma_start(
                                out=self.ag2_in.ap()[r0:r0+128, Fh*512:(Fh+1)*512],
                                in_=m_out[:, q % 2, :]), 16)
            self.sec("gpsimd", f, snap=gsn, raw="free")

            def f(s):
                for ci in range(NCh):
                    for isl in range(8):
                        tw = ci * 8 + isl
                        if tw >= 4:
                            c2, i2 = divmod(tw - 4, 8)
                            s.wait_ge(sP.h, P[("m1", c2, i2 * 4 + 3)])
                        self.sD.t(s.dma_start(out=m_wi[:, tw % 4, :, :],
                                              in_=self.wi_t.ap()[isl]), 16)
                    for Fh in range(2):
                        for isl in range(8):
                            to = ci * 16 + Fh * 8 + isl
                            if to >= 4:
                                c2, r2 = divmod(to - 4, 16)
                                Fh2, isl2 = divmod(r2, 8)
                                s.wait_ge(sP.h, P[("m2", c2, Fh2, isl2)])
                            self.sD.t(s.dma_start(out=m_wo[:, to % 4, :, :],
                                                  in_=self.wo_t.ap()[Fh * 8 + isl]), 16)
            self.sec("sync", f, snap=gsn, raw="free")

            def f(g):
                self.sC.t(g.collective_compute(
                    "AllGather", ALU.bypass, replica_groups=[list(range(8))],
                    ins=[self.ag2_in.ap().opt()], outs=[self.ag2_out.ap().opt()]), 1)
            self.sec("gpsimd", f)

    def ph_final(self):
        cfg = self.cfg
        TO, D = cfg.TO, cfg.D

        with ExitStack() as FS:
            SB = lambda n, sh, dt: FS.enter_context(self.nc.sbuf_tensor(n, list(sh), dt))
            yt = SB("ytf", (128, 2, D), F32)
            g2 = SB("g2", (128, 2, 2, D), F32)
            idxc = SB("idxc", (128, cfg.NT_O, 2), I32)

            def f(v):
                v.tensor_copy(idxc[:, :, :], self.g_row[:, :, :])
            self.sec("vector", f)

            for tt in range(cfg.NT_O):
                gsn = self.snap()
                b = tt % 2

                def f(s, tt=tt, b=b):
                    self.sD.t(s.dma_start(out=yt[:, b, :],
                                          in_=self.y_mid.ap()[tt*128:(tt+1)*128, :]), 16)
                self.sec("sync", f, snap=gsn)

                def f(g, tt=tt, b=b):
                    for k2 in range(2):
                        self.sG.t(g.indirect_dma_start(
                            g2[:, b, k2, :], None, self.ag2_out.ap(),
                            IndirectOffsetOnAxis(idxc[:, tt, k2:k2+1], 0)), 16)
                self.sec("gpsimd", f, snap=gsn)

                def f(v, tt=tt, b=b):
                    v.scalar_tensor_tensor(yt[:, b, :], g2[:, b, 0, :], self.g_val[:, tt, 0:1],
                                           yt[:, b, :], ALU.mult, ALU.add)
                    v.scalar_tensor_tensor(yt[:, b, :], g2[:, b, 1, :], self.g_val[:, tt, 1:2],
                                           yt[:, b, :], ALU.mult, ALU.add)
                self.sec("vector", f)

                def f(s, tt=tt, b=b):
                    self.sD.t(s.dma_start(out=self.y_out.ap()[tt*128:(tt+1)*128, :],
                                          in_=yt[:, b, :]), 16)
                self.sec("sync", f)

            def f(s):
                s.wait_ge(self.sG.h, self.sG.v)
                s.wait_ge(self.sD.h, self.sD.v)
            self.blk.sync(f)


_CFG = CFG(T=4096)
_NC_CACHE = {}


def _build(cfg):
    key = cfg.T
    if key not in _NC_CACHE:
        _NC_CACHE[key] = K(cfg).build()
    return _NC_CACHE[key]


def kernel(L0_attn_x, L0_attn_W_kv, L0_attn_g_kv, L0_attn_W_kdec, L0_attn_W_vdec,
           L0_attn_W_q, L0_attn_W_qdec, L0_attn_W_krope, L0_attn_W_qrope, L0_attn_W_o,
           L0_ffn_W_router, L0_ffn_b_router, L0_ffn_experts_inter, L0_ffn_experts_out,
           cfg=None):
    cfg = cfg or _CFG
    nc = _build(cfg)
    x = np.asarray(L0_attn_x, dtype=np.float32)
    B, T, D = x.shape
    assert (B, T, D) == (cfg.B, cfg.T, cfg.D)
    WrT = np.ascontiguousarray(np.asarray(L0_ffn_W_router, np.float32).T)

    in_maps = []
    for core in range(8):
        b, qh = core // 2, core % 2
        m = dict(host_consts(cfg, core))
        m["x_b"] = np.ascontiguousarray(x[b])
        xo = np.zeros((cfg.TO, cfg.D), np.float32)
        for j in range(cfg.SPO):
            g = 2 * j + qh
            xo[j*512:(j+1)*512] = x[b, g*512:(g+1)*512]
        m["x_own"] = xo
        m["W_kv"] = np.asarray(L0_attn_W_kv, np.float32)
        m["g_kv"] = np.asarray(L0_attn_g_kv, np.float32)
        m["W_kdec"] = np.asarray(L0_attn_W_kdec, np.float32)
        m["W_vdec"] = np.asarray(L0_attn_W_vdec, np.float32)
        m["W_q"] = np.asarray(L0_attn_W_q, np.float32)
        m["W_qdec"] = np.asarray(L0_attn_W_qdec, np.float32)
        m["W_krope"] = np.asarray(L0_attn_W_krope, np.float32)
        m["W_qrope"] = np.asarray(L0_attn_W_qrope, np.float32)
        m["W_o"] = np.asarray(L0_attn_W_o, np.float32)
        m["Wr_T"] = WrT
        m["b_r"] = np.asarray(L0_ffn_b_router, np.float32)
        m["Wi"] = np.ascontiguousarray(np.asarray(L0_ffn_experts_inter, np.float32)[core])
        m["Wo"] = np.ascontiguousarray(np.asarray(L0_ffn_experts_out, np.float32)[core])
        in_maps.append(m)

    res = run_bass_kernel_spmd(nc, in_maps, core_ids=list(range(8)))

    out = np.zeros((B, T, D), np.float32)
    for core in range(8):
        b, qh = core // 2, core % 2
        yc = res.results[core]["y"]
        for j in range(cfg.SPO):
            g = 2 * j + qh
            out[b, g*512:(g+1)*512, :] = yc[j*512:(j+1)*512, :]
    return out


# version 17
# speedup vs baseline: 1.0005x; 1.0005x over previous
import math
from contextlib import ExitStack

import numpy as np
import ml_dtypes

import concourse.bass as bass
import concourse.mybir as mybir
import concourse.library_config as libcfg
from concourse.bass_utils import run_bass_kernel_spmd
from concourse.bass_core_rust import IndirectOffsetOnAxis

F32 = mybir.dt.float32
BF16 = mybir.dt.bfloat16
FP16 = mybir.dt.float16
I16 = mybir.dt.int16
I32 = mybir.dt.int32
AF = mybir.ActivationFunctionType
ALU = mybir.AluOpType
AX = mybir.AxisListType
BF16NP = ml_dtypes.bfloat16
ts = bass.ts


class CFG:
    def __init__(self, T=4096):
        self.B = 4
        self.T = T
        self.TO = T // 2
        self.D = 1024
        self.H = 16
        self.HD = 64
        self.RD = 32
        self.KVR = 256
        self.QR = 512
        self.E = 8
        self.I = 4096
        self.TT = self.B * T
        self.CAP = math.ceil(self.TT / self.E * 1.25)
        assert self.CAP % 128 == 0
        self.DCH = self.D // 128
        self.KVCH = self.KVR // 128
        self.QRCH = self.QR // 128
        self.SPB = T // 512
        self.SPO = self.SPB // 2
        self.NT_B = T // 128
        self.NT_O = self.TO // 128
        self.CAPT = self.CAP // 128
        self.ICH = self.I // 128
        self.ISL = self.I // 512
        self.SCH = []
        r = self.CAP
        while r > 0:
            c = min(512, r)
            self.SCH.append(c)
            r -= c
        self.QD = self.HD + self.RD
        self.MARGIN = 30.0
        self.KB = [8 * j + 8 for j in range(self.SPO)]


def _rope_tables(T, dim):
    inv_freq = 1.0 / 10000.0 ** (np.arange(0, dim, 2, dtype=np.float32) / dim)
    t = np.arange(T, dtype=np.float32)
    freqs = np.outer(t, inv_freq)
    emb = np.concatenate([freqs, freqs], -1)
    return np.cos(emb), np.sin(emb)


def host_consts(cfg, core):
    b, qh = core // 2, core % 2
    c = {}
    c["ident"] = np.eye(128, dtype=np.float32)
    c["identb"] = np.eye(128, dtype=np.float32).astype(BF16NP)
    tri = (np.arange(128)[:, None] <= np.arange(128)[None, :]).astype(np.float32)
    c["tri"] = tri.astype(BF16NP)
    mt = np.zeros((8, 128, 512), np.float32)
    for mi in range(8):
        if qh == 0:
            if mi < 4:
                mt[mi] = (np.arange(128)[:, None] <= (np.arange(512)[None, :] - 128 * mi))
        else:
            if mi < 4:
                mt[mi] = 1.0
            else:
                mt[mi] = (np.arange(128)[:, None] <= (np.arange(512)[None, :] - 128 * (mi - 4)))
    c["masktbl"] = np.ascontiguousarray(mt.transpose(1, 0, 2)).astype(BF16NP)
    cos, sin = _rope_tables(cfg.T, cfg.RD)
    c["cos4"] = np.ascontiguousarray(np.tile(cos.T, (4, 1))).astype(BF16NP)
    c["sin4"] = np.ascontiguousarray(np.tile(sin.T, (4, 1))).astype(BF16NP)
    h2 = cfg.RD // 2
    R = np.zeros((cfg.RD, cfg.RD), np.float32)
    for dd in range(h2):
        R[dd, dd + h2] = -1.0
        R[dd + h2, dd] = 1.0
    R4T = np.zeros((128, 128), np.float32)
    for g in range(4):
        R4T[g * 32:(g + 1) * 32, g * 32:(g + 1) * 32] = R.T
    c["r4t"] = R4T.astype(BF16NP)
    c["ones_colb"] = np.ones((128, 1), np.float32).astype(BF16NP)
    c["ones_rowb"] = np.ones((1, 128), np.float32).astype(BF16NP)
    c["ones_col"] = np.ones((128, 1), np.float32)
    c["ones_row"] = np.ones((1, 128), np.float32)
    e_i = np.tile(np.arange(cfg.E, dtype=np.float32), (128, 1))
    c["eidx1000"] = (e_i + 1000.0).astype(np.float32)
    c["eidxp1"] = (e_i + 1.0).astype(np.float32)
    pm = np.zeros((4 * cfg.SPB, cfg.SPO), np.float32)
    for j in range(cfg.SPO):
        g = b * cfg.SPB + 2 * j + qh
        pm[:g, j] = 1.0
    c["presum"] = pm
    pl = np.zeros((cfg.SPO, 4 * cfg.SPB), np.float32)
    for j in range(cfg.SPO):
        pl[j, b * cfg.SPB + 2 * j + qh] = 1.0
    c["placemat"] = pl
    agp1 = (cfg.TO * core + 1 + np.arange(cfg.TO, dtype=np.int32)).astype(np.int16)
    c["agp1"] = np.tile(agp1, (16, 1))
    hs = np.zeros((16, cfg.H * cfg.HD), np.float32)
    for h in range(cfg.H):
        hs[h, h * cfg.HD:(h + 1) * cfg.HD] = 1.0
    c["headsel"] = hs
    c["cmeta"] = np.array([[qh, cfg.SPB * b + qh, core, 0, 0, 0, 0, 0]], np.int32)
    nt_b, nt_o = cfg.NT_B, cfg.NT_O
    oid = np.zeros((1, nt_b), np.int32)
    for bt in range(nt_b):
        sp, tl = bt // 4, bt % 4
        oid[0, bt] = ((sp - qh) // 2) * 4 + tl if sp % 2 == qh else nt_o
    c["oid_tbl"] = oid
    osp = np.zeros((1, cfg.SPB), np.int32)
    for sp in range(cfg.SPB):
        osp[0, sp] = (sp - qh) // 2 if sp % 2 == qh else cfg.SPO
    c["osp_tbl"] = osp
    return c


class Sem:
    def __init__(self, stack, nc, name):
        self.h = stack.enter_context(nc.semaphore(name))
        self.v = 0

    def t(self, inst, n=1):
        inst.then_inc(self.h, n)
        self.v += n
        return self.v


class K:
    def __init__(self, cfg):
        self.cfg = cfg
        self.nc = bass.Bass("TRN2", target_bir_lowering=False)
        self.st = ExitStack()
        self.d = {}
        self.last_wait = {}
        self._freeq = []

    def DI(self, name, shape, dt=F32):
        t = self.nc.dram_tensor(name, list(shape), dt, kind="ExternalInput")
        self.d[name] = t
        return t

    def snap(self):
        return {id(s): s.v for s in self.all_sems}

    TSP_OPS = frozenset({
        "tensor_scalar", "tensor_single_scalar", "tensor_scalar_add",
        "tensor_scalar_sub", "tensor_scalar_mul", "tensor_scalar_max",
        "tensor_scalar_min", "scalar_tensor_tensor", "activation",
    })
    NOEMIT = frozenset({
        "wait_ge", "sem_inc", "value_load", "snap", "free_register",
        "alloc_register", "load_library", "maybe_wait_ge",
    })

    MANUAL_SEM = frozenset({
        "dma_start", "indirect_dma_start", "dma_start_transpose",
        "collective_compute", "local_scatter", "dma_reset", "trigger_dma",
    })

    class _RawProxy:

        def __init__(self, outer, eng, own):
            self._o, self._e, self._own = outer, eng, own
            self._lastv = own.v

        def __getattr__(self, name):
            attr = getattr(self._e, name)
            if not callable(attr) or name in K.NOEMIT:
                return attr

            def call(*a, **kw):
                if self._own.v != self._lastv:
                    self._e.wait_ge(self._own.h, self._own.v)
                    self._lastv = self._own.v
                r = attr(*a, **kw)
                return r
            return call

    class _FreeProxy:

        def __init__(self, outer, eng, own):
            self._o, self._e, self._own = outer, eng, own

        def __getattr__(self, name):
            return getattr(self._e, name)

    class _Proxy:

        def __init__(self, outer, eng, own):
            self._o, self._e, self._own = outer, eng, own
            self._after_tsp = False

        def __getattr__(self, name):
            attr = getattr(self._e, name)
            if not callable(attr):
                return attr
            if name in K.NOEMIT or name in K.MANUAL_SEM:
                return attr

            def call(*a, **kw):
                self._e.wait_ge(self._own.h, self._own.v)
                r = attr(*a, **kw)
                if hasattr(r, "then_inc"):
                    r.then_inc(self._own.h, 1)
                    self._own.v += 1
                return r
            return call

    def sec(self, eng_name, fn, snap=None, raw=False):
        own = self.eng_sem[eng_name]
        lw = self.last_wait.setdefault(eng_name, {})
        own_target = own.v

        def wrapped(eng):
            if own_target > 0:
                eng.wait_ge(own.h, own_target)
            for s in self.all_sems:
                if s is own:
                    continue
                tgt = s.v if snap is None else snap[id(s)]
                if tgt > 0 and lw.get(id(s), -1) < tgt:
                    eng.wait_ge(s.h, tgt)
                    lw[id(s)] = tgt
            if raw == "free" or (raw and eng_name == "tensor"):
                px = K._FreeProxy(self, eng, own)
            elif raw:
                px = K._RawProxy(self, eng, own)
            else:
                px = K._Proxy(self, eng, own)
            fn(px)
            for h in self._freeq:
                try:
                    eng.free_register(h)
                except Exception:
                    pass
            self._freeq = []
            eng.wait_ge(own.h, own.v)
            eng.sem_inc(own.h, 1)
        getattr(self.blk, eng_name)(wrapped)
        own.v += 1
        for k2 in self.last_wait:
            if k2 != eng_name:
                self.last_wait[k2].pop(id(own), None)

    def build(self):
        cfg, nc = self.cfg, self.nc
        T, TO, D, E, I, CAP = cfg.T, cfg.TO, cfg.D, cfg.E, cfg.I, cfg.CAP

        self.DI("x_b", (T, D))
        self.DI("x_own", (TO, D))
        self.DI("W_kv", (D, cfg.KVR)); self.DI("g_kv", (cfg.KVR,))
        self.DI("W_kdec", (cfg.KVR, D)); self.DI("W_vdec", (cfg.KVR, D))
        self.DI("W_q", (D, cfg.QR)); self.DI("W_qdec", (cfg.QR, D))
        self.DI("W_krope", (D, cfg.H * cfg.RD)); self.DI("W_qrope", (cfg.QR, cfg.H * cfg.RD))
        self.DI("W_o", (D, D))
        self.DI("Wr_T", (D, E)); self.DI("b_r", (E,))
        self.DI("Wi", (D, I)); self.DI("Wo", (I, D))
        for nm, sh, dt in [
            ("ident", (128, 128), F32), ("identb", (128, 128), BF16),
            ("tri", (128, 128), BF16),
            ("masktbl", (128, 8, 512), BF16),
            ("cos4", (128, T), BF16), ("sin4", (128, T), BF16),
            ("r4t", (128, 128), BF16), ("ones_colb", (128, 1), BF16),
            ("ones_rowb", (1, 128), BF16), ("ones_col", (128, 1), F32),
            ("ones_row", (1, 128), F32),
            ("eidx1000", (128, E), F32), ("eidxp1", (128, E), F32),
            ("presum", (4 * cfg.SPB, cfg.SPO), F32),
            ("placemat", (cfg.SPO, 4 * cfg.SPB), F32), ("agp1", (16, TO), I16),
            ("headsel", (16, cfg.H * cfg.HD), F32), ("cmeta", (1, 8), I32),
            ("oid_tbl", (1, cfg.NT_B), I32), ("osp_tbl", (1, cfg.SPB), I32),
        ]:
            self.DI(nm, sh, dt)
        self.y_out = nc.dram_tensor("y", [TO, D], F32, kind="ExternalOutput")

        self.xn_spill = nc.dram_tensor("xn_spill", [TO + 128, D], BF16)
        self.xn_bounce = nc.dram_tensor("xn_bounce", [T, D], BF16)
        self.q_dram = nc.dram_tensor("q_dram", [cfg.SPB, 128, cfg.QRCH * 512], BF16)
        self.qr_dram = nc.dram_tensor("qr_dram", [cfg.SPB, 128, 4 * 512], BF16)
        self.kr_dram = nc.dram_tensor("kr_dram", [128, 4, T], BF16)
        self.qro_dram = nc.dram_tensor("qro_dram", [128, 4, TO], BF16)
        self.lg_dram = nc.dram_tensor("lg_dram", [cfg.SPB, 128, 4 * E], F32)
        self.ag1_out = nc.dram_tensor("ag1_out", [8 * TO, D], BF16, addr_space="Shared")
        self.ar1_in = nc.dram_tensor("ar1_in", [4 * cfg.SPB, E], F32)
        self.ar1_out = nc.dram_tensor("ar1_out", [4 * cfg.SPB, E], F32)
        self.tbi_in = nc.dram_tensor("tbi_in", [16, CAP], I16)
        self.tbi_out = nc.dram_tensor("tbi_out", [128, CAP], I16, addr_space="Shared")
        self.tbg_in = nc.dram_tensor("tbg_in", [16, CAP], FP16)
        self.tbg_out = nc.dram_tensor("tbg_out", [128, CAP], FP16, addr_space="Shared")
        self.aoT_sp = nc.dram_tensor("aoT_sp", [D, TO], BF16)
        self.y_mid = nc.dram_tensor("y_mid", [TO, D], F32)
        self.wi_t = nc.dram_tensor("wi_t", [cfg.ISL, 128, cfg.DCH, 512], BF16)
        self.wo_t = nc.dram_tensor("wo_t", [2 * cfg.ISL, 128, 4, 512], BF16)
        self.ag2_in = nc.dram_tensor("ag2_in", [CAP, D], F32)
        self.ag2_out = nc.dram_tensor("ag2_out", [8 * CAP, D], F32, addr_space="Shared")

        S0 = self.st
        SB = lambda n, sh, dt: S0.enter_context(nc.sbuf_tensor(n, list(sh), dt))
        self.b_ident = SB("b_ident", (128, 128), F32)
        self.b_identb = SB("b_identb", (128, 128), BF16)
        self.b_tri = SB("b_tri", (128, 128), BF16)
        self.b_mask = SB("b_mask", (128, 8, 512), BF16)
        self.b_r4t = SB("b_r4t", (128, 128), BF16)
        self.b_ocb = SB("b_ocb", (128, 1), BF16)
        self.b_orb = SB("b_orb", (1, 128), BF16)
        self.b_oc = SB("b_oc", (128, 1), F32)
        self.b_or = SB("b_or", (1, 128), F32)
        self.b_e1000 = SB("b_e1000", (128, E), F32)
        self.b_ep1 = SB("b_ep1", (128, E), F32)
        self.b_presum = SB("b_presum", (4 * cfg.SPB, cfg.SPO), F32)
        self.b_place = SB("b_place", (cfg.SPO, 4 * cfg.SPB), F32)
        self.b_agp1 = SB("b_agp1", (16, TO), I16)
        self.b_hsel = SB("b_hsel", (16, cfg.H * cfg.HD), F32)
        self.b_cmeta = {e: SB(f"b_cmeta_{e}", (1, 8), I32)
                        for e in ("sync", "gpsimd", "vector", "tensor", "scalar")}

        self.b_gkv = SB("b_gkv", (128, cfg.KVCH), F32)
        self.b_br = SB("b_br", (E, 1), F32)

        self.r_lg = SB("r_lg", (128, cfg.NT_O, E), F32)
        self.r_w = SB("r_w", (128, cfg.NT_O, E), F32)
        self.r_tk = SB("r_tk", (128, cfg.NT_O, E), F32)
        self.r_M = SB("r_M", (128, cfg.NT_O, E), BF16)
        self.r_C = SB("r_C", (128, cfg.NT_O, E), F32)
        self.g_row = SB("g_row", (128, cfg.NT_O, 2), F32)
        self.g_val = SB("g_val", (128, cfg.NT_O, 2), F32)
        self.e_idx = SB("e_idx", (128, cfg.CAPT), I32)
        self.e_gate = SB("e_gate", (128, cfg.CAPT), F32)
        self.s_recip = SB("s_recip", (16, TO), F32)
        self.scr = SB("scr", (128, 32), F32)
        self.rrow = SB("rrow", (1, 4 * E), F32)
        self.rrowb = SB("rrowb", (1, 2 * E), BF16)
        self.spsum = SB("spsum", (1, cfg.SPO * E), F32)
        self.spsum2 = SB("spsum2", (cfg.SPO, E), F32)
        self.ar_sb = SB("ar_sb", (4 * cfg.SPB, E), F32)
        self.sbase = SB("sbase", (1, cfg.SPO * E), F32)
        self.cntrow = SB("cntrow", (1, cfg.NT_O * E), F32)
        self.rbrow = SB("rbrow", (1, cfg.NT_O * E), F32)

        PS = lambda n, sh: self.st.enter_context(nc.psum_tensor(n, list(sh), F32))
        self.psG3a = PS("psG3a", (128, 1536))
        self.psG3b = PS("psG3b", (128, 1536))
        self.psC1 = PS("psC1", (128, 512))
        self.psT = PS("psT", (128, 128))
        self.ps = [self.psG3a[:, 0:512], self.psG3a[:, 512:1024], self.psG3a[:, 1024:1536],
                   self.psG3b[:, 0:512], self.psG3b[:, 512:1024], self.psG3b[:, 1024:1536],
                   self.psC1[:, 0:512]]

        st = self.st
        self.sD = Sem(st, nc, "sD")
        self.sG = Sem(st, nc, "sG")
        self.sP = Sem(st, nc, "sP")
        self.sV = Sem(st, nc, "sV")
        self.sS = Sem(st, nc, "sS")
        self.sQ = Sem(st, nc, "sQ")
        self.sC = Sem(st, nc, "sC")
        self.sX = Sem(st, nc, "sX")
        self.all_sems = [self.sD, self.sG, self.sP, self.sV, self.sS, self.sQ, self.sC, self.sX]
        self.eng_sem = {"sync": self.sD, "gpsimd": self.sQ, "tensor": self.sP,
                        "vector": self.sV, "scalar": self.sS}

        with nc.Block() as blk:
            self.blk = blk
            with nc.named_scope("phL"):
                self.ph_load()
            with ExitStack() as RA:
                SB2 = lambda n, sh, dt: RA.enter_context(nc.sbuf_tensor(n, list(sh), dt))
                self.kv_nT = SB2("kv_nT", (128, cfg.KVCH, T), BF16)
                self.q_cT = SB2("q_cT", (128, cfg.QRCH, TO), BF16)
                with nc.named_scope("phR"):
                    self.ph_R()
                with nc.named_scope("phRoute"):
                    self.ph_route()
                with nc.named_scope("phAttn"):
                    self.ph_attn()
                with nc.named_scope("phRoute2"):
                    self.ph_route2()
            with nc.named_scope("phWo"):
                self.ph_wo()
            with nc.named_scope("phMoe"):
                self.ph_moe()
            with nc.named_scope("phFinal"):
                self.ph_final()
        self.st.close()
        mybir.codegen_inst_isa_subclasses(nc)
        return nc

    def _vload(self, eng, ap, lo, hi):
        val = eng.value_load(ap, min_val=lo, max_val=hi)
        try:
            self._freeq.append(val.val)
        except Exception:
            pass
        return val

    def ph_load(self):
        d = self.d

        def f(s):
            for nm, dst in [
                ("ident", self.b_ident), ("identb", self.b_identb),
                ("tri", self.b_tri), ("masktbl", self.b_mask),
                ("r4t", self.b_r4t), ("ones_colb", self.b_ocb), ("ones_rowb", self.b_orb),
                ("ones_col", self.b_oc), ("ones_row", self.b_or),
                ("eidx1000", self.b_e1000), ("eidxp1", self.b_ep1),
                ("presum", self.b_presum), ("placemat", self.b_place),
                ("agp1", self.b_agp1),
                ("headsel", self.b_hsel),
            ]:
                self.sD.t(s.dma_start(out=dst[:], in_=d[nm].ap()), 16)
            self.sD.t(s.dma_start(out=self.b_br[:, :],
                                  in_=d["b_r"].ap().rearrange("(e o) -> e o", o=1)), 16)
            for dd in self.b_cmeta.values():
                self.sD.t(s.dma_start(out=dd[:, :], in_=d["cmeta"].ap()), 16)

            self.sD.t(s.dma_start(out=self.b_gkv[:, :],
                                  in_=d["g_kv"].ap().rearrange("(c p) -> p c", p=128),
                                  allow_slow_non_contiguous=True), 16)
        self.sec("sync", f)

    def ph_R(self):
        cfg, d = self.cfg, self.d
        T, TO, D, E = cfg.T, cfg.TO, cfg.D, cfg.E
        DCH, KVCH, QRCH, SPB = cfg.DCH, cfg.KVCH, cfg.QRCH, cfg.SPB
        sP, sV = self.sP, self.sV
        ps = self.ps

        with ExitStack() as RS:
            SB = lambda n, sh, dt: RS.enter_context(self.nc.sbuf_tensor(n, list(sh), dt))
            w_kv = SB("w_kv", (128, DCH, cfg.KVR), BF16)
            w_q = SB("w_q", (128, DCH, cfg.QR), BF16)
            w_kro = SB("w_kro", (128, DCH, cfg.H * cfg.RD), BF16)
            w_qro = SB("w_qro", (128, QRCH, cfg.H * cfg.RD), BF16)
            w_rt = SB("w_rt", (128, DCH, E), BF16)
            x_sp = SB("x_sp", (128, 2, 4, D), F32)
            sqs = SB("sqs", (128, D), F32)
            xn_bf = SB("xn_bf", (128, 2, 4, D), BF16)
            lnsq = SB("lnsq", (128, 2, 4), F32)
            lnm = SB("lnm", (128, 2, 4), F32)
            m_col = SB("m_col", (128, 2, 4), F32)
            r_col = SB("r_col", (128, 2, 4), F32)
            vtmp = SB("vtmp", (128, 4), F32)
            xfm = SB("xfm", (128, DCH, 512), BF16)
            kvsp = SB("kvsp", (128, KVCH, 512), BF16)
            sqv = SB("sqv", (128, KVCH, 512), F32)
            rmsrow = SB("rmsrow", (1, 512), F32)
            q_sp = SB("q_sp", (128, 2, QRCH, 512), BF16)
            cr_t = SB("cr_t", (128, 2, 512), BF16)
            sr_t = SB("sr_t", (128, 2, 512), BF16)
            rp_a = SB("rp_a", (128, 2, 512), BF16)
            rp_b = SB("rp_b", (128, 512), F32)
            rp_c = SB("rp_c", (128, 512), F32)
            lgev = SB("lgev", (E, 512), F32)
            qr_sp = SB("qr_sp", (128, 2, 4, 512), BF16)
            kr_sp = SB("kr_sp", (128, 2, 4, 512), BF16)
            lg_sp = SB("lg_sp", (128, 2, 4, E), F32)

            def f(g):
                for nm, dst in [("W_kv", w_kv), ("W_q", w_q), ("W_krope", w_kro),
                                ("W_qrope", w_qro), ("Wr_T", w_rt)]:
                    self.sG.t(g.dma_start(out=dst[:],
                                          in_=d[nm].ap().rearrange("(c p) o -> p c o", p=128)), 16)
            self.sec("gpsimd", f)

            def f(v):
                v.tensor_scalar(w_qro[:, :, :], w_qro[:, :, :],
                                1.0 / math.sqrt(float(cfg.QD)), None, ALU.mult)
            self.sec("vector", f)

            sG, sD, sS = self.sG, self.sD, self.sS
            gsn0 = self.snap()
            norm_done = {}
            x_dma = {}
            cs_dma = {}
            spill_done = {}
            evac = {}

            def emit_x(sp, wait_v):
                def f(s, sp=sp, wait_v=wait_v):
                    if wait_v:
                        s.wait_ge(sV.h, wait_v)
                    self.sD.t(s.dma_start(
                        out=x_sp[:, sp % 2, :, :],
                        in_=d["x_b"].ap()[sp*512:(sp+1)*512, :]
                        .rearrange("(a p) b -> p a b", p=128)), 16)
                    x_dma[sp] = sD.v
                    self.sD.t(s.dma_start(out=cr_t[:, sp % 2, :],
                                          in_=d["cos4"].ap()[:, sp*512:(sp+1)*512]), 16)
                    self.sD.t(s.dma_start(out=sr_t[:, sp % 2, :],
                                          in_=d["sin4"].ap()[:, sp*512:(sp+1)*512]), 16)
                    cs_dma[sp] = sD.v
                self.sec("sync", f, snap=gsn0, raw="free")

            def emit_stats_sq(sp):
                def f(sc, sp=sp):
                    sc.wait_ge(sD.h, x_dma[sp])
                    for tl in range(4):
                        self.sS.t(sc.activation(out=sqs[:, :], in_=x_sp[:, sp % 2, tl, :],
                                                func=AF.Square,
                                                accum_out=lnsq[:, sp % 2, tl:tl+1]))
                self.sec("scalar", f, snap=gsn0, raw="free")
                return sS.v

            def emit_stats_vec(sp, sq_val):
                def f(v, sp=sp, sq_val=sq_val):
                    v.wait_ge(sD.h, x_dma[sp])
                    par = sp % 2
                    for tl in range(4):
                        sV.t(v.tensor_reduce(lnm[:, par, tl:tl+1], x_sp[:, par, tl, :],
                                             AX.X, ALU.add))
                    v.wait_ge(sS.h, sq_val)
                    sV.t(v.tensor_scalar(m_col[:, par, :], lnm[:, par, :], 1.0 / D,
                                         None, ALU.mult))
                    sV.t(v.tensor_scalar(vtmp[:, :], lnsq[:, par, :], 1.0 / D,
                                         None, ALU.mult))
                    v.wait_ge(sV.h, sV.v)
                    sV.t(v.tensor_mul(r_col[:, par, :], m_col[:, par, :], m_col[:, par, :]))
                    v.wait_ge(sV.h, sV.v)
                    sV.t(v.tensor_sub(vtmp[:, :], vtmp[:, :], r_col[:, par, :]))
                    v.wait_ge(sV.h, sV.v)
                    sV.t(v.tensor_scalar(vtmp[:, :], vtmp[:, :], 1e-5, None, ALU.add))
                    v.wait_ge(sV.h, sV.v)
                    sV.t(v.reciprocal(r_col[:, par, :], vtmp[:, :]))
                self.sec("vector", f, snap=gsn0, raw="free")
                return sV.v

            def emit_stats_rst(sp, var_val):
                def f(sc, sp=sp, var_val=var_val):
                    sc.wait_ge(sV.h, var_val)
                    self.sS.t(sc.activation(out=r_col[:, sp % 2, :], in_=r_col[:, sp % 2, :],
                                            func=AF.Sqrt))
                self.sec("scalar", f, snap=gsn0, raw="free")
                return sS.v

            def emit_norm(sp, rst_val, xn_spill_val):
                vals = []

                def f(v, sp=sp, rst_val=rst_val):
                    par = sp % 2
                    v.wait_ge(sS.h, rst_val)
                    if xn_spill_val:
                        v.wait_ge(sG.h, xn_spill_val)
                    for tl in range(4):
                        sV.t(v.tensor_scalar(xn_bf[:, par, tl, :], x_sp[:, par, tl, :],
                                             m_col[:, par, tl:tl+1], r_col[:, par, tl:tl+1],
                                             ALU.subtract, ALU.mult))
                        vals.append(sV.v)
                self.sec("vector", f, snap=gsn0, raw="free")
                norm_done[sp] = list(vals)

            emit_x(0, 0)
            sq0 = emit_stats_sq(0)
            var0 = emit_stats_vec(0, sq0)
            rst0 = emit_stats_rst(0, var0)
            emit_norm(0, rst0, 0)
            emit_x(1, 0)

            for sp in range(SPB):
                par = sp % 2
                p0, v0 = sP.v, sV.v

                def f(t, sp=sp, par=par, v0=v0):
                    for i in range(32):
                        tl, fc = divmod(i, 8)
                        if fc == 0:
                            t.wait_ge(sV.h, norm_done[sp][tl])
                        if i >= 2:
                            t.wait_ge(sV.h, v0 + i - 1)
                        dst = (self.psT[:, :] if i % 2 == 0
                               else ps[2]).bitcast(BF16)[0:128, 0:128]
                        sP.t(t.transpose(dst, xn_bf[:, par, tl, fc*128:(fc+1)*128],
                                         self.b_identb[:, :]))
                self.sec("tensor", f, snap=gsn0, raw="free")

                def f(v, p0=p0):
                    for i in range(32):
                        tl, fc = divmod(i, 8)
                        v.wait_ge(sP.h, p0 + i + 1)
                        src = (self.psT[:, :] if i % 2 == 0
                               else ps[2]).bitcast(BF16)[0:128, 0:128]
                        sV.t(v.tensor_copy(xfm[:, fc, tl*128:(tl+1)*128], src))
                self.sec("vector", f, snap=gsn0, raw="free")
                te_last = sV.v

                p0, v0 = sP.v, sV.v

                def f(t, v0=v0, te_last=te_last):
                    t.wait_ge(sV.h, te_last)
                    if ("kvn", 1) in evac:
                        t.wait_ge(sV.h, evac[("kvn", 1)])
                    for kvc in range(KVCH):
                        for fc in range(DCH):
                            inst = t.matmul(ps[kvc][:, :], w_kv[:, fc, kvc*128:(kvc+1)*128],
                                            xfm[:, fc, :], start=(fc == 0), stop=(fc == DCH - 1))
                        sP.t(inst)
                self.sec("tensor", f, snap=gsn0, raw="free")

                def f(v, p0=p0):
                    for kvc in range(KVCH):
                        v.wait_ge(sP.h, p0 + kvc + 1)
                        sV.t(v.tensor_copy(kvsp[:, kvc, :], ps[kvc][:, :]))
                self.sec("vector", f, snap=gsn0, raw="free")
                kvsp_done = sV.v

                def f(sc, sp=sp, kvsp_done=kvsp_done):
                    sc.wait_ge(sV.h, kvsp_done)
                    for kvc in range(KVCH):
                        self.sS.t(sc.activation(out=sqv[:, kvc, :], in_=kvsp[:, kvc, :],
                                                func=AF.Square))
                self.sec("scalar", f, snap=gsn0, raw="free")
                sqv_done = sS.v
                sqn = emit_stats_sq(sp + 1) if sp + 1 < SPB else 0

                p0, v0 = sP.v, sV.v

                def f(t, sp=sp, kvsp_done=kvsp_done):
                    if ("qev", 3) in evac:
                        t.wait_ge(sV.h, evac[("qev", 3)])
                    for qc in range(QRCH):
                        if qc < 2:
                            t.wait_ge(sV.h, kvsp_done)
                        for fc in range(DCH):
                            inst = t.matmul(ps[qc % 2][:, :], w_q[:, fc, qc*128:(qc+1)*128],
                                            xfm[:, fc, :], start=(fc == 0), stop=(fc == DCH - 1))
                        sP.t(inst)
                self.sec("tensor", f, snap=gsn0, raw="free")

                def f(v, sp=sp, par=par, p0=p0):
                    if ("qspill", sp - 2) in spill_done:
                        v.wait_ge(sG.h, spill_done[("qspill", sp - 2)])
                    for qc in range(QRCH):
                        v.wait_ge(sP.h, p0 + qc + 1)
                        sV.t(v.tensor_copy(q_sp[:, par, qc, :], ps[qc % 2][:, :]))
                        evac[("qev", qc)] = sV.v
                self.sec("vector", f, snap=gsn0, raw="free")

                if sp + 1 < SPB:
                    varn = emit_stats_vec(sp + 1, sqn)

                p0 = sP.v

                def f(t, sp=sp, sqv_done=sqv_done):
                    t.wait_ge(sS.h, sqv_done)
                    if ("kvn", 1) in evac:
                        t.wait_ge(sV.h, evac[("kvn", 1)])
                    for kvc in range(KVCH):
                        inst = t.matmul(ps[5][0:1, :], self.b_oc[:, 0:1], sqv[:, kvc, :],
                                        start=(kvc == 0), stop=(kvc == KVCH - 1))
                    sP.t(inst)
                self.sec("tensor", f, snap=gsn0, raw="free")

                def f(v, p0=p0):
                    v.wait_ge(sP.h, p0 + 1)
                    sV.t(v.tensor_scalar(rmsrow[:, :], ps[5][0:1, :], 1.0 / cfg.KVR, 1e-6,
                                         ALU.mult, ALU.add))
                    v.wait_ge(sV.h, sV.v)
                    sV.t(v.reciprocal(rmsrow[:, :], rmsrow[:, :]))
                self.sec("vector", f, snap=gsn0, raw="free")
                rms_scaled = sV.v

                def f(sc, rms_scaled=rms_scaled):
                    sc.wait_ge(sV.h, rms_scaled)
                    self.sS.t(sc.activation(out=rmsrow[:, :], in_=rmsrow[:, :], func=AF.Sqrt))
                self.sec("scalar", f, snap=gsn0, raw="free")
                rmsr_done = sS.v
                rstn = emit_stats_rst(sp + 1, varn) if sp + 1 < SPB else 0

                pe_at = {}
                c = 0
                for g4 in range(4):
                    c += 1
                    pe_at[("proj", g4)] = c
                    if g4 >= 1:
                        c += 1
                        pe_at[("rot", g4 - 1)] = c
                c += 1
                pe_at[("rot", 3)] = c

                def emit_rope_pair(wsrc, nw, dstbuf, kind):
                    p0, v0 = sP.v, sV.v
                    rpa_at = lambda g4: v0 + 4 * g4 + 1
                    comb_at = lambda g4: v0 + 4 * g4 + 4

                    def f(t, sp=sp, par=par, wsrc=wsrc, nw=nw):
                        for g4 in range(4):
                            if g4 >= 2:
                                t.wait_ge(sV.h, rpa_at(g4 - 2))
                            for w in range(nw):
                                rhs = xfm[:, w, :] if nw == DCH else q_sp[:, par, w, :]
                                inst = t.matmul(ps[g4 % 2][:, :], wsrc[:, w, g4*128:(g4+1)*128],
                                                rhs, start=(w == 0), stop=(w == nw - 1))
                            sP.t(inst)
                            if g4 >= 1:
                                t.wait_ge(sV.h, rpa_at(g4 - 1))
                                if g4 >= 3:
                                    t.wait_ge(sV.h, comb_at(g4 - 3))
                                sP.t(t.matmul(ps[3 + ((g4 - 1) % 2)][:, :], self.b_r4t[:, :],
                                              rp_a[:, (g4 - 1) % 2, :], start=True, stop=True))
                        t.wait_ge(sV.h, rpa_at(3))
                        t.wait_ge(sV.h, comb_at(1))
                        sP.t(t.matmul(ps[3 + (3 % 2)][:, :], self.b_r4t[:, :],
                                      rp_a[:, 3 % 2, :], start=True, stop=True))
                    self.sec("tensor", f, snap=gsn0, raw="free")

                    def f(v, sp=sp, par=par, p0=p0, dstbuf=dstbuf, kind=kind):
                        if (kind, sp - 2) in spill_done:
                            v.wait_ge(sG.h, spill_done[(kind, sp - 2)])
                        for g4 in range(4):
                            v.wait_ge(sP.h, p0 + pe_at[("proj", g4)])
                            sV.t(v.tensor_copy(rp_a[:, g4 % 2, :], ps[g4 % 2][:, :]))
                            v.wait_ge(sP.h, p0 + pe_at[("rot", g4)])
                            sV.t(v.tensor_mul(rp_b[:, :], ps[3 + (g4 % 2)][:, :],
                                              sr_t[:, par, :]))
                            sV.t(v.tensor_mul(rp_c[:, :], rp_a[:, g4 % 2, :], cr_t[:, par, :]))
                            v.wait_ge(sV.h, sV.v)
                            sV.t(v.tensor_add(dstbuf[:, par, g4, :], rp_c[:, :], rp_b[:, :]))
                    self.sec("vector", f, snap=gsn0, raw="free")
                    return comb_at(3)

                combk = emit_rope_pair(w_kro, DCH, kr_sp, "krspill")

                p0 = sP.v

                def f(t, rmsr_done=rmsr_done):
                    t.wait_ge(sS.h, rmsr_done)
                    sP.t(t.matmul(ps[5][:, :], self.b_or[:, :], rmsrow[:, :],
                                  start=True, stop=True))
                self.sec("tensor", f, snap=gsn0, raw="free")

                def f(v, sp=sp, p0=p0):
                    v.wait_ge(sP.h, p0 + 1)
                    for kvc in range(KVCH):
                        sV.t(v.scalar_tensor_tensor(self.kv_nT[:, kvc, sp*512:(sp+1)*512],
                                                    kvsp[:, kvc, :], self.b_gkv[:, kvc:kvc+1],
                                                    ps[5][:, :], ALU.mult, ALU.mult))
                        evac[("kvn", kvc)] = sV.v
                self.sec("vector", f, snap=gsn0, raw="free")

                combq = emit_rope_pair(w_qro, QRCH, qr_sp, "qrspill")

                p0 = sP.v

                def f(t):
                    for fc in range(DCH):
                        inst = t.matmul(ps[0][0:E, :], w_rt[:, fc, :], xfm[:, fc, :],
                                        start=(fc == 0), stop=(fc == DCH - 1))
                    sP.t(inst)
                self.sec("tensor", f, snap=gsn0, raw="free")

                def f(v, p0=p0):
                    v.wait_ge(sP.h, p0 + 1)
                    sV.t(v.tensor_scalar(lgev[:, :], ps[0][0:E, :], self.b_br[:, 0:1],
                                         None, ALU.add))
                self.sec("vector", f, snap=gsn0, raw="free")
                lgev_done = sV.v

                p0, v0 = sP.v, sV.v

                def f(t, te_last=te_last, lgev_done=lgev_done, v0=v0):
                    t.wait_ge(sV.h, lgev_done)
                    for tl in range(4):
                        if tl >= 2:
                            t.wait_ge(sV.h, v0 + tl - 1)
                        pp = self.psT[0:128, 0:E] if tl % 2 == 0 else ps[2][0:128, 0:E]
                        sP.t(t.transpose(pp, lgev[:, tl*128:(tl+1)*128],
                                         self.b_ident[0:E, 0:E]))
                self.sec("tensor", f, snap=gsn0, raw="free")

                def f(v, sp=sp, par=par, p0=p0):
                    if ("lgspill", sp - 2) in spill_done:
                        v.wait_ge(sG.h, spill_done[("lgspill", sp - 2)])
                    for tl in range(4):
                        v.wait_ge(sP.h, p0 + tl + 1)
                        pp = self.psT[0:128, 0:E] if tl % 2 == 0 else ps[2][0:128, 0:E]
                        sV.t(v.tensor_copy(lg_sp[:, par, tl, :], pp))
                self.sec("vector", f, snap=gsn0, raw="free")
                lg_done = sV.v

                if sp + 1 < SPB:
                    emit_norm(sp + 1, rstn, spill_done.get(("xnspill", sp - 1), 0))

                qev3 = evac[("qev", 3)]

                def f(g, sp=sp, par=par, qev3=qev3, combk=combk, combq=combq,
                      lg_done=lg_done):
                    g.wait_ge(sV.h, norm_done[sp][3])
                    self.sG.t(g.dma_start(
                        out=self.xn_bounce.ap()[sp*512:(sp+1)*512, :]
                        .rearrange("(a p) b -> p a b", p=128),
                        in_=xn_bf[:, par, :, :]), 16)
                    spill_done[("xnspill", sp)] = sG.v
                    g.wait_ge(sV.h, qev3)
                    self.sG.t(g.dma_start(
                        out=self.q_dram.ap()[sp, :, :],
                        in_=q_sp[:, par, :, :].rearrange("p a b -> p (a b)")), 16)
                    spill_done[("qspill", sp)] = sG.v
                    g.wait_ge(sV.h, combk)
                    self.sG.t(g.dma_start(
                        out=self.kr_dram.ap()[:, :, sp*512:(sp+1)*512],
                        in_=kr_sp[:, par, :, :]), 16)
                    spill_done[("krspill", sp)] = sG.v
                    g.wait_ge(sV.h, combq)
                    self.sG.t(g.dma_start(
                        out=self.qr_dram.ap()[sp, :, :],
                        in_=qr_sp[:, par, :, :].rearrange("p a b -> p (a b)")), 16)
                    spill_done[("qrspill", sp)] = sG.v
                    g.wait_ge(sV.h, lg_done)
                    self.sG.t(g.dma_start(
                        out=self.lg_dram.ap()[sp, :, :],
                        in_=lg_sp[:, par, :, :].rearrange("p a b -> p (a b)")), 16)
                    spill_done[("lgspill", sp)] = sG.v
                self.sec("gpsimd", f, snap=gsn0, raw="free")

                if sp + 2 < SPB:
                    emit_x(sp + 2, combq)

        cfg2 = self.cfg

        def f(g):
            qh = self._vload(g, self.b_cmeta['gpsimd'][0:1, 0:1], 0, 1)
            for j in range(cfg2.SPO):
                gsp = g.snap(qh + 2 * j, min_val=0, max_val=cfg2.SPB - 1)
                self.sG.t(g.dma_start(
                    out=self.q_cT[:, :, j*512:(j+1)*512],
                    in_=self.q_dram.ap()[ts(gsp, 1), :, :]
                    .rearrange("a p (c b) -> p (a c) b", c=cfg2.QRCH)), 16)
                self.sG.t(g.dma_start(
                    out=self.xn_spill.ap()[j*512:(j+1)*512, :],
                    in_=self.xn_bounce.ap()[ts(gsp, 512), :]), 16)
        self.sec("gpsimd", f)

        def f(sc):
            qh = self._vload(sc, self.b_cmeta['scalar'][0:1, 0:1], 0, 1)
            for j in range(cfg2.SPO):
                gsp = sc.snap(qh + 2 * j, min_val=0, max_val=cfg2.SPB - 1)
                self.sD.t(sc.dma_start(
                    out=self.qro_dram.ap()[:, :, j*512:(j+1)*512],
                    in_=self.qr_dram.ap()[ts(gsp, 1), :, :]
                    .rearrange("a p (c b) -> p (a c) b", c=4)), 16)
                self.sD.t(sc.dma_start(
                    out=self.r_lg[:, 4*j:4*(j+1), :],
                    in_=self.lg_dram.ap()[ts(gsp, 1), :, :]
                    .rearrange("a p (c b) -> p (a c) b", c=4)), 16)
        self.sec("scalar", f, raw=True)

    def ph_route(self):
        cfg = self.cfg
        E, CAP, TO = cfg.E, cfg.CAP, cfg.TO
        sV, sP = self.sV, self.sP
        ps = self.ps

        def f(g):
            for isl in range(self.cfg.ISL):
                self.sG.t(g.dma_start(
                    out=self.wi_t.ap()[isl],
                    in_=self.d["Wi"].ap()[:, isl*512:(isl+1)*512]
                    .rearrange("(c p) o -> p c o", p=128)), 16)
            for k2 in range(2 * self.cfg.ISL):
                Fh, isl = k2 // self.cfg.ISL, k2 % self.cfg.ISL
                self.sG.t(g.dma_start(
                    out=self.wo_t.ap()[k2],
                    in_=self.d["Wo"].ap()[isl*512:(isl+1)*512, Fh*512:(Fh+1)*512]
                    .rearrange("(c p) o -> p c o", p=128)), 16)
        self.sec("gpsimd", f)

        if True:
            cntrow, rbrow = self.cntrow, self.rbrow
            rsn = self.snap()
            v0, s0, p0 = sV.v, self.sS.v, self.sP.v
            S1 = lambda tt: s0 + tt + 1
            VA1, VB1 = {}, {}
            PB = lambda tt: (0, 8) if tt % 2 == 0 else (16, 24)

            def f(v):
                def A(tt):
                    b, m8 = PB(tt)
                    if tt >= 2:
                        v.wait_ge(self.sS.h, S1(tt - 2))
                    sV.t(v.reduce_max(self.scr[:, b:b+1], self.r_lg[:, tt, :], AX.X))
                    v.wait_ge(sV.h, sV.v)
                    sV.t(v.tensor_scalar(self.scr[:, b+1:b+2], self.scr[:, b:b+1], -1.0,
                                         None, ALU.mult))
                    VA1[tt] = sV.v
                A(0); A(1)
                for tt in range(cfg.NT_O):
                    b, m8 = PB(tt)
                    v.wait_ge(self.sS.h, S1(tt))
                    sV.t(v.reciprocal(self.scr[:, b+3:b+4], self.scr[:, b+2:b+3]))
                    v.wait_ge(sV.h, sV.v)
                    sV.t(v.tensor_scalar(self.r_w[:, tt, :], self.r_w[:, tt, :],
                                         self.scr[:, b+3:b+4], None, ALU.mult))
                    v.wait_ge(sV.h, sV.v)
                    sV.t(v.max(out=self.scr[:, m8:m8+8], in_=self.r_w[:, tt, :]))
                    v.wait_ge(sV.h, sV.v)
                    sV.t(v.scalar_tensor_tensor(self.r_tk[:, tt, :], self.r_w[:, tt, :],
                                                self.scr[:, m8+1:m8+2], self.r_w[:, tt, :],
                                                ALU.is_ge, ALU.mult))
                    v.wait_ge(sV.h, sV.v)
                    sV.t(v.tensor_scalar(self.r_M[:, tt, :], self.r_tk[:, tt, :], 0.0,
                                         None, ALU.is_gt))
                    VB1[tt] = sV.v
                    if tt + 2 < cfg.NT_O:
                        A(tt + 2)
                v.wait_ge(self.sP.h, p0 + cfg.NT_O)
                sV.t(v.tensor_copy(cntrow[:, :], ps[3][0:1, 0:cfg.NT_O * E]))
                for j in range(cfg.SPO):
                    v.wait_ge(sV.h, sV.v)
                    sV.t(v.tensor_add(self.spsum[:, j*E:(j+1)*E],
                                      cntrow[:, (4*j)*E:(4*j+1)*E],
                                      cntrow[:, (4*j+1)*E:(4*j+2)*E]))
                    v.wait_ge(sV.h, sV.v)
                    sV.t(v.tensor_add(self.spsum[:, j*E:(j+1)*E],
                                      self.spsum[:, j*E:(j+1)*E],
                                      cntrow[:, (4*j+2)*E:(4*j+3)*E]))
                    v.wait_ge(sV.h, sV.v)
                    sV.t(v.tensor_add(self.spsum[:, j*E:(j+1)*E],
                                      self.spsum[:, j*E:(j+1)*E],
                                      cntrow[:, (4*j+3)*E:(4*j+4)*E]))
            self.sec("vector", f, snap=rsn, raw="free")

            def f(sc):
                for tt in range(cfg.NT_O):
                    b, m8 = PB(tt)
                    sc.wait_ge(sV.h, VA1[tt])
                    sc.wait_ge(self.sS.h, self.sS.v)
                    self.sS.t(sc.activation(out=self.r_w[:, tt, :], in_=self.r_lg[:, tt, :],
                                            func=AF.Exp, bias=self.scr[:, b+1:b+2],
                                            accum_out=self.scr[:, b+2:b+3]))
            self.sec("scalar", f, snap=rsn, raw="free")

            def f(t):
                for tt in range(cfg.NT_O):
                    t.wait_ge(sV.h, VB1[tt])
                    sP.t(t.matmul(ps[3][0:1, tt*E:(tt+1)*E], self.b_ocb[:, 0:1],
                                  self.r_M[:, tt, :], start=True, stop=True))
            self.sec("tensor", f, snap=rsn, raw="free")

            with ExitStack() as XS:
                r_ar = XS.enter_context(self.nc.sbuf_tensor("r_ar", [4 * cfg.SPB, E], F32))

                def f(g):
                    for j in range(cfg.SPO):
                        self.sG.t(g.dma_start(out=self.spsum2[j:j+1, :],
                                              in_=self.spsum[0:1, j*E:(j+1)*E]), 16)
                self.sec("gpsimd", f)

                def f(t):
                    t.matmul(ps[2][0:4*cfg.SPB, 0:E], self.b_place[:, :], self.spsum2[:, :],
                             start=True, stop=True)
                self.sec("tensor", f)

                def f(v):
                    v.tensor_copy(self.ar_sb[:, :], ps[2][0:4*cfg.SPB, 0:E])
                self.sec("vector", f)

                def f(g):
                    self.sG.t(g.dma_start(out=self.ar1_in.ap(), in_=self.ar_sb[:, :]), 16)
                self.sec("gpsimd", f)

                def f(g):
                    self.sC.t(g.collective_compute(
                        "AllReduce", ALU.add, replica_groups=[list(range(8))],
                        ins=[self.ar1_in.ap().opt()], outs=[self.ar1_out.ap().opt()]), 1)
                    self.sC.t(g.collective_compute(
                        "AllGather", ALU.bypass, replica_groups=[list(range(8))],
                        ins=[self.xn_spill.ap()[0:TO, :].opt()],
                        outs=[self.ag1_out.ap().opt()]), 1)
                self.sec("gpsimd", f)

    def ph_route2(self):
        cfg = self.cfg
        E, CAP, TO = cfg.E, cfg.CAP, cfg.TO
        sV, sP = self.sV, self.sP
        ps = self.ps
        cntrow, rbrow = self.cntrow, self.rbrow
        with ExitStack() as XS:
            r_ar = XS.enter_context(self.nc.sbuf_tensor("r_ar2", [4 * cfg.SPB, E], F32))
            if True:
                def f(s):
                    self.sD.t(s.dma_start(out=r_ar[:, :], in_=self.ar1_out.ap()), 16)
                self.sec("sync", f)

                for j in range(cfg.SPO):
                    def f(t, j=j):
                        t.matmul(ps[2][0:1, 0:E], self.b_presum[:, j:j+1], r_ar[:, :],
                                 start=True, stop=True)
                    self.sec("tensor", f)

                    def f(v, j=j):
                        v.tensor_copy(self.sbase[:, j*E:(j+1)*E], ps[2][0:1, 0:E])
                    self.sec("vector", f)

            def f(v):
                for j in range(cfg.SPO):
                    sV.t(v.tensor_copy(rbrow[:, (4*j)*E:(4*j+1)*E],
                                       self.sbase[:, j*E:(j+1)*E]))
                    for k in range(1, 4):
                        v.wait_ge(sV.h, sV.v)
                        sV.t(v.tensor_add(rbrow[:, (4*j+k)*E:(4*j+k+1)*E],
                                          rbrow[:, (4*j+k-1)*E:(4*j+k)*E],
                                          cntrow[:, (4*j+k-1)*E:(4*j+k)*E]))
            self.sec("vector", f, raw="free")

            rsn2 = self.snap()
            p0, v0 = self.sP.v, sV.v
            RC = lambda tt: v0 + 2 * (tt + 1)

            def f(t):
                for tt in range(cfg.NT_O):
                    pA, pB = (ps[2], ps[3]) if tt % 2 == 0 else (ps[4], ps[5])
                    if tt >= 2:
                        t.wait_ge(sV.h, RC(tt - 2))
                    sP.t(t.matmul(pA[:, 0:E], self.b_or[:, :], rbrow[0:1, tt*E:(tt+1)*E],
                                  start=True, stop=True))
                    sP.t(t.matmul(pB[:, 0:E], self.b_tri[:, :], self.r_M[:, tt, :],
                                  start=True, stop=True))
            self.sec("tensor", f, snap=rsn2, raw="free")

            def f(v):
                for tt in range(cfg.NT_O):
                    pA, pB = (ps[2], ps[3]) if tt % 2 == 0 else (ps[4], ps[5])
                    v.wait_ge(self.sP.h, p0 + 2 * (tt + 1))
                    sV.t(v.tensor_copy(self.r_C[:, tt, :], pB[:, 0:E]))
                    v.wait_ge(sV.h, sV.v)
                    sV.t(v.tensor_add(self.r_C[:, tt, :], self.r_C[:, tt, :], pA[:, 0:E]))
            self.sec("vector", f, snap=rsn2, raw="free")

        with ExitStack() as XS:
            SB = lambda n, sh, dt: XS.enter_context(self.nc.sbuf_tensor(n, list(sh), dt))
            selb = SB("selb", (128, E), F32)
            tmp1 = SB("tmp1", (128, E), F32)
            tmp2 = SB("tmp2", (128, E), F32)
            gsl = SB("gsl", (128, E), F32)
            r_i16 = SB("r_i16", (16, TO), I16)
            r_i16b = SB("r_i16b", (16, TO), I16)
            r_g16 = SB("r_g16", (16, TO), FP16)
            r_gslf = SB("r_gslf", (8, TO), F32)
            r_gtmp = SB("r_gtmp", (8, TO), F32)

            def f(v):
                v.memset(r_i16[:, :], -1)
                v.memset(r_i16b[:, :], -1)
                v.memset(r_g16[:, :], 0.0)
            self.sec("vector", f)

            for tt in range(cfg.NT_O):
                def f(v, tt=tt):
                    v.tensor_scalar(selb[:, :], self.r_C[:, tt, :], float(CAP), None, ALU.is_le)
                    v.tensor_scalar(tmp1[:, :], self.r_tk[:, tt, :], 0.0, None, ALU.is_gt)
                    v.tensor_mul(selb[:, :], selb[:, :], tmp1[:, :])
                    v.tensor_mul(gsl[:, :], self.r_C[:, tt, :], selb[:, :])
                    v.tensor_scalar(gsl[:, :], gsl[:, :], 1.0, None, ALU.subtract)
                    v.scalar_tensor_tensor(tmp1[:, :], selb[:, :], -1000.0,
                                           self.b_e1000[:, :], ALU.mult, ALU.add)
                    v.tensor_reduce(self.scr[:, 0:1], tmp1[:, :], AX.X, ALU.min)
                    v.tensor_scalar(tmp2[:, :], tmp1[:, :], self.scr[:, 0:1], None,
                                    ALU.is_equal)
                    v.tensor_mul(tmp2[:, :], tmp2[:, :], self.r_C[:, tt, :])
                    v.tensor_reduce(self.scr[:, 1:2], tmp2[:, :], AX.X, ALU.add)
                    v.tensor_scalar(self.scr[:, 2:3], self.scr[:, 0:1], float(CAP), None, ALU.mult)
                    v.tensor_add(self.scr[:, 2:3], self.scr[:, 2:3], self.scr[:, 1:2])
                    v.tensor_scalar(self.scr[:, 2:3], self.scr[:, 2:3], 1.0, None, ALU.subtract)
                    v.tensor_scalar(self.scr[:, 3:4], self.scr[:, 0:1], 999.0, None, ALU.is_le)
                    v.tensor_mul(self.g_row[:, tt, 0:1], self.scr[:, 2:3], self.scr[:, 3:4])
                    v.tensor_copy(self.g_val[:, tt, 0:1], self.scr[:, 3:4])
                    v.tensor_mul(tmp2[:, :], selb[:, :], self.b_ep1[:, :])
                    v.tensor_scalar(tmp2[:, :], tmp2[:, :], 1.0, None, ALU.subtract)
                    v.tensor_reduce(self.scr[:, 4:5], tmp2[:, :], AX.X, ALU.max)
                    v.tensor_scalar(tmp1[:, :], tmp2[:, :], self.scr[:, 4:5], None,
                                    ALU.is_equal)
                    v.tensor_mul(tmp1[:, :], tmp1[:, :], self.r_C[:, tt, :])
                    v.tensor_reduce(self.scr[:, 5:6], tmp1[:, :], AX.X, ALU.add)
                    v.tensor_scalar(self.scr[:, 6:7], self.scr[:, 4:5], float(CAP), None, ALU.mult)
                    v.tensor_add(self.scr[:, 6:7], self.scr[:, 6:7], self.scr[:, 5:6])
                    v.tensor_scalar(self.scr[:, 6:7], self.scr[:, 6:7], 1.0, None, ALU.subtract)
                    v.tensor_tensor(self.scr[:, 7:8], self.scr[:, 4:5], self.scr[:, 0:1], ALU.is_gt)
                    v.tensor_mul(self.g_row[:, tt, 1:2], self.scr[:, 6:7], self.scr[:, 7:8])
                    v.tensor_copy(self.g_val[:, tt, 1:2], self.scr[:, 7:8])
                self.sec("vector", f)

                def f(t, tt=tt):
                    t.transpose(self.psT[0:E, 0:128], gsl[:, :], self.b_ident[:, :])
                    t.transpose(self.psC1[0:E, 0:128], self.r_tk[:, tt, :], self.b_ident[:, :])
                self.sec("tensor", f)

                def f(v, tt=tt):
                    v.tensor_copy(r_gslf[0:E, tt*128:(tt+1)*128], self.psT[0:E, 0:128])
                    v.tensor_copy(r_g16[0:E, tt*128:(tt+1)*128], self.psC1[0:E, 0:128])
                self.sec("vector", f)

            halfc = CAP // 2
            with ExitStack() as YS:
                t_i = YS.enter_context(self.nc.sbuf_tensor("t_i", [16, CAP], I16))
                t_g = YS.enter_context(self.nc.sbuf_tensor("t_g", [16, CAP], FP16))

                def f(v):
                    v.tensor_scalar(r_gtmp[:, :], r_gslf[:, :], float(halfc), None, ALU.is_lt)
                    v.tensor_scalar(r_gslf[:, :].copy() if False else r_gslf[:, :],
                                    r_gslf[:, :], 0.0, None, ALU.add)
                    v.scalar_tensor_tensor(r_gtmp[:, :], r_gslf[:, :], 1.0, r_gtmp[:, :],
                                           ALU.add, ALU.mult)
                    v.tensor_scalar(r_gtmp[:, :], r_gtmp[:, :], 1.0, None, ALU.subtract)
                    v.tensor_copy(r_i16[0:E, :], r_gtmp[:, :])
                    v.tensor_scalar(r_gtmp[:, :], r_gslf[:, :], float(halfc), None, ALU.is_ge)
                    v.scalar_tensor_tensor(r_gtmp[:, :], r_gslf[:, :], float(1 - halfc),
                                           r_gtmp[:, :], ALU.add, ALU.mult)
                    v.tensor_scalar(r_gtmp[:, :], r_gtmp[:, :], 1.0, None, ALU.subtract)
                    v.tensor_copy(r_i16b[0:E, :], r_gtmp[:, :])
                self.sec("vector", f)

                def f(g):
                    g.load_library(libcfg.local_scatter)
                    self.sQ.t(g.local_scatter(t_i[:, 0:halfc], self.b_agp1[:, :], r_i16[:, :],
                                              channels=16, num_elems=halfc, num_idxs=TO))
                    self.sQ.t(g.local_scatter(t_i[:, halfc:CAP], self.b_agp1[:, :], r_i16b[:, :],
                                              channels=16, num_elems=CAP - halfc, num_idxs=TO))
                    self.sQ.t(g.local_scatter(t_g[:, 0:halfc], r_g16[:, :], r_i16[:, :],
                                              channels=16, num_elems=halfc, num_idxs=TO))
                    self.sQ.t(g.local_scatter(t_g[:, halfc:CAP], r_g16[:, :], r_i16b[:, :],
                                              channels=16, num_elems=CAP - halfc, num_idxs=TO))
                self.sec("gpsimd", f)

                def f(g):
                    self.sG.t(g.dma_start(out=self.tbi_in.ap(), in_=t_i[:, :]), 16)
                    self.sG.t(g.dma_start(out=self.tbg_in.ap(), in_=t_g[:, :]), 16)
                self.sec("gpsimd", f)

                def f(g):
                    self.sC.t(g.collective_compute(
                        "AllGather", ALU.bypass, replica_groups=[list(range(8))],
                        ins=[self.tbi_in.ap().opt()], outs=[self.tbi_out.ap().opt()]), 1)
                    self.sC.t(g.collective_compute(
                        "AllGather", ALU.bypass, replica_groups=[list(range(8))],
                        ins=[self.tbg_in.ap().opt()], outs=[self.tbg_out.ap().opt()]), 1)
                self.sec("gpsimd", f)


    def ph_attn(self):
        cfg, d = self.cfg, self.d
        T, TO = cfg.T, cfg.TO
        H, HD, RD = cfg.H, cfg.HD, cfg.RD
        KVCH, QRCH, SPO = cfg.KVCH, cfg.QRCH, cfg.SPO
        sP, sV, sS, sX = self.sP, self.sV, self.sS, self.sX
        sG, sD = self.sG, self.sD
        ps = self.ps
        qsc = 1.0 / math.sqrt(float(cfg.QD))

        with ExitStack() as AS:
            SB = lambda n, sh, dt: AS.enter_context(self.nc.sbuf_tensor(n, list(sh), dt))
            w_kd = SB("w_kd", (128, KVCH, cfg.D), BF16)
            w_vd = SB("w_vd", (128, KVCH, cfg.D), BF16)
            w_qd = SB("w_qd", (128, QRCH, cfg.D), BF16)
            v_g = SB("v_g", (128, T // 128, 4 * (HD + 1)), BF16)
            kt2 = SB("kt2", (128, 2, T), BF16)
            qt2 = SB("qt2", (128, 2, TO), BF16)
            sqk = SB("sqk", (128, T), BF16)
            sqq = SB("sqq", (128, TO), BF16)
            probs = SB("probs", (128, 2, 1536), BF16)
            att_acc = SB("att_acc", (128, SPO, 512), F32)
            brow = SB("brow", (1, T), F32)
            bqn = SB("bqn", (1, TO), F32)
            bqnb = SB("bqnb", (1, TO), BF16)
            att_bf = SB("att_bf", (64, 2, 512), BF16)
            att_f = SB("att_f", (64, 512), F32)

            def f(g):
                for nm, dst in [("W_kdec", w_kd), ("W_vdec", w_vd), ("W_qdec", w_qd)]:
                    self.sG.t(g.dma_start(out=dst[:],
                                          in_=d[nm].ap().rearrange("(c p) o -> p c o", p=128)), 16)
            self.sec("gpsimd", f)

            def f(v):
                v.memset(self.s_recip[:, :], 0.0)
                v.memset(kt2[96:97, 0, :], 1.0)
                v.memset(kt2[96:97, 1, :], 1.0)
                v.memset(sqk[96:97, :], 1.0)
                for l in range(4):
                    v.memset(v_g[:, :, l*(HD+1)+HD : l*(HD+1)+HD+1], 1.0)
            self.sec("vector", f)

            def emit_build(h, snap0):
                buf = h % 2
                p0, v0 = sP.v, sV.v
                items = [("k", sp8) for sp8 in range(T // 512)] + \
                        [("q", so) for so in range(SPO)]

                def f(t, h=h, v0=v0, items=items):
                    for idx, (kind, sp) in enumerate(items):
                        if idx >= 2:
                            t.wait_ge(sV.h, v0 + 2 * idx - 3)
                        pp = ps[idx % 2]
                        if kind == "k":
                            for kvc in range(KVCH):
                                inst = t.matmul(pp[0:HD, :], w_kd[:, kvc, h*HD:(h+1)*HD],
                                                self.kv_nT[:, kvc, sp*512:(sp+1)*512],
                                                start=(kvc == 0), stop=(kvc == KVCH - 1))
                        else:
                            for qc in range(QRCH):
                                inst = t.matmul(pp[0:HD, :], w_qd[:, qc, h*HD:(h+1)*HD],
                                                self.q_cT[:, qc, sp*512:(sp+1)*512],
                                                start=(qc == 0), stop=(qc == QRCH - 1))
                        sP.t(inst)
                self.sec("tensor", f, snap=snap0, raw="free")

                def f(v, buf=buf, p0=p0, items=items):
                    for idx, (kind, sp) in enumerate(items):
                        v.wait_ge(sP.h, p0 + idx + 1)
                        pp = ps[idx % 2]
                        if kind == "k":
                            sV.t(v.tensor_copy(kt2[0:HD, buf, sp*512:(sp+1)*512], pp[0:HD, :]))
                            sV.t(v.tensor_mul(sqk[0:HD, sp*512:(sp+1)*512],
                                              kt2[0:HD, buf, sp*512:(sp+1)*512],
                                              kt2[0:HD, buf, sp*512:(sp+1)*512]))
                        else:
                            sV.t(v.tensor_scalar(qt2[0:HD, buf, sp*512:(sp+1)*512], pp[0:HD, :],
                                                 qsc, None, ALU.mult))
                            sV.t(v.tensor_mul(sqq[0:HD, sp*512:(sp+1)*512],
                                              qt2[0:HD, buf, sp*512:(sp+1)*512],
                                              qt2[0:HD, buf, sp*512:(sp+1)*512]))
                self.sec("vector", f, snap=snap0, raw=True)

            def emit_rope_dma(h):
                buf, g4, l = h % 2, h // 4, h % 4

                def f(g, buf=buf, g4=g4, l=l):
                    self.sG.t(g.dma_start(
                        out=kt2[HD:HD+RD, buf, :],
                        in_=self.kr_dram.ap()[l*RD:(l+1)*RD, g4:g4+1, :]
                        .rearrange("p a b -> p (a b)")), 16)
                    self.sG.t(g.dma_start(
                        out=qt2[HD:HD+RD, buf, :],
                        in_=self.qro_dram.ap()[l*RD:(l+1)*RD, g4:g4+1, :]
                        .rearrange("p a b -> p (a b)")), 16)
                self.sec("gpsimd", f)

            def emit_rope_sq(h):
                buf = h % 2

                def f(v, buf=buf):
                    v.tensor_mul(sqk[HD:HD+RD, :], kt2[HD:HD+RD, buf, :],
                                 kt2[HD:HD+RD, buf, :])
                    v.tensor_mul(sqq[HD:HD+RD, :], qt2[HD:HD+RD, buf, :],
                                 qt2[HD:HD+RD, buf, :])
                self.sec("vector", f)

            def emit_vg(g4):
                gsn = self.snap()
                p0, v0 = sP.v, sV.v
                ntc = T // 128

                def f(t, g4=g4, v0=v0, ntc=ntc):
                    for tc in range(ntc):
                        if tc >= 2:
                            t.wait_ge(sV.h, v0 + tc - 1)
                        pp = ps[tc % 2]
                        for kvc in range(KVCH):
                            inst = t.matmul(pp[:, 0:4*HD], self.kv_nT[:, kvc, tc*128:(tc+1)*128],
                                            w_vd[:, kvc, g4*4*HD:(g4+1)*4*HD],
                                            start=(kvc == 0), stop=(kvc == KVCH - 1))
                        sP.t(inst)
                self.sec("tensor", f, snap=gsn, raw="free")

                def f(v, p0=p0, ntc=ntc):
                    for tc in range(ntc):
                        v.wait_ge(sP.h, p0 + tc + 1)
                        pp = ps[tc % 2]
                        dst = v_g[:, tc, :].rearrange("p (l d) -> p l d", l=4)[:, :, 0:HD]
                        src = pp[:, 0:4*HD].rearrange("p (l d) -> p l d", l=4)
                        sV.t(v.tensor_copy(dst, src))
                self.sec("vector", f, snap=gsn, raw=True)

            def emit_bound(h):
                gsn = self.snap()
                p0, v0 = sP.v, sV.v
                nsq = T // 512 + SPO

                def f(t, v0=v0, nsq=nsq):
                    for i in range(nsq):
                        if i >= 2:
                            t.wait_ge(sV.h, v0 + i - 1)
                        pp = self.psG3a[0:1, 0:512] if i % 2 == 0 else self.psG3b[0:1, 0:512]
                        if i < T // 512:
                            sP.t(t.matmul(pp, self.b_ocb[0:97, 0:1],
                                          sqk[0:97, i*512:(i+1)*512], start=True, stop=True))
                        else:
                            so = i - T // 512
                            sP.t(t.matmul(pp, self.b_ocb[0:96, 0:1],
                                          sqq[0:96, so*512:(so+1)*512], start=True, stop=True))
                self.sec("tensor", f, snap=gsn, raw="free")

                def f(v, p0=p0, nsq=nsq):
                    for i in range(nsq):
                        v.wait_ge(sP.h, p0 + i + 1)
                        pp = self.psG3a[0:1, 0:512] if i % 2 == 0 else self.psG3b[0:1, 0:512]
                        if i < T // 512:
                            if i == 0:
                                sV.t(v.tensor_copy(brow[:, 0:512], pp))
                            else:
                                sV.t(v.tensor_tensor(brow[:, 0:512], brow[:, 0:512], pp,
                                                     ALU.max))
                        else:
                            so = i - T // 512
                            sV.t(v.tensor_copy(bqn[:, so*512:(so+1)*512], pp))
                self.sec("vector", f, snap=gsn, raw=True)

            def emit_chain(h):
                buf = h % 2

                def f(v):
                    v.tensor_reduce(self.scr[0:1, 0:1], brow[:, 0:512], AX.X, ALU.max)
                    v.tensor_scalar(self.scr[0:1, 0:1], self.scr[0:1, 0:1], 1e-30, None,
                                    ALU.add)
                    v.tensor_scalar(bqn[:, :], bqn[:, :], 1e-30, None, ALU.add)
                    v.tensor_scalar(bqn[:, :], bqn[:, :], self.scr[0:1, 0:1], None,
                                    ALU.mult)
                self.sec("vector", f)

                def f(sc):
                    sc.activation(out=bqn[:, :], in_=bqn[:, :], func=AF.Sqrt)
                self.sec("scalar", f)

                def f(v):
                    v.tensor_scalar(bqn[:, :], bqn[:, :], -1.0, cfg.MARGIN,
                                    ALU.mult, ALU.add)
                    v.tensor_copy(bqnb[:, :], bqn[:, :])
                self.sec("vector", f)

                def f(g, buf=buf):
                    self.sG.t(g.dma_start(out=qt2[96:97, buf, :], in_=bqnb[0:1, :]), 16)
                self.sec("gpsimd", f)

            def emit_norm_g0(h):
                def f(g, h=h):
                    self.sG.t(g.dma_start(out=self.s_recip[h:h+1, :],
                                          in_=att_acc[HD:HD+1, :, :]
                                          .rearrange("p a b -> p (a b)")), 16)
                self.sec("gpsimd", f)
                return sG.v

            def emit_norm_g3(h):
                gsn = self.snap()
                v0, g0 = sV.v, sG.v

                def f(v, g0=g0):
                    for j in range(SPO):
                        if j >= 2:
                            v.wait_ge(sG.h, g0 + 16 * (j - 1))
                        sV.t(v.tensor_copy(att_bf[:, j % 2, :], att_acc[0:HD, j, :]))
                self.sec("vector", f, snap=gsn, raw=True)

                def f(g, h=h, v0=v0):
                    for j in range(SPO):
                        g.wait_ge(sV.h, v0 + j + 1)
                        self.sG.t(g.dma_start(out=self.aoT_sp.ap()[h*HD:(h+1)*HD,
                                                                   j*512:(j+1)*512],
                                              in_=att_bf[:, j % 2, :]), 16)
                self.sec("gpsimd", f, snap=gsn, raw="free")

            def emit_inner(h, snap0, v_guard, s4_guard):
                buf, l = h % 2, h % 4
                for j in range(SPO):
                    nk = cfg.KB[j]
                    ngr = (nk + 2) // 3
                    grp = [list(range(3*g, min(3*g+3, nk))) for g in range(ngr)]

                    def g_masked(g, grp=grp, nk=nk):
                        return [kc for kc in grp[g] if kc >= nk - 8]
                    scores_done = [0] * ngr
                    av_done = [0] * ngr
                    exp_done = [0] * ngr
                    mask_grp_done = [0] * ngr
                    pe_c = 0
                    for g in range(ngr):
                        pe_c += 1
                        scores_done[g] = pe_c
                        if g >= 1:
                            pe_c += 1
                            av_done[g - 1] = pe_c
                    pe_c += 1
                    av_done[ngr - 1] = pe_c
                    for g in range(ngr):
                        exp_done[g] = g + 1
                    mk_c = 0
                    for g in range(ngr):
                        if g_masked(g):
                            mk_c += 1
                        mask_grp_done[g] = mk_c

                    gsn = snap0 if j == 0 else self.snap()
                    p0, s0, x0 = sP.v, sS.v, sX.v
                    psets = [self.psG3a, self.psG3b]

                    def f(t, j=j, nk=nk, ngr=ngr, grp=grp, s0=s0, x0=x0, l=l, buf=buf,
                          av_done=av_done, exp_done=exp_done,
                          mask_grp_done=mask_grp_done, psets=psets, g_masked=g_masked):
                        if j == 0:
                            t.wait_ge(sV.h, v_guard)

                        def av_group(g):
                            t.wait_ge(sS.h, s0 + exp_done[g])
                            if g_masked(g):
                                t.wait_ge(sX.h, x0 + mask_grp_done[g])
                            for i, kc in enumerate(grp[g]):
                                inst = t.matmul(ps[6][0:HD+1, :],
                                                v_g[:, kc, l*(HD+1):(l+1)*(HD+1)],
                                                probs[:, g % 2, i*512:(i+1)*512],
                                                start=(kc == 0), stop=(kc == nk - 1))
                                if i == len(grp[g]) - 1:
                                    sP.t(inst)
                        for g in range(ngr):
                            if g >= 2:
                                t.wait_ge(sS.h, s0 + exp_done[g - 2])
                            pset = psets[g % 2]
                            for i, kc in enumerate(grp[g]):
                                inst = t.matmul(pset[:, i*512:(i+1)*512],
                                                kt2[0:97, buf, kc*128:(kc+1)*128],
                                                qt2[0:97, buf, j*512:(j+1)*512],
                                                start=True, stop=True)
                                if i == len(grp[g]) - 1:
                                    sP.t(inst)
                            if g >= 1:
                                av_group(g - 1)
                        av_group(ngr - 1)
                    self.sec("tensor", f, snap=gsn, raw="free")

                    def f(sc, ngr=ngr, grp=grp, p0=p0,
                          scores_done=scores_done, av_done=av_done, psets=psets):
                        for g in range(ngr):
                            sc.wait_ge(sP.h, p0 + scores_done[g])
                            if g >= 2:
                                sc.wait_ge(sP.h, p0 + av_done[g - 2])
                            w = len(grp[g])
                            self.sS.t(sc.activation(out=probs[:, g % 2, 0:w*512],
                                                    in_=psets[g % 2][:, 0:w*512], func=AF.Exp))
                    self.sec("scalar", f, snap=gsn, raw="free")

                    def f(v, nk=nk, ngr=ngr, grp=grp, s0=s0,
                          exp_done=exp_done, g_masked=g_masked):
                        for g in range(ngr):
                            mk = g_masked(g)
                            if not mk:
                                continue
                            v.wait_ge(sS.h, s0 + exp_done[g])
                            for kc in mk:
                                i = kc - grp[g][0]
                                mi = kc - (nk - 8)
                                inst = v.tensor_mul(probs[:, g % 2, i*512:(i+1)*512],
                                                    probs[:, g % 2, i*512:(i+1)*512],
                                                    self.b_mask[:, mi, :])
                                if kc == mk[-1]:
                                    self.sX.t(inst, 1)
                    self.sec("vector", f, snap=gsn, raw="free")

                    def f(v, j=j):
                        if j == 0 and s4_guard > 0:
                            v.wait_ge(sG.h, s4_guard)
                        v.tensor_copy(att_acc[0:HD+1, j, :], ps[6][0:HD+1, :])
                    self.sec("vector", f)

            emit_rope_dma(0)
            emit_build(0, self.snap())
            emit_rope_sq(0)
            emit_vg(0)
            emit_bound(0)
            emit_chain(0)
            s4_guard = 0
            for h in range(H):
                snap0 = self.snap()
                if h + 1 < H:
                    emit_rope_dma(h + 1)
                if h >= 1:
                    s4_guard = emit_norm_g0(h - 1)
                if h + 1 < H:
                    emit_build(h + 1, snap0)
                    emit_rope_sq(h + 1)
                    emit_bound(h + 1)
                if h >= 1:
                    emit_norm_g3(h - 1)
                v_guard = sV.v
                emit_inner(h, snap0, v_guard, s4_guard)
                if h + 1 < H:
                    emit_chain(h + 1)
                if (h + 1) % 4 == 0 and h + 1 < H:
                    emit_vg((h + 1) // 4)
            s4_guard = emit_norm_g0(H - 1)
            emit_norm_g3(H - 1)

            def f(v):
                v.reciprocal(self.s_recip[:, :], self.s_recip[:, :])
            self.sec("vector", f)

    def ph_wo(self):
        cfg, d = self.cfg, self.d
        TO, D, DCH, SPO = cfg.TO, cfg.D, cfg.DCH, cfg.SPO
        ps = self.ps
        sP, sV = self.sP, self.sV

        with ExitStack() as WS:
            SB = lambda n, sh, dt: WS.enter_context(self.nc.sbuf_tensor(n, list(sh), dt))
            w_o = SB("w_o", (128, DCH, D), BF16)
            ao = SB("ao", (128, DCH, 512), BF16)
            xx = SB("xx", (128, 4, D), F32)
            yt = SB("yt", (128, D), F32)

            def f(g):
                self.sG.t(g.dma_start(out=w_o[:],
                                      in_=d["W_o"].ap().rearrange("(c p) o -> p c o", p=128)), 16)
            self.sec("gpsimd", f)

            for j in range(SPO):
                def f(s, j=j):
                    for fc in range(DCH):
                        self.sD.t(s.dma_start(out=ao[:, fc, :],
                                              in_=self.aoT_sp.ap()[fc*128:(fc+1)*128,
                                                                   j*512:(j+1)*512]), 16)
                self.sec("sync", f)

                def f(sc, j=j):
                    self.sD.t(sc.dma_start(out=xx[:, :, :],
                                          in_=d["x_own"].ap()[j*512:(j+1)*512, :]
                                          .rearrange("(a p) b -> p a b", p=128)), 16)
                self.sec("scalar", f, raw=True)

                gsn = self.snap()
                p0n, v0n = sP.v, sV.v

                def f(t, j=j, v0n=v0n):
                    for fc in range(DCH):
                        if fc >= 2:
                            t.wait_ge(sV.h, v0n + fc - 1)
                        pp = ps[3] if fc % 2 == 0 else ps[4]
                        sP.t(t.matmul(pp[:, :], self.b_hsel[:, fc*128:(fc+1)*128],
                                      self.s_recip[:, j*512:(j+1)*512],
                                      start=True, stop=True))
                self.sec("tensor", f, snap=gsn, raw="free")

                def f(v, j=j, p0n=p0n):
                    for fc in range(DCH):
                        v.wait_ge(sP.h, p0n + fc + 1)
                        pp = ps[3] if fc % 2 == 0 else ps[4]
                        sV.t(v.tensor_mul(ao[:, fc, :], ao[:, fc, :], pp[:, :]))
                self.sec("vector", f, snap=gsn, raw=True)

                for tl in range(4):
                    def f(t, j=j, tl=tl):
                        for Fh in range(2):
                            pp = ps[Fh]
                            for fc in range(DCH):
                                t.matmul(pp[:, :], ao[:, fc, tl*128:(tl+1)*128],
                                         w_o[:, fc, Fh*512:(Fh+1)*512],
                                         start=(fc == 0), stop=(fc == DCH - 1))
                    self.sec("tensor", f)

                    def f(v, j=j, tl=tl):
                        for Fh in range(2):
                            v.tensor_add(yt[:, Fh*512:(Fh+1)*512], ps[Fh][:, :],
                                         xx[:, tl, Fh*512:(Fh+1)*512])
                    self.sec("vector", f)

                    def f(s, j=j, tl=tl):
                        r0 = (j * 4 + tl) * 128
                        self.sD.t(s.dma_start(out=self.y_mid.ap()[r0:r0+128, :], in_=yt[:, :]), 16)
                    self.sec("sync", f)

    def ph_moe(self):
        cfg, d = self.cfg, self.d
        D, I, CAP, E = cfg.D, cfg.I, cfg.CAP, cfg.E
        DCH, ICH, ISL, CAPT = cfg.DCH, cfg.ICH, cfg.ISL, cfg.CAPT
        ps = self.ps
        sP, sV, sS = self.sP, self.sV, self.sS

        with ExitStack() as MS:
            SB = lambda n, sh, dt: MS.enter_context(self.nc.sbuf_tensor(n, list(sh), dt))
            cmb = SB("cmb", (8, CAP), F32)
            cmb16 = SB("cmb16", (8, CAP), I16)
            cmg16 = SB("cmg16", (8, CAP), FP16)
            rowf = SB("rowf", (1, CAP), F32)
            rowg = SB("rowg", (1, CAP), F32)
            m_xg = SB("m_xg", (128, 2, 4, D), BF16)
            m_xgf = SB("m_xgf", (128, 2, DCH, 512), BF16)
            m_wi = SB("m_wi", (128, 4, DCH, 512), BF16)
            m_wo = SB("m_wo", (128, 4, 4, 512), BF16)
            m_h = SB("m_h", (128, ICH, 512), BF16)
            m_out = SB("m_out", (128, 2, 512), F32)

            _ns_mc = self.nc.named_scope("mComb"); _ns_mc.__enter__()
            def f(s):
                cid = self._vload(s, self.b_cmeta['sync'][0:1, 2:3], 0, 7)
                for r in range(8):
                    rr = s.snap(cid + 16 * r, min_val=0, max_val=127)
                    self.sD.t(s.dma_start(out=cmb16[r:r+1, :],
                                          in_=self.tbi_out.ap()[ts(rr, 1), :]), 16)
                    self.sD.t(s.dma_start(out=cmg16[r:r+1, :],
                                          in_=self.tbg_out.ap()[ts(rr, 1), :]), 16)
            self.sec("sync", f)

            def f(v):
                v.tensor_copy(cmb[:, :], cmb16[:, :])
            self.sec("vector", f)

            spans5 = [(o, min(512, CAP - o)) for o in range(0, CAP, 512)]
            for o5, w5 in spans5:
                def f(t, o5=o5, w5=w5):
                    t.matmul(ps[0][0:1, 0:w5], self.b_oc[0:8, 0:1], cmb[:, o5:o5+w5],
                             start=True, stop=True)
                self.sec("tensor", f)

                def f(v, o5=o5, w5=w5):
                    v.tensor_copy(rowf[:, o5:o5+w5], ps[0][0:1, 0:w5])
                self.sec("vector", f)

            def f(v):
                v.tensor_copy(cmb[:, :], cmg16[:, :])
            self.sec("vector", f)

            for o5, w5 in spans5:
                def f(t, o5=o5, w5=w5):
                    t.matmul(ps[0][0:1, 0:w5], self.b_oc[0:8, 0:1], cmb[:, o5:o5+w5],
                             start=True, stop=True)
                self.sec("tensor", f)

                def f(v, o5=o5, w5=w5):
                    v.tensor_copy(rowg[:, o5:o5+w5], ps[0][0:1, 0:w5])
                self.sec("vector", f)

            def f(v):
                v.tensor_scalar(rowf[:, :], rowf[:, :], 1.0, None, ALU.subtract)
                v.tensor_scalar(rowf[:, :], rowf[:, :], 0.0, None, ALU.max)
            self.sec("vector", f)

            gsn = self.snap()
            p0, v0 = self.sP.v, self.sV.v

            def f(t, p0=p0, v0=v0):
                for st in range(CAPT):
                    if st >= 2:
                        t.wait_ge(self.sV.h, v0 + 2 * (st - 1))
                    pp = self.psT if st % 2 == 0 else self.psC1
                    self.sP.t(t.transpose(pp[0:128, 0:1], rowf[:, st*128:(st+1)*128],
                                          self.b_ident[0:1, 0:1]))
                    self.sP.t(t.transpose(pp[0:128, 1:2], rowg[:, st*128:(st+1)*128],
                                          self.b_ident[0:1, 0:1]))
            self.sec("tensor", f, snap=gsn, raw=True)

            def f(v, p0=p0):
                for st in range(CAPT):
                    v.wait_ge(self.sP.h, p0 + 2 * (st + 1))
                    pp = self.psT if st % 2 == 0 else self.psC1
                    self.sV.t(v.tensor_copy(self.e_idx[:, st:st+1], pp[0:128, 0:1]))
                    self.sV.t(v.tensor_copy(self.e_gate[:, st:st+1], pp[0:128, 1:2]))
            self.sec("vector", f, snap=gsn, raw=True)

            _ns_mc.__exit__(None, None, None)
            NCh = len(cfg.SCH)
            assert all(c == 512 for c in cfg.SCH)
            sV, sG, sD = self.sV, self.sG, self.sD

            def f(g):
                for st in range(4):
                    self.sG.t(g.indirect_dma_start(
                        m_xg[:, 0, st, :], None, self.ag1_out.ap(),
                        IndirectOffsetOnAxis(self.e_idx[:, st:st+1], 0)), 16)
            self.sec("gpsimd", f)

            gsn = self.snap()
            p0, s0, v0, d0, g0 = sP.v, sS.v, sV.v, sD.v, sG.v
            P, S, V, G, D = {}, {}, {}, {}, {}
            pl = p0
            for ci in range(NCh):
                for i in range(32):
                    pl += 1; P[("tr", ci, i)] = pl
                for k in range(32):
                    pl += 1; P[("m1", ci, k)] = pl
                for Fh in range(2):
                    for isl in range(8):
                        pl += 1; P[("m2", ci, Fh, isl)] = pl
            sl = s0
            for ci in range(NCh):
                for k in range(32):
                    sl += 1; S[("relu", ci, k)] = sl
            vl = v0
            for ci in range(NCh):
                for i in range(32):
                    vl += 1; V[("te", ci, i)] = vl
                for Fh in range(2):
                    for st in range(4):
                        vl += 1; V[("gate", ci, Fh, st)] = vl
            gl = g0
            for ci in range(NCh):
                if ci + 1 < NCh:
                    gl += 64; G[("gath", ci + 1)] = gl
                for Fh in range(2):
                    for st in range(4):
                        gl += 16; G[("out", (ci * 2 + Fh) * 4 + st)] = gl
            dl = d0
            for ci in range(NCh):
                for isl in range(8):
                    dl += 16; D[("wi", ci, isl)] = dl
                for Fh in range(2):
                    for isl in range(8):
                        dl += 16; D[("wo", ci, Fh, isl)] = dl

            def f(t):
                for ci in range(NCh):
                    for i in range(32):
                        st, fc = divmod(i, DCH)
                        if i == 0 and ci >= 1:
                            t.wait_ge(sG.h, G[("gath", ci)])
                        if i >= 2:
                            t.wait_ge(sV.h, V[("te", ci, i - 2)])
                        dst = (self.psT[:, :] if i % 2 == 0
                               else ps[2]).bitcast(BF16)[0:128, 0:128]
                        sP.t(t.transpose(dst, m_xg[:, ci % 2, st, fc*128:(fc+1)*128],
                                         self.b_identb[:, :]))
                    for isl in range(8):
                        for ic2 in range(4):
                            k = isl * 4 + ic2
                            if k == 0:
                                t.wait_ge(sV.h, V[("te", ci, 31)])
                            if ic2 == 0:
                                t.wait_ge(sD.h, D[("wi", ci, isl)])
                            if k >= 2:
                                t.wait_ge(sS.h, S[("relu", ci, k - 2)])
                            elif ci > 0:
                                t.wait_ge(sS.h, S[("relu", ci - 1, 30 + k)])
                            pp = ps[k % 2]
                            for fc in range(DCH):
                                inst = t.matmul(pp[:, :],
                                                m_wi[:, (ci*8+isl) % 4, fc, ic2*128:(ic2+1)*128],
                                                m_xgf[:, ci % 2, fc, :],
                                                start=(fc == 0), stop=(fc == DCH - 1))
                            sP.t(inst)
                    for Fh in range(2):
                        if Fh == 1:
                            t.wait_ge(sV.h, V[("gate", ci, 0, 3)])
                        elif ci > 0:
                            t.wait_ge(sV.h, V[("gate", ci - 1, 1, 3)])
                        for isl in range(8):
                            t.wait_ge(sD.h, D[("wo", ci, Fh, isl)])
                            t.wait_ge(sS.h, S[("relu", ci, isl * 4 + 3)])
                            buf = (ci * 16 + Fh * 8 + isl) % 4
                            for st in range(4):
                                for ic2 in range(4):
                                    inst = t.matmul(ps[3 + st][:, :],
                                                    m_h[:, isl*4+ic2, st*128:(st+1)*128],
                                                    m_wo[:, buf, ic2, :],
                                                    start=(isl == 0 and ic2 == 0),
                                                    stop=(isl == 7 and ic2 == 3))
                            sP.t(inst)
            self.sec("tensor", f, snap=gsn, raw="free")

            def f(sc):
                for ci in range(NCh):
                    for k in range(32):
                        sc.wait_ge(sP.h, P[("m1", ci, k)])
                        self.sS.t(sc.activation(out=m_h[:, k, :], in_=ps[k % 2][:, :],
                                                func=AF.Relu))
            self.sec("scalar", f, snap=gsn, raw="free")

            def f(v):
                for ci in range(NCh):
                    for i in range(32):
                        st, fc = divmod(i, DCH)
                        v.wait_ge(sP.h, P[("tr", ci, i)])
                        src = (self.psT[:, :] if i % 2 == 0
                               else ps[2]).bitcast(BF16)[0:128, 0:128]
                        self.sV.t(v.tensor_copy(m_xgf[:, ci % 2, fc, st*128:(st+1)*128], src))
                    for Fh in range(2):
                        v.wait_ge(sP.h, P[("m2", ci, Fh, 7)])
                        for st in range(4):
                            q = (ci * 2 + Fh) * 4 + st
                            if q >= 2:
                                v.wait_ge(sG.h, G[("out", q - 2)])
                            self.sV.t(v.tensor_scalar(m_out[:, q % 2, :], ps[3 + st][:, :],
                                                      self.e_gate[:, ci*4+st:ci*4+st+1],
                                                      None, ALU.mult))
            self.sec("vector", f, snap=gsn, raw="free")

            def f(g):
                for ci in range(NCh):
                    if ci + 1 < NCh:
                        if ci >= 1:
                            g.wait_ge(sP.h, P[("tr", ci - 1, 31)])
                        for st in range(4):
                            c = (ci + 1) * 4 + st
                            self.sG.t(g.indirect_dma_start(
                                m_xg[:, (ci + 1) % 2, st, :], None, self.ag1_out.ap(),
                                IndirectOffsetOnAxis(self.e_idx[:, c:c+1], 0)), 16)
                    for Fh in range(2):
                        for st in range(4):
                            q = (ci * 2 + Fh) * 4 + st
                            g.wait_ge(sV.h, V[("gate", ci, Fh, st)])
                            r0 = (ci * 4 + st) * 128
                            self.sG.t(g.dma_start(
                                out=self.ag2_in.ap()[r0:r0+128, Fh*512:(Fh+1)*512],
                                in_=m_out[:, q % 2, :]), 16)
            self.sec("gpsimd", f, snap=gsn, raw="free")

            def f(s):
                for ci in range(NCh):
                    for isl in range(8):
                        tw = ci * 8 + isl
                        if tw >= 4:
                            c2, i2 = divmod(tw - 4, 8)
                            s.wait_ge(sP.h, P[("m1", c2, i2 * 4 + 3)])
                        self.sD.t(s.dma_start(out=m_wi[:, tw % 4, :, :],
                                              in_=self.wi_t.ap()[isl]), 16)
                    for Fh in range(2):
                        for isl in range(8):
                            to = ci * 16 + Fh * 8 + isl
                            if to >= 4:
                                c2, r2 = divmod(to - 4, 16)
                                Fh2, isl2 = divmod(r2, 8)
                                s.wait_ge(sP.h, P[("m2", c2, Fh2, isl2)])
                            self.sD.t(s.dma_start(out=m_wo[:, to % 4, :, :],
                                                  in_=self.wo_t.ap()[Fh * 8 + isl]), 16)
            self.sec("sync", f, snap=gsn, raw="free")

            def f(g):
                self.sC.t(g.collective_compute(
                    "AllGather", ALU.bypass, replica_groups=[list(range(8))],
                    ins=[self.ag2_in.ap().opt()], outs=[self.ag2_out.ap().opt()]), 1)
            self.sec("gpsimd", f)

    def ph_final(self):
        cfg = self.cfg
        TO, D = cfg.TO, cfg.D

        with ExitStack() as FS:
            SB = lambda n, sh, dt: FS.enter_context(self.nc.sbuf_tensor(n, list(sh), dt))
            yt = SB("ytf", (128, 2, D), F32)
            g2 = SB("g2", (128, 2, 2, D), F32)
            idxc = SB("idxc", (128, cfg.NT_O, 2), I32)

            def f(v):
                v.tensor_copy(idxc[:, :, :], self.g_row[:, :, :])
            self.sec("vector", f)

            for tt in range(cfg.NT_O):
                gsn = self.snap()
                b = tt % 2

                def f(s, tt=tt, b=b):
                    self.sD.t(s.dma_start(out=yt[:, b, :],
                                          in_=self.y_mid.ap()[tt*128:(tt+1)*128, :]), 16)
                self.sec("sync", f, snap=gsn)

                def f(g, tt=tt, b=b):
                    for k2 in range(2):
                        self.sG.t(g.indirect_dma_start(
                            g2[:, b, k2, :], None, self.ag2_out.ap(),
                            IndirectOffsetOnAxis(idxc[:, tt, k2:k2+1], 0)), 16)
                self.sec("gpsimd", f, snap=gsn)

                def f(v, tt=tt, b=b):
                    v.scalar_tensor_tensor(yt[:, b, :], g2[:, b, 0, :], self.g_val[:, tt, 0:1],
                                           yt[:, b, :], ALU.mult, ALU.add)
                    v.scalar_tensor_tensor(yt[:, b, :], g2[:, b, 1, :], self.g_val[:, tt, 1:2],
                                           yt[:, b, :], ALU.mult, ALU.add)
                self.sec("vector", f)

                def f(s, tt=tt, b=b):
                    self.sD.t(s.dma_start(out=self.y_out.ap()[tt*128:(tt+1)*128, :],
                                          in_=yt[:, b, :]), 16)
                self.sec("sync", f)

            def f(s):
                s.wait_ge(self.sG.h, self.sG.v)
                s.wait_ge(self.sD.h, self.sD.v)
            self.blk.sync(f)


_CFG = CFG(T=4096)
_NC_CACHE = {}


def _build(cfg):
    key = cfg.T
    if key not in _NC_CACHE:
        _NC_CACHE[key] = K(cfg).build()
    return _NC_CACHE[key]


def kernel(L0_attn_x, L0_attn_W_kv, L0_attn_g_kv, L0_attn_W_kdec, L0_attn_W_vdec,
           L0_attn_W_q, L0_attn_W_qdec, L0_attn_W_krope, L0_attn_W_qrope, L0_attn_W_o,
           L0_ffn_W_router, L0_ffn_b_router, L0_ffn_experts_inter, L0_ffn_experts_out,
           cfg=None):
    cfg = cfg or _CFG
    nc = _build(cfg)
    x = np.asarray(L0_attn_x, dtype=np.float32)
    B, T, D = x.shape
    assert (B, T, D) == (cfg.B, cfg.T, cfg.D)
    WrT = np.ascontiguousarray(np.asarray(L0_ffn_W_router, np.float32).T)

    in_maps = []
    for core in range(8):
        b, qh = core // 2, core % 2
        m = dict(host_consts(cfg, core))
        m["x_b"] = np.ascontiguousarray(x[b])
        xo = np.zeros((cfg.TO, cfg.D), np.float32)
        for j in range(cfg.SPO):
            g = 2 * j + qh
            xo[j*512:(j+1)*512] = x[b, g*512:(g+1)*512]
        m["x_own"] = xo
        m["W_kv"] = np.asarray(L0_attn_W_kv, np.float32)
        m["g_kv"] = np.asarray(L0_attn_g_kv, np.float32)
        m["W_kdec"] = np.asarray(L0_attn_W_kdec, np.float32)
        m["W_vdec"] = np.asarray(L0_attn_W_vdec, np.float32)
        m["W_q"] = np.asarray(L0_attn_W_q, np.float32)
        m["W_qdec"] = np.asarray(L0_attn_W_qdec, np.float32)
        m["W_krope"] = np.asarray(L0_attn_W_krope, np.float32)
        m["W_qrope"] = np.asarray(L0_attn_W_qrope, np.float32)
        m["W_o"] = np.asarray(L0_attn_W_o, np.float32)
        m["Wr_T"] = WrT
        m["b_r"] = np.asarray(L0_ffn_b_router, np.float32)
        m["Wi"] = np.ascontiguousarray(np.asarray(L0_ffn_experts_inter, np.float32)[core])
        m["Wo"] = np.ascontiguousarray(np.asarray(L0_ffn_experts_out, np.float32)[core])
        in_maps.append(m)

    res = run_bass_kernel_spmd(nc, in_maps, core_ids=list(range(8)))

    out = np.zeros((B, T, D), np.float32)
    for core in range(8):
        b, qh = core // 2, core % 2
        yc = res.results[core]["y"]
        for j in range(cfg.SPO):
            g = 2 * j + qh
            out[b, g*512:(g+1)*512, :] = yc[j*512:(j+1)*512, :]
    return out


# version 19
# speedup vs baseline: 1.0217x; 1.0212x over previous
import math
from contextlib import ExitStack

import numpy as np
import ml_dtypes

import concourse.bass as bass
import concourse.mybir as mybir
import concourse.library_config as libcfg
from concourse.bass_utils import run_bass_kernel_spmd
from concourse.bass_core_rust import IndirectOffsetOnAxis

F32 = mybir.dt.float32
BF16 = mybir.dt.bfloat16
FP16 = mybir.dt.float16
I16 = mybir.dt.int16
I32 = mybir.dt.int32
AF = mybir.ActivationFunctionType
ALU = mybir.AluOpType
AX = mybir.AxisListType
BF16NP = ml_dtypes.bfloat16
ts = bass.ts


class CFG:
    def __init__(self, T=4096):
        self.B = 4
        self.T = T
        self.TO = T // 2
        self.D = 1024
        self.H = 16
        self.HD = 64
        self.RD = 32
        self.KVR = 256
        self.QR = 512
        self.E = 8
        self.I = 4096
        self.TT = self.B * T
        self.CAP = math.ceil(self.TT / self.E * 1.25)
        assert self.CAP % 128 == 0
        self.DCH = self.D // 128
        self.KVCH = self.KVR // 128
        self.QRCH = self.QR // 128
        self.SPB = T // 512
        self.SPO = self.SPB // 2
        self.NT_B = T // 128
        self.NT_O = self.TO // 128
        self.CAPT = self.CAP // 128
        self.ICH = self.I // 128
        self.ISL = self.I // 512
        self.SCH = []
        r = self.CAP
        while r > 0:
            c = min(512, r)
            self.SCH.append(c)
            r -= c
        self.QD = self.HD + self.RD
        self.MARGIN = 30.0
        self.KB = [8 * j + 8 for j in range(self.SPO)]


def _rope_tables(T, dim):
    inv_freq = 1.0 / 10000.0 ** (np.arange(0, dim, 2, dtype=np.float32) / dim)
    t = np.arange(T, dtype=np.float32)
    freqs = np.outer(t, inv_freq)
    emb = np.concatenate([freqs, freqs], -1)
    return np.cos(emb), np.sin(emb)


def host_consts(cfg, core):
    b, qh = core // 2, core % 2
    c = {}
    c["ident"] = np.eye(128, dtype=np.float32)
    c["identb"] = np.eye(128, dtype=np.float32).astype(BF16NP)
    tri = (np.arange(128)[:, None] <= np.arange(128)[None, :]).astype(np.float32)
    c["tri"] = tri.astype(BF16NP)
    mt = np.zeros((8, 128, 512), np.float32)
    for mi in range(8):
        if qh == 0:
            if mi < 4:
                mt[mi] = (np.arange(128)[:, None] <= (np.arange(512)[None, :] - 128 * mi))
        else:
            if mi < 4:
                mt[mi] = 1.0
            else:
                mt[mi] = (np.arange(128)[:, None] <= (np.arange(512)[None, :] - 128 * (mi - 4)))
    c["masktbl"] = np.ascontiguousarray(mt.transpose(1, 0, 2)).astype(BF16NP)
    cos, sin = _rope_tables(cfg.T, cfg.RD)
    c["cos4"] = np.ascontiguousarray(np.tile(cos.T, (4, 1))).astype(BF16NP)
    c["sin4"] = np.ascontiguousarray(np.tile(sin.T, (4, 1))).astype(BF16NP)
    h2 = cfg.RD // 2
    R = np.zeros((cfg.RD, cfg.RD), np.float32)
    for dd in range(h2):
        R[dd, dd + h2] = -1.0
        R[dd + h2, dd] = 1.0
    R4T = np.zeros((128, 128), np.float32)
    for g in range(4):
        R4T[g * 32:(g + 1) * 32, g * 32:(g + 1) * 32] = R.T
    c["r4t"] = R4T.astype(BF16NP)
    c["ones_colb"] = np.ones((128, 1), np.float32).astype(BF16NP)
    c["ones_rowb"] = np.ones((1, 128), np.float32).astype(BF16NP)
    c["ones_col"] = np.ones((128, 1), np.float32)
    c["ones_row"] = np.ones((1, 128), np.float32)
    e_i = np.tile(np.arange(cfg.E, dtype=np.float32), (128, 1))
    c["eidx1000"] = (e_i + 1000.0).astype(np.float32)
    c["eidxp1"] = (e_i + 1.0).astype(np.float32)
    pm = np.zeros((4 * cfg.SPB, cfg.SPO), np.float32)
    for j in range(cfg.SPO):
        g = b * cfg.SPB + 2 * j + qh
        pm[:g, j] = 1.0
    c["presum"] = pm
    pl = np.zeros((cfg.SPO, 4 * cfg.SPB), np.float32)
    for j in range(cfg.SPO):
        pl[j, b * cfg.SPB + 2 * j + qh] = 1.0
    c["placemat"] = pl
    agp1 = (cfg.TO * core + 1 + np.arange(cfg.TO, dtype=np.int32)).astype(np.int16)
    c["agp1"] = np.tile(agp1, (16, 1))
    hs = np.zeros((16, cfg.H * cfg.HD), np.float32)
    for h in range(cfg.H):
        hs[h, h * cfg.HD:(h + 1) * cfg.HD] = 1.0
    c["headsel"] = hs
    c["cmeta"] = np.array([[qh, cfg.SPB * b + qh, core, 0, 0, 0, 0, 0]], np.int32)
    nt_b, nt_o = cfg.NT_B, cfg.NT_O
    oid = np.zeros((1, nt_b), np.int32)
    for bt in range(nt_b):
        sp, tl = bt // 4, bt % 4
        oid[0, bt] = ((sp - qh) // 2) * 4 + tl if sp % 2 == qh else nt_o
    c["oid_tbl"] = oid
    osp = np.zeros((1, cfg.SPB), np.int32)
    for sp in range(cfg.SPB):
        osp[0, sp] = (sp - qh) // 2 if sp % 2 == qh else cfg.SPO
    c["osp_tbl"] = osp
    return c


class Sem:
    def __init__(self, stack, nc, name):
        self.h = stack.enter_context(nc.semaphore(name))
        self.v = 0

    def t(self, inst, n=1):
        inst.then_inc(self.h, n)
        self.v += n
        return self.v


class K:
    def __init__(self, cfg):
        self.cfg = cfg
        self.nc = bass.Bass("TRN2", target_bir_lowering=False)
        self.st = ExitStack()
        self.d = {}
        self.last_wait = {}
        self._freeq = []

    def DI(self, name, shape, dt=F32):
        t = self.nc.dram_tensor(name, list(shape), dt, kind="ExternalInput")
        self.d[name] = t
        return t

    def snap(self):
        return {id(s): s.v for s in self.all_sems}

    TSP_OPS = frozenset({
        "tensor_scalar", "tensor_single_scalar", "tensor_scalar_add",
        "tensor_scalar_sub", "tensor_scalar_mul", "tensor_scalar_max",
        "tensor_scalar_min", "scalar_tensor_tensor", "activation",
    })
    NOEMIT = frozenset({
        "wait_ge", "sem_inc", "value_load", "snap", "free_register",
        "alloc_register", "load_library", "maybe_wait_ge",
    })

    MANUAL_SEM = frozenset({
        "dma_start", "indirect_dma_start", "dma_start_transpose",
        "collective_compute", "local_scatter", "dma_reset", "trigger_dma",
    })

    class _RawProxy:

        def __init__(self, outer, eng, own):
            self._o, self._e, self._own = outer, eng, own
            self._lastv = own.v

        def __getattr__(self, name):
            attr = getattr(self._e, name)
            if not callable(attr) or name in K.NOEMIT:
                return attr

            def call(*a, **kw):
                if self._own.v != self._lastv:
                    self._e.wait_ge(self._own.h, self._own.v)
                    self._lastv = self._own.v
                r = attr(*a, **kw)
                return r
            return call

    class _FreeProxy:

        def __init__(self, outer, eng, own):
            self._o, self._e, self._own = outer, eng, own

        def __getattr__(self, name):
            return getattr(self._e, name)

    class _Proxy:

        def __init__(self, outer, eng, own):
            self._o, self._e, self._own = outer, eng, own
            self._after_tsp = False

        def __getattr__(self, name):
            attr = getattr(self._e, name)
            if not callable(attr):
                return attr
            if name in K.NOEMIT or name in K.MANUAL_SEM:
                return attr

            def call(*a, **kw):
                self._e.wait_ge(self._own.h, self._own.v)
                r = attr(*a, **kw)
                if hasattr(r, "then_inc"):
                    r.then_inc(self._own.h, 1)
                    self._own.v += 1
                return r
            return call

    def sec(self, eng_name, fn, snap=None, raw=False):
        own = self.eng_sem[eng_name]
        lw = self.last_wait.setdefault(eng_name, {})
        own_target = own.v

        def wrapped(eng):
            if own_target > 0:
                eng.wait_ge(own.h, own_target)
            for s in self.all_sems:
                if s is own:
                    continue
                tgt = s.v if snap is None else snap[id(s)]
                if tgt > 0 and lw.get(id(s), -1) < tgt:
                    eng.wait_ge(s.h, tgt)
                    lw[id(s)] = tgt
            if raw == "free" or (raw and eng_name == "tensor"):
                px = K._FreeProxy(self, eng, own)
            elif raw:
                px = K._RawProxy(self, eng, own)
            else:
                px = K._Proxy(self, eng, own)
            fn(px)
            for h in self._freeq:
                try:
                    eng.free_register(h)
                except Exception:
                    pass
            self._freeq = []
            eng.wait_ge(own.h, own.v)
            eng.sem_inc(own.h, 1)
        getattr(self.blk, eng_name)(wrapped)
        own.v += 1
        for k2 in self.last_wait:
            if k2 != eng_name:
                self.last_wait[k2].pop(id(own), None)

    def build(self):
        cfg, nc = self.cfg, self.nc
        T, TO, D, E, I, CAP = cfg.T, cfg.TO, cfg.D, cfg.E, cfg.I, cfg.CAP

        self.DI("x_b", (T, D))
        self.DI("x_own", (TO, D))
        self.DI("W_kv", (D, cfg.KVR)); self.DI("g_kv", (cfg.KVR,))
        self.DI("W_kdec", (cfg.KVR, D)); self.DI("W_vdec", (cfg.KVR, D))
        self.DI("W_q", (D, cfg.QR)); self.DI("W_qdec", (cfg.QR, D))
        self.DI("W_krope", (D, cfg.H * cfg.RD)); self.DI("W_qrope", (cfg.QR, cfg.H * cfg.RD))
        self.DI("W_o", (D, D))
        self.DI("Wr_T", (D, E)); self.DI("b_r", (E,))
        self.DI("Wi", (D, I)); self.DI("Wo", (I, D))
        for nm, sh, dt in [
            ("ident", (128, 128), F32), ("identb", (128, 128), BF16),
            ("tri", (128, 128), BF16),
            ("masktbl", (128, 8, 512), BF16),
            ("cos4", (128, T), BF16), ("sin4", (128, T), BF16),
            ("r4t", (128, 128), BF16), ("ones_colb", (128, 1), BF16),
            ("ones_rowb", (1, 128), BF16), ("ones_col", (128, 1), F32),
            ("ones_row", (1, 128), F32),
            ("eidx1000", (128, E), F32), ("eidxp1", (128, E), F32),
            ("presum", (4 * cfg.SPB, cfg.SPO), F32),
            ("placemat", (cfg.SPO, 4 * cfg.SPB), F32), ("agp1", (16, TO), I16),
            ("headsel", (16, cfg.H * cfg.HD), F32), ("cmeta", (1, 8), I32),
            ("oid_tbl", (1, cfg.NT_B), I32), ("osp_tbl", (1, cfg.SPB), I32),
        ]:
            self.DI(nm, sh, dt)
        self.y_out = nc.dram_tensor("y", [TO, D], F32, kind="ExternalOutput")

        self.xn_spill = nc.dram_tensor("xn_spill", [TO + 128, D], BF16)
        self.xn_bounce = nc.dram_tensor("xn_bounce", [T, D], BF16)
        self.q_dram = nc.dram_tensor("q_dram", [cfg.SPB, 128, cfg.QRCH * 512], BF16)
        self.qr_dram = nc.dram_tensor("qr_dram", [cfg.SPB, 128, 4 * 512], BF16)
        self.kr_dram = nc.dram_tensor("kr_dram", [128, 4, T], BF16)
        self.qro_dram = nc.dram_tensor("qro_dram", [128, 4, TO], BF16)
        self.lg_dram = nc.dram_tensor("lg_dram", [cfg.SPB, 128, 4 * E], F32)
        self.ag1_out = nc.dram_tensor("ag1_out", [8 * TO, D], BF16, addr_space="Shared")
        self.ar1_in = nc.dram_tensor("ar1_in", [4 * cfg.SPB, E], F32)
        self.ar1_out = nc.dram_tensor("ar1_out", [4 * cfg.SPB, E], F32)
        self.tbi_in = nc.dram_tensor("tbi_in", [16, CAP], I16)
        self.tbi_out = nc.dram_tensor("tbi_out", [128, CAP], I16, addr_space="Shared")
        self.tbg_in = nc.dram_tensor("tbg_in", [16, CAP], FP16)
        self.tbg_out = nc.dram_tensor("tbg_out", [128, CAP], FP16, addr_space="Shared")
        self.aoT_sp = nc.dram_tensor("aoT_sp", [D, TO], BF16)
        self.y_mid = nc.dram_tensor("y_mid", [TO, D], F32)
        self.wi_t = nc.dram_tensor("wi_t", [cfg.ISL, 128, cfg.DCH, 512], BF16)
        self.wo_t = nc.dram_tensor("wo_t", [2 * cfg.ISL, 128, 4, 512], BF16)
        self.ag2_in = nc.dram_tensor("ag2_in", [CAP, D], F32)
        self.ag2_out = nc.dram_tensor("ag2_out", [8 * CAP, D], F32, addr_space="Shared")

        S0 = self.st
        SB = lambda n, sh, dt: S0.enter_context(nc.sbuf_tensor(n, list(sh), dt))
        self.b_ident = SB("b_ident", (128, 128), F32)
        self.b_identb = SB("b_identb", (128, 128), BF16)
        self.b_tri = SB("b_tri", (128, 128), BF16)
        self.b_mask = SB("b_mask", (128, 8, 512), BF16)
        self.b_r4t = SB("b_r4t", (128, 128), BF16)
        self.b_ocb = SB("b_ocb", (128, 1), BF16)
        self.b_orb = SB("b_orb", (1, 128), BF16)
        self.b_oc = SB("b_oc", (128, 1), F32)
        self.b_or = SB("b_or", (1, 128), F32)
        self.b_e1000 = SB("b_e1000", (128, E), F32)
        self.b_ep1 = SB("b_ep1", (128, E), F32)
        self.b_presum = SB("b_presum", (4 * cfg.SPB, cfg.SPO), F32)
        self.b_place = SB("b_place", (cfg.SPO, 4 * cfg.SPB), F32)
        self.b_agp1 = SB("b_agp1", (16, TO), I16)
        self.b_hsel = SB("b_hsel", (16, cfg.H * cfg.HD), F32)
        self.b_cmeta = {e: SB(f"b_cmeta_{e}", (1, 8), I32)
                        for e in ("sync", "gpsimd", "vector", "tensor", "scalar")}

        self.b_gkv = SB("b_gkv", (128, cfg.KVCH), F32)
        self.b_br = SB("b_br", (E, 1), F32)

        self.r_lg = SB("r_lg", (128, cfg.NT_O, E), F32)
        self.r_w = SB("r_w", (128, cfg.NT_O, E), F32)
        self.r_tk = SB("r_tk", (128, cfg.NT_O, E), F32)
        self.r_M = SB("r_M", (128, cfg.NT_O, E), BF16)
        self.r_C = SB("r_C", (128, cfg.NT_O, E), F32)
        self.g_row = SB("g_row", (128, cfg.NT_O, 2), F32)
        self.g_val = SB("g_val", (128, cfg.NT_O, 2), F32)
        self.e_idx = SB("e_idx", (128, cfg.CAPT), I32)
        self.e_gate = SB("e_gate", (128, cfg.CAPT), F32)
        self.s_recip = SB("s_recip", (16, TO), F32)
        self.scr = SB("scr", (128, 32), F32)
        self.rrow = SB("rrow", (1, 4 * E), F32)
        self.rrowb = SB("rrowb", (1, 2 * E), BF16)
        self.spsum = SB("spsum", (1, cfg.SPO * E), F32)
        self.spsum2 = SB("spsum2", (cfg.SPO, E), F32)
        self.ar_sb = SB("ar_sb", (4 * cfg.SPB, E), F32)
        self.sbase = SB("sbase", (1, cfg.SPO * E), F32)

        PS = lambda n, sh: self.st.enter_context(nc.psum_tensor(n, list(sh), F32))
        self.psG3a = PS("psG3a", (128, 1536))
        self.psG3b = PS("psG3b", (128, 1536))
        self.psC1 = PS("psC1", (128, 512))
        self.psT = PS("psT", (128, 128))
        self.ps = [self.psG3a[:, 0:512], self.psG3a[:, 512:1024], self.psG3a[:, 1024:1536],
                   self.psG3b[:, 0:512], self.psG3b[:, 512:1024], self.psG3b[:, 1024:1536],
                   self.psC1[:, 0:512]]

        st = self.st
        self.sD = Sem(st, nc, "sD")
        self.sG = Sem(st, nc, "sG")
        self.sP = Sem(st, nc, "sP")
        self.sV = Sem(st, nc, "sV")
        self.sS = Sem(st, nc, "sS")
        self.sQ = Sem(st, nc, "sQ")
        self.sC = Sem(st, nc, "sC")
        self.sX = Sem(st, nc, "sX")
        self.all_sems = [self.sD, self.sG, self.sP, self.sV, self.sS, self.sQ, self.sC, self.sX]
        self.eng_sem = {"sync": self.sD, "gpsimd": self.sQ, "tensor": self.sP,
                        "vector": self.sV, "scalar": self.sS}

        with nc.Block() as blk:
            self.blk = blk
            with nc.named_scope("phL"):
                self.ph_load()
            with ExitStack() as RA:
                SB2 = lambda n, sh, dt: RA.enter_context(nc.sbuf_tensor(n, list(sh), dt))
                self.kv_nT = SB2("kv_nT", (128, cfg.KVCH, T), BF16)
                self.q_cT = SB2("q_cT", (128, cfg.QRCH, TO), BF16)
                with nc.named_scope("phR"):
                    self.ph_R()
                with nc.named_scope("phRoute"):
                    self.ph_route()
                with nc.named_scope("phAttn"):
                    self.ph_attn()
            with nc.named_scope("phWo"):
                self.ph_wo()
            with nc.named_scope("phMoe"):
                self.ph_moe()
            with nc.named_scope("phFinal"):
                self.ph_final()
        self.st.close()
        mybir.codegen_inst_isa_subclasses(nc)
        return nc

    def _vload(self, eng, ap, lo, hi):
        val = eng.value_load(ap, min_val=lo, max_val=hi)
        try:
            self._freeq.append(val.val)
        except Exception:
            pass
        return val

    def ph_load(self):
        d = self.d

        def f(s):
            for nm, dst in [
                ("ident", self.b_ident), ("identb", self.b_identb),
                ("tri", self.b_tri), ("masktbl", self.b_mask),
                ("r4t", self.b_r4t), ("ones_colb", self.b_ocb), ("ones_rowb", self.b_orb),
                ("ones_col", self.b_oc), ("ones_row", self.b_or),
                ("eidx1000", self.b_e1000), ("eidxp1", self.b_ep1),
                ("presum", self.b_presum), ("placemat", self.b_place),
                ("agp1", self.b_agp1),
                ("headsel", self.b_hsel),
            ]:
                self.sD.t(s.dma_start(out=dst[:], in_=d[nm].ap()), 16)
            self.sD.t(s.dma_start(out=self.b_br[:, :],
                                  in_=d["b_r"].ap().rearrange("(e o) -> e o", o=1)), 16)
            for dd in self.b_cmeta.values():
                self.sD.t(s.dma_start(out=dd[:, :], in_=d["cmeta"].ap()), 16)

            self.sD.t(s.dma_start(out=self.b_gkv[:, :],
                                  in_=d["g_kv"].ap().rearrange("(c p) -> p c", p=128),
                                  allow_slow_non_contiguous=True), 16)
        self.sec("sync", f)

    def ph_R(self):
        cfg, d = self.cfg, self.d
        T, TO, D, E = cfg.T, cfg.TO, cfg.D, cfg.E
        DCH, KVCH, QRCH, SPB = cfg.DCH, cfg.KVCH, cfg.QRCH, cfg.SPB
        sP, sV = self.sP, self.sV
        ps = self.ps

        with ExitStack() as RS:
            SB = lambda n, sh, dt: RS.enter_context(self.nc.sbuf_tensor(n, list(sh), dt))
            w_kv = SB("w_kv", (128, DCH, cfg.KVR), BF16)
            w_q = SB("w_q", (128, DCH, cfg.QR), BF16)
            w_kro = SB("w_kro", (128, DCH, cfg.H * cfg.RD), BF16)
            w_qro = SB("w_qro", (128, QRCH, cfg.H * cfg.RD), BF16)
            w_rt = SB("w_rt", (128, DCH, E), BF16)
            x_sp = SB("x_sp", (128, 2, 4, D), F32)
            sqs = SB("sqs", (128, D), F32)
            xn_bf = SB("xn_bf", (128, 2, 4, D), BF16)
            lnsq = SB("lnsq", (128, 2, 4), F32)
            lnm = SB("lnm", (128, 2, 4), F32)
            m_col = SB("m_col", (128, 2, 4), F32)
            r_col = SB("r_col", (128, 2, 4), F32)
            vtmp = SB("vtmp", (128, 4), F32)
            xfm = SB("xfm", (128, DCH, 512), BF16)
            kvsp = SB("kvsp", (128, KVCH, 512), BF16)
            sqv = SB("sqv", (128, KVCH, 512), F32)
            rmsrow = SB("rmsrow", (1, 512), F32)
            q_sp = SB("q_sp", (128, 2, QRCH, 512), BF16)
            cr_t = SB("cr_t", (128, 2, 512), BF16)
            sr_t = SB("sr_t", (128, 2, 512), BF16)
            rp_a = SB("rp_a", (128, 2, 512), BF16)
            rp_b = SB("rp_b", (128, 512), F32)
            rp_c = SB("rp_c", (128, 512), F32)
            lgev = SB("lgev", (E, 512), F32)
            qr_sp = SB("qr_sp", (128, 2, 4, 512), BF16)
            kr_sp = SB("kr_sp", (128, 2, 4, 512), BF16)
            lg_sp = SB("lg_sp", (128, 2, 4, E), F32)

            def f(g):
                for nm, dst in [("W_kv", w_kv), ("W_q", w_q), ("W_krope", w_kro),
                                ("W_qrope", w_qro), ("Wr_T", w_rt)]:
                    self.sG.t(g.dma_start(out=dst[:],
                                          in_=d[nm].ap().rearrange("(c p) o -> p c o", p=128)), 16)
            self.sec("gpsimd", f)

            def f(v):
                v.tensor_scalar(w_qro[:, :, :], w_qro[:, :, :],
                                1.0 / math.sqrt(float(cfg.QD)), None, ALU.mult)
            self.sec("vector", f)

            sG, sD, sS = self.sG, self.sD, self.sS
            gsn0 = self.snap()
            norm_done = {}
            x_dma = {}
            cs_dma = {}
            spill_done = {}
            evac = {}

            def emit_x(sp, wait_v):
                def f(s, sp=sp, wait_v=wait_v):
                    if wait_v:
                        s.wait_ge(sV.h, wait_v)
                    self.sD.t(s.dma_start(
                        out=x_sp[:, sp % 2, :, :],
                        in_=d["x_b"].ap()[sp*512:(sp+1)*512, :]
                        .rearrange("(a p) b -> p a b", p=128)), 16)
                    x_dma[sp] = sD.v
                    self.sD.t(s.dma_start(out=cr_t[:, sp % 2, :],
                                          in_=d["cos4"].ap()[:, sp*512:(sp+1)*512]), 16)
                    self.sD.t(s.dma_start(out=sr_t[:, sp % 2, :],
                                          in_=d["sin4"].ap()[:, sp*512:(sp+1)*512]), 16)
                    cs_dma[sp] = sD.v
                self.sec("sync", f, snap=gsn0, raw="free")

            def emit_stats_sq(sp):
                def f(sc, sp=sp):
                    sc.wait_ge(sD.h, x_dma[sp])
                    for tl in range(4):
                        self.sS.t(sc.activation(out=sqs[:, :], in_=x_sp[:, sp % 2, tl, :],
                                                func=AF.Square,
                                                accum_out=lnsq[:, sp % 2, tl:tl+1]))
                self.sec("scalar", f, snap=gsn0, raw="free")
                return sS.v

            def emit_stats_vec(sp, sq_val):
                def f(v, sp=sp, sq_val=sq_val):
                    v.wait_ge(sD.h, x_dma[sp])
                    par = sp % 2
                    for tl in range(4):
                        sV.t(v.tensor_reduce(lnm[:, par, tl:tl+1], x_sp[:, par, tl, :],
                                             AX.X, ALU.add))
                    v.wait_ge(sS.h, sq_val)
                    sV.t(v.tensor_scalar(m_col[:, par, :], lnm[:, par, :], 1.0 / D,
                                         None, ALU.mult))
                    sV.t(v.tensor_scalar(vtmp[:, :], lnsq[:, par, :], 1.0 / D,
                                         None, ALU.mult))
                    v.wait_ge(sV.h, sV.v)
                    sV.t(v.tensor_mul(r_col[:, par, :], m_col[:, par, :], m_col[:, par, :]))
                    v.wait_ge(sV.h, sV.v)
                    sV.t(v.tensor_sub(vtmp[:, :], vtmp[:, :], r_col[:, par, :]))
                    v.wait_ge(sV.h, sV.v)
                    sV.t(v.tensor_scalar(vtmp[:, :], vtmp[:, :], 1e-5, None, ALU.add))
                    v.wait_ge(sV.h, sV.v)
                    sV.t(v.reciprocal(r_col[:, par, :], vtmp[:, :]))
                self.sec("vector", f, snap=gsn0, raw="free")
                return sV.v

            def emit_stats_rst(sp, var_val):
                def f(sc, sp=sp, var_val=var_val):
                    sc.wait_ge(sV.h, var_val)
                    self.sS.t(sc.activation(out=r_col[:, sp % 2, :], in_=r_col[:, sp % 2, :],
                                            func=AF.Sqrt))
                self.sec("scalar", f, snap=gsn0, raw="free")
                return sS.v

            def emit_norm(sp, rst_val, xn_spill_val):
                vals = []

                def f(v, sp=sp, rst_val=rst_val):
                    par = sp % 2
                    v.wait_ge(sS.h, rst_val)
                    if xn_spill_val:
                        v.wait_ge(sG.h, xn_spill_val)
                    for tl in range(4):
                        sV.t(v.tensor_scalar(xn_bf[:, par, tl, :], x_sp[:, par, tl, :],
                                             m_col[:, par, tl:tl+1], r_col[:, par, tl:tl+1],
                                             ALU.subtract, ALU.mult))
                        vals.append(sV.v)
                self.sec("vector", f, snap=gsn0, raw="free")
                norm_done[sp] = list(vals)

            emit_x(0, 0)
            sq0 = emit_stats_sq(0)
            var0 = emit_stats_vec(0, sq0)
            rst0 = emit_stats_rst(0, var0)
            emit_norm(0, rst0, 0)
            emit_x(1, 0)

            for sp in range(SPB):
                par = sp % 2
                p0, v0 = sP.v, sV.v

                def f(t, sp=sp, par=par, v0=v0):
                    for i in range(32):
                        tl, fc = divmod(i, 8)
                        if fc == 0:
                            t.wait_ge(sV.h, norm_done[sp][tl])
                        if i >= 2:
                            t.wait_ge(sV.h, v0 + i - 1)
                        dst = (self.psT[:, :] if i % 2 == 0
                               else ps[2]).bitcast(BF16)[0:128, 0:128]
                        sP.t(t.transpose(dst, xn_bf[:, par, tl, fc*128:(fc+1)*128],
                                         self.b_identb[:, :]))
                self.sec("tensor", f, snap=gsn0, raw="free")

                def f(v, p0=p0):
                    for i in range(32):
                        tl, fc = divmod(i, 8)
                        v.wait_ge(sP.h, p0 + i + 1)
                        src = (self.psT[:, :] if i % 2 == 0
                               else ps[2]).bitcast(BF16)[0:128, 0:128]
                        sV.t(v.tensor_copy(xfm[:, fc, tl*128:(tl+1)*128], src))
                self.sec("vector", f, snap=gsn0, raw="free")
                te_last = sV.v

                p0, v0 = sP.v, sV.v

                def f(t, v0=v0, te_last=te_last):
                    t.wait_ge(sV.h, te_last)
                    if ("kvn", 1) in evac:
                        t.wait_ge(sV.h, evac[("kvn", 1)])
                    for kvc in range(KVCH):
                        for fc in range(DCH):
                            inst = t.matmul(ps[kvc][:, :], w_kv[:, fc, kvc*128:(kvc+1)*128],
                                            xfm[:, fc, :], start=(fc == 0), stop=(fc == DCH - 1))
                        sP.t(inst)
                self.sec("tensor", f, snap=gsn0, raw="free")

                def f(v, p0=p0):
                    for kvc in range(KVCH):
                        v.wait_ge(sP.h, p0 + kvc + 1)
                        sV.t(v.tensor_copy(kvsp[:, kvc, :], ps[kvc][:, :]))
                self.sec("vector", f, snap=gsn0, raw="free")
                kvsp_done = sV.v

                def f(sc, sp=sp, kvsp_done=kvsp_done):
                    sc.wait_ge(sV.h, kvsp_done)
                    for kvc in range(KVCH):
                        self.sS.t(sc.activation(out=sqv[:, kvc, :], in_=kvsp[:, kvc, :],
                                                func=AF.Square))
                self.sec("scalar", f, snap=gsn0, raw="free")
                sqv_done = sS.v
                sqn = emit_stats_sq(sp + 1) if sp + 1 < SPB else 0

                p0, v0 = sP.v, sV.v

                def f(t, sp=sp, kvsp_done=kvsp_done):
                    if ("qev", 3) in evac:
                        t.wait_ge(sV.h, evac[("qev", 3)])
                    for qc in range(QRCH):
                        if qc < 2:
                            t.wait_ge(sV.h, kvsp_done)
                        for fc in range(DCH):
                            inst = t.matmul(ps[qc % 2][:, :], w_q[:, fc, qc*128:(qc+1)*128],
                                            xfm[:, fc, :], start=(fc == 0), stop=(fc == DCH - 1))
                        sP.t(inst)
                self.sec("tensor", f, snap=gsn0, raw="free")

                def f(v, sp=sp, par=par, p0=p0):
                    if ("qspill", sp - 2) in spill_done:
                        v.wait_ge(sG.h, spill_done[("qspill", sp - 2)])
                    for qc in range(QRCH):
                        v.wait_ge(sP.h, p0 + qc + 1)
                        sV.t(v.tensor_copy(q_sp[:, par, qc, :], ps[qc % 2][:, :]))
                        evac[("qev", qc)] = sV.v
                self.sec("vector", f, snap=gsn0, raw="free")

                if sp + 1 < SPB:
                    varn = emit_stats_vec(sp + 1, sqn)

                p0 = sP.v

                def f(t, sp=sp, sqv_done=sqv_done):
                    t.wait_ge(sS.h, sqv_done)
                    if ("kvn", 1) in evac:
                        t.wait_ge(sV.h, evac[("kvn", 1)])
                    for kvc in range(KVCH):
                        inst = t.matmul(ps[5][0:1, :], self.b_oc[:, 0:1], sqv[:, kvc, :],
                                        start=(kvc == 0), stop=(kvc == KVCH - 1))
                    sP.t(inst)
                self.sec("tensor", f, snap=gsn0, raw="free")

                def f(v, p0=p0):
                    v.wait_ge(sP.h, p0 + 1)
                    sV.t(v.tensor_scalar(rmsrow[:, :], ps[5][0:1, :], 1.0 / cfg.KVR, 1e-6,
                                         ALU.mult, ALU.add))
                    v.wait_ge(sV.h, sV.v)
                    sV.t(v.reciprocal(rmsrow[:, :], rmsrow[:, :]))
                self.sec("vector", f, snap=gsn0, raw="free")
                rms_scaled = sV.v

                def f(sc, rms_scaled=rms_scaled):
                    sc.wait_ge(sV.h, rms_scaled)
                    self.sS.t(sc.activation(out=rmsrow[:, :], in_=rmsrow[:, :], func=AF.Sqrt))
                self.sec("scalar", f, snap=gsn0, raw="free")
                rmsr_done = sS.v
                rstn = emit_stats_rst(sp + 1, varn) if sp + 1 < SPB else 0

                pe_at = {}
                c = 0
                for g4 in range(4):
                    c += 1
                    pe_at[("proj", g4)] = c
                    if g4 >= 1:
                        c += 1
                        pe_at[("rot", g4 - 1)] = c
                c += 1
                pe_at[("rot", 3)] = c

                def emit_rope_pair(wsrc, nw, dstbuf, kind):
                    p0, v0 = sP.v, sV.v
                    rpa_at = lambda g4: v0 + 4 * g4 + 1
                    comb_at = lambda g4: v0 + 4 * g4 + 4

                    def f(t, sp=sp, par=par, wsrc=wsrc, nw=nw):
                        for g4 in range(4):
                            if g4 >= 2:
                                t.wait_ge(sV.h, rpa_at(g4 - 2))
                            for w in range(nw):
                                rhs = xfm[:, w, :] if nw == DCH else q_sp[:, par, w, :]
                                inst = t.matmul(ps[g4 % 2][:, :], wsrc[:, w, g4*128:(g4+1)*128],
                                                rhs, start=(w == 0), stop=(w == nw - 1))
                            sP.t(inst)
                            if g4 >= 1:
                                t.wait_ge(sV.h, rpa_at(g4 - 1))
                                if g4 >= 3:
                                    t.wait_ge(sV.h, comb_at(g4 - 3))
                                sP.t(t.matmul(ps[3 + ((g4 - 1) % 2)][:, :], self.b_r4t[:, :],
                                              rp_a[:, (g4 - 1) % 2, :], start=True, stop=True))
                        t.wait_ge(sV.h, rpa_at(3))
                        t.wait_ge(sV.h, comb_at(1))
                        sP.t(t.matmul(ps[3 + (3 % 2)][:, :], self.b_r4t[:, :],
                                      rp_a[:, 3 % 2, :], start=True, stop=True))
                    self.sec("tensor", f, snap=gsn0, raw="free")

                    def f(v, sp=sp, par=par, p0=p0, dstbuf=dstbuf, kind=kind):
                        if (kind, sp - 2) in spill_done:
                            v.wait_ge(sG.h, spill_done[(kind, sp - 2)])
                        for g4 in range(4):
                            v.wait_ge(sP.h, p0 + pe_at[("proj", g4)])
                            sV.t(v.tensor_copy(rp_a[:, g4 % 2, :], ps[g4 % 2][:, :]))
                            v.wait_ge(sP.h, p0 + pe_at[("rot", g4)])
                            sV.t(v.tensor_mul(rp_b[:, :], ps[3 + (g4 % 2)][:, :],
                                              sr_t[:, par, :]))
                            sV.t(v.tensor_mul(rp_c[:, :], rp_a[:, g4 % 2, :], cr_t[:, par, :]))
                            v.wait_ge(sV.h, sV.v)
                            sV.t(v.tensor_add(dstbuf[:, par, g4, :], rp_c[:, :], rp_b[:, :]))
                    self.sec("vector", f, snap=gsn0, raw="free")
                    return comb_at(3)

                combk = emit_rope_pair(w_kro, DCH, kr_sp, "krspill")

                p0 = sP.v

                def f(t, rmsr_done=rmsr_done):
                    t.wait_ge(sS.h, rmsr_done)
                    sP.t(t.matmul(ps[5][:, :], self.b_or[:, :], rmsrow[:, :],
                                  start=True, stop=True))
                self.sec("tensor", f, snap=gsn0, raw="free")

                def f(v, sp=sp, p0=p0):
                    v.wait_ge(sP.h, p0 + 1)
                    for kvc in range(KVCH):
                        sV.t(v.scalar_tensor_tensor(self.kv_nT[:, kvc, sp*512:(sp+1)*512],
                                                    kvsp[:, kvc, :], self.b_gkv[:, kvc:kvc+1],
                                                    ps[5][:, :], ALU.mult, ALU.mult))
                        evac[("kvn", kvc)] = sV.v
                self.sec("vector", f, snap=gsn0, raw="free")

                combq = emit_rope_pair(w_qro, QRCH, qr_sp, "qrspill")

                p0 = sP.v

                def f(t):
                    for fc in range(DCH):
                        inst = t.matmul(ps[0][0:E, :], w_rt[:, fc, :], xfm[:, fc, :],
                                        start=(fc == 0), stop=(fc == DCH - 1))
                    sP.t(inst)
                self.sec("tensor", f, snap=gsn0, raw="free")

                def f(v, p0=p0):
                    v.wait_ge(sP.h, p0 + 1)
                    sV.t(v.tensor_scalar(lgev[:, :], ps[0][0:E, :], self.b_br[:, 0:1],
                                         None, ALU.add))
                self.sec("vector", f, snap=gsn0, raw="free")
                lgev_done = sV.v

                p0, v0 = sP.v, sV.v

                def f(t, te_last=te_last, lgev_done=lgev_done, v0=v0):
                    t.wait_ge(sV.h, lgev_done)
                    for tl in range(4):
                        if tl >= 2:
                            t.wait_ge(sV.h, v0 + tl - 1)
                        pp = self.psT[0:128, 0:E] if tl % 2 == 0 else ps[2][0:128, 0:E]
                        sP.t(t.transpose(pp, lgev[:, tl*128:(tl+1)*128],
                                         self.b_ident[0:E, 0:E]))
                self.sec("tensor", f, snap=gsn0, raw="free")

                def f(v, sp=sp, par=par, p0=p0):
                    if ("lgspill", sp - 2) in spill_done:
                        v.wait_ge(sG.h, spill_done[("lgspill", sp - 2)])
                    for tl in range(4):
                        v.wait_ge(sP.h, p0 + tl + 1)
                        pp = self.psT[0:128, 0:E] if tl % 2 == 0 else ps[2][0:128, 0:E]
                        sV.t(v.tensor_copy(lg_sp[:, par, tl, :], pp))
                self.sec("vector", f, snap=gsn0, raw="free")
                lg_done = sV.v

                if sp + 1 < SPB:
                    emit_norm(sp + 1, rstn, spill_done.get(("xnspill", sp - 1), 0))

                qev3 = evac[("qev", 3)]

                def f(g, sp=sp, par=par, qev3=qev3, combk=combk, combq=combq,
                      lg_done=lg_done):
                    g.wait_ge(sV.h, norm_done[sp][3])
                    self.sG.t(g.dma_start(
                        out=self.xn_bounce.ap()[sp*512:(sp+1)*512, :]
                        .rearrange("(a p) b -> p a b", p=128),
                        in_=xn_bf[:, par, :, :]), 16)
                    spill_done[("xnspill", sp)] = sG.v
                    g.wait_ge(sV.h, qev3)
                    self.sG.t(g.dma_start(
                        out=self.q_dram.ap()[sp, :, :],
                        in_=q_sp[:, par, :, :].rearrange("p a b -> p (a b)")), 16)
                    spill_done[("qspill", sp)] = sG.v
                    g.wait_ge(sV.h, combk)
                    self.sG.t(g.dma_start(
                        out=self.kr_dram.ap()[:, :, sp*512:(sp+1)*512],
                        in_=kr_sp[:, par, :, :]), 16)
                    spill_done[("krspill", sp)] = sG.v
                    g.wait_ge(sV.h, combq)
                    self.sG.t(g.dma_start(
                        out=self.qr_dram.ap()[sp, :, :],
                        in_=qr_sp[:, par, :, :].rearrange("p a b -> p (a b)")), 16)
                    spill_done[("qrspill", sp)] = sG.v
                    g.wait_ge(sV.h, lg_done)
                    self.sG.t(g.dma_start(
                        out=self.lg_dram.ap()[sp, :, :],
                        in_=lg_sp[:, par, :, :].rearrange("p a b -> p (a b)")), 16)
                    spill_done[("lgspill", sp)] = sG.v
                self.sec("gpsimd", f, snap=gsn0, raw="free")

                if sp + 2 < SPB:
                    emit_x(sp + 2, combq)

        cfg2 = self.cfg

        def f(g):
            qh = self._vload(g, self.b_cmeta['gpsimd'][0:1, 0:1], 0, 1)
            for j in range(cfg2.SPO):
                gsp = g.snap(qh + 2 * j, min_val=0, max_val=cfg2.SPB - 1)
                self.sG.t(g.dma_start(
                    out=self.q_cT[:, :, j*512:(j+1)*512],
                    in_=self.q_dram.ap()[ts(gsp, 1), :, :]
                    .rearrange("a p (c b) -> p (a c) b", c=cfg2.QRCH)), 16)
                self.sG.t(g.dma_start(
                    out=self.xn_spill.ap()[j*512:(j+1)*512, :],
                    in_=self.xn_bounce.ap()[ts(gsp, 512), :]), 16)
        self.sec("gpsimd", f)

        def f(sc):
            qh = self._vload(sc, self.b_cmeta['scalar'][0:1, 0:1], 0, 1)
            for j in range(cfg2.SPO):
                gsp = sc.snap(qh + 2 * j, min_val=0, max_val=cfg2.SPB - 1)
                self.sD.t(sc.dma_start(
                    out=self.qro_dram.ap()[:, :, j*512:(j+1)*512],
                    in_=self.qr_dram.ap()[ts(gsp, 1), :, :]
                    .rearrange("a p (c b) -> p (a c) b", c=4)), 16)
                self.sD.t(sc.dma_start(
                    out=self.r_lg[:, 4*j:4*(j+1), :],
                    in_=self.lg_dram.ap()[ts(gsp, 1), :, :]
                    .rearrange("a p (c b) -> p (a c) b", c=4)), 16)
        self.sec("scalar", f, raw=True)

    def ph_route(self):
        cfg = self.cfg
        E, CAP, TO = cfg.E, cfg.CAP, cfg.TO
        sV, sP = self.sV, self.sP
        ps = self.ps

        def f(g):
            self.sC.t(g.collective_compute(
                "AllGather", ALU.bypass, replica_groups=[list(range(8))],
                ins=[self.xn_spill.ap()[0:TO, :].opt()], outs=[self.ag1_out.ap().opt()]), 1)
            for isl in range(self.cfg.ISL):
                self.sG.t(g.dma_start(
                    out=self.wi_t.ap()[isl],
                    in_=self.d["Wi"].ap()[:, isl*512:(isl+1)*512]
                    .rearrange("(c p) o -> p c o", p=128)), 16)
            for k2 in range(2 * self.cfg.ISL):
                Fh, isl = k2 // self.cfg.ISL, k2 % self.cfg.ISL
                self.sG.t(g.dma_start(
                    out=self.wo_t.ap()[k2],
                    in_=self.d["Wo"].ap()[isl*512:(isl+1)*512, Fh*512:(Fh+1)*512]
                    .rearrange("(c p) o -> p c o", p=128)), 16)
        self.sec("gpsimd", f)

        with ExitStack() as XR:
            cntrow = XR.enter_context(self.nc.sbuf_tensor("cntrow", [1, cfg.NT_O * E], F32))
            rbrow = XR.enter_context(self.nc.sbuf_tensor("rbrow", [1, cfg.NT_O * E], F32))
            rsn = self.snap()
            v0, s0, p0 = sV.v, self.sS.v, self.sP.v
            S1 = lambda tt: s0 + tt + 1
            VA1, VB1 = {}, {}
            PB = lambda tt: (0, 8) if tt % 2 == 0 else (16, 24)

            def f(v):
                def A(tt):
                    b, m8 = PB(tt)
                    if tt >= 2:
                        v.wait_ge(self.sS.h, S1(tt - 2))
                    sV.t(v.reduce_max(self.scr[:, b:b+1], self.r_lg[:, tt, :], AX.X))
                    v.wait_ge(sV.h, sV.v)
                    sV.t(v.tensor_scalar(self.scr[:, b+1:b+2], self.scr[:, b:b+1], -1.0,
                                         None, ALU.mult))
                    VA1[tt] = sV.v
                A(0); A(1)
                for tt in range(cfg.NT_O):
                    b, m8 = PB(tt)
                    v.wait_ge(self.sS.h, S1(tt))
                    sV.t(v.reciprocal(self.scr[:, b+3:b+4], self.scr[:, b+2:b+3]))
                    v.wait_ge(sV.h, sV.v)
                    sV.t(v.tensor_scalar(self.r_w[:, tt, :], self.r_w[:, tt, :],
                                         self.scr[:, b+3:b+4], None, ALU.mult))
                    v.wait_ge(sV.h, sV.v)
                    sV.t(v.max(out=self.scr[:, m8:m8+8], in_=self.r_w[:, tt, :]))
                    v.wait_ge(sV.h, sV.v)
                    sV.t(v.scalar_tensor_tensor(self.r_tk[:, tt, :], self.r_w[:, tt, :],
                                                self.scr[:, m8+1:m8+2], self.r_w[:, tt, :],
                                                ALU.is_ge, ALU.mult))
                    v.wait_ge(sV.h, sV.v)
                    sV.t(v.tensor_scalar(self.r_M[:, tt, :], self.r_tk[:, tt, :], 0.0,
                                         None, ALU.is_gt))
                    VB1[tt] = sV.v
                    if tt + 2 < cfg.NT_O:
                        A(tt + 2)
                v.wait_ge(self.sP.h, p0 + cfg.NT_O)
                sV.t(v.tensor_copy(cntrow[:, :], ps[3][0:1, 0:cfg.NT_O * E]))
                for j in range(cfg.SPO):
                    v.wait_ge(sV.h, sV.v)
                    sV.t(v.tensor_add(self.spsum[:, j*E:(j+1)*E],
                                      cntrow[:, (4*j)*E:(4*j+1)*E],
                                      cntrow[:, (4*j+1)*E:(4*j+2)*E]))
                    v.wait_ge(sV.h, sV.v)
                    sV.t(v.tensor_add(self.spsum[:, j*E:(j+1)*E],
                                      self.spsum[:, j*E:(j+1)*E],
                                      cntrow[:, (4*j+2)*E:(4*j+3)*E]))
                    v.wait_ge(sV.h, sV.v)
                    sV.t(v.tensor_add(self.spsum[:, j*E:(j+1)*E],
                                      self.spsum[:, j*E:(j+1)*E],
                                      cntrow[:, (4*j+3)*E:(4*j+4)*E]))
            self.sec("vector", f, snap=rsn, raw="free")

            def f(sc):
                for tt in range(cfg.NT_O):
                    b, m8 = PB(tt)
                    sc.wait_ge(sV.h, VA1[tt])
                    sc.wait_ge(self.sS.h, self.sS.v)
                    self.sS.t(sc.activation(out=self.r_w[:, tt, :], in_=self.r_lg[:, tt, :],
                                            func=AF.Exp, bias=self.scr[:, b+1:b+2],
                                            accum_out=self.scr[:, b+2:b+3]))
            self.sec("scalar", f, snap=rsn, raw="free")

            def f(t):
                for tt in range(cfg.NT_O):
                    t.wait_ge(sV.h, VB1[tt])
                    sP.t(t.matmul(ps[3][0:1, tt*E:(tt+1)*E], self.b_ocb[:, 0:1],
                                  self.r_M[:, tt, :], start=True, stop=True))
            self.sec("tensor", f, snap=rsn, raw="free")

            with ExitStack() as XS:
                r_ar = XS.enter_context(self.nc.sbuf_tensor("r_ar", [4 * cfg.SPB, E], F32))

                def f(g):
                    for j in range(cfg.SPO):
                        self.sG.t(g.dma_start(out=self.spsum2[j:j+1, :],
                                              in_=self.spsum[0:1, j*E:(j+1)*E]), 16)
                self.sec("gpsimd", f)

                def f(t):
                    t.matmul(ps[2][0:4*cfg.SPB, 0:E], self.b_place[:, :], self.spsum2[:, :],
                             start=True, stop=True)
                self.sec("tensor", f)

                def f(v):
                    v.tensor_copy(self.ar_sb[:, :], ps[2][0:4*cfg.SPB, 0:E])
                self.sec("vector", f)

                def f(g):
                    self.sG.t(g.dma_start(out=self.ar1_in.ap(), in_=self.ar_sb[:, :]), 16)
                self.sec("gpsimd", f)

                def f(g):
                    self.sC.t(g.collective_compute(
                        "AllReduce", ALU.add, replica_groups=[list(range(8))],
                        ins=[self.ar1_in.ap().opt()], outs=[self.ar1_out.ap().opt()]), 1)
                self.sec("gpsimd", f)

                def f(s):
                    self.sD.t(s.dma_start(out=r_ar[:, :], in_=self.ar1_out.ap()), 16)
                self.sec("sync", f)

                for j in range(cfg.SPO):
                    def f(t, j=j):
                        t.matmul(ps[2][0:1, 0:E], self.b_presum[:, j:j+1], r_ar[:, :],
                                 start=True, stop=True)
                    self.sec("tensor", f)

                    def f(v, j=j):
                        v.tensor_copy(self.sbase[:, j*E:(j+1)*E], ps[2][0:1, 0:E])
                    self.sec("vector", f)

            def f(v):
                for j in range(cfg.SPO):
                    sV.t(v.tensor_copy(rbrow[:, (4*j)*E:(4*j+1)*E],
                                       self.sbase[:, j*E:(j+1)*E]))
                    for k in range(1, 4):
                        v.wait_ge(sV.h, sV.v)
                        sV.t(v.tensor_add(rbrow[:, (4*j+k)*E:(4*j+k+1)*E],
                                          rbrow[:, (4*j+k-1)*E:(4*j+k)*E],
                                          cntrow[:, (4*j+k-1)*E:(4*j+k)*E]))
            self.sec("vector", f, raw="free")

            rsn2 = self.snap()
            p0, v0 = self.sP.v, sV.v
            RC = lambda tt: v0 + 2 * (tt + 1)

            def f(t):
                for tt in range(cfg.NT_O):
                    pA, pB = (ps[2], ps[3]) if tt % 2 == 0 else (ps[4], ps[5])
                    if tt >= 2:
                        t.wait_ge(sV.h, RC(tt - 2))
                    sP.t(t.matmul(pA[:, 0:E], self.b_or[:, :], rbrow[0:1, tt*E:(tt+1)*E],
                                  start=True, stop=True))
                    sP.t(t.matmul(pB[:, 0:E], self.b_tri[:, :], self.r_M[:, tt, :],
                                  start=True, stop=True))
            self.sec("tensor", f, snap=rsn2, raw="free")

            def f(v):
                for tt in range(cfg.NT_O):
                    pA, pB = (ps[2], ps[3]) if tt % 2 == 0 else (ps[4], ps[5])
                    v.wait_ge(self.sP.h, p0 + 2 * (tt + 1))
                    sV.t(v.tensor_copy(self.r_C[:, tt, :], pB[:, 0:E]))
                    v.wait_ge(sV.h, sV.v)
                    sV.t(v.tensor_add(self.r_C[:, tt, :], self.r_C[:, tt, :], pA[:, 0:E]))
            self.sec("vector", f, snap=rsn2, raw="free")

        with ExitStack() as XS:
            SB = lambda n, sh, dt: XS.enter_context(self.nc.sbuf_tensor(n, list(sh), dt))
            selb = SB("selb", (128, E), F32)
            tmp1 = SB("tmp1", (128, E), F32)
            tmp2 = SB("tmp2", (128, E), F32)
            gsl = SB("gsl", (128, E), F32)
            r_i16 = SB("r_i16", (16, TO), I16)
            r_i16b = SB("r_i16b", (16, TO), I16)
            r_g16 = SB("r_g16", (16, TO), FP16)
            r_gslf = SB("r_gslf", (8, TO), F32)
            r_gtmp = SB("r_gtmp", (8, TO), F32)

            def f(v):
                v.memset(r_i16[:, :], -1)
                v.memset(r_i16b[:, :], -1)
                v.memset(r_g16[:, :], 0.0)
            self.sec("vector", f)

            for tt in range(cfg.NT_O):
                def f(v, tt=tt):
                    v.tensor_scalar(selb[:, :], self.r_C[:, tt, :], float(CAP), None, ALU.is_le)
                    v.tensor_scalar(tmp1[:, :], self.r_tk[:, tt, :], 0.0, None, ALU.is_gt)
                    v.tensor_mul(selb[:, :], selb[:, :], tmp1[:, :])
                    v.tensor_mul(gsl[:, :], self.r_C[:, tt, :], selb[:, :])
                    v.tensor_scalar(gsl[:, :], gsl[:, :], 1.0, None, ALU.subtract)
                    v.scalar_tensor_tensor(tmp1[:, :], selb[:, :], -1000.0,
                                           self.b_e1000[:, :], ALU.mult, ALU.add)
                    v.tensor_reduce(self.scr[:, 0:1], tmp1[:, :], AX.X, ALU.min)
                    v.tensor_scalar(tmp2[:, :], tmp1[:, :], self.scr[:, 0:1], None,
                                    ALU.is_equal)
                    v.tensor_mul(tmp2[:, :], tmp2[:, :], self.r_C[:, tt, :])
                    v.tensor_reduce(self.scr[:, 1:2], tmp2[:, :], AX.X, ALU.add)
                    v.tensor_scalar(self.scr[:, 2:3], self.scr[:, 0:1], float(CAP), None, ALU.mult)
                    v.tensor_add(self.scr[:, 2:3], self.scr[:, 2:3], self.scr[:, 1:2])
                    v.tensor_scalar(self.scr[:, 2:3], self.scr[:, 2:3], 1.0, None, ALU.subtract)
                    v.tensor_scalar(self.scr[:, 3:4], self.scr[:, 0:1], 999.0, None, ALU.is_le)
                    v.tensor_mul(self.g_row[:, tt, 0:1], self.scr[:, 2:3], self.scr[:, 3:4])
                    v.tensor_copy(self.g_val[:, tt, 0:1], self.scr[:, 3:4])
                    v.tensor_mul(tmp2[:, :], selb[:, :], self.b_ep1[:, :])
                    v.tensor_scalar(tmp2[:, :], tmp2[:, :], 1.0, None, ALU.subtract)
                    v.tensor_reduce(self.scr[:, 4:5], tmp2[:, :], AX.X, ALU.max)
                    v.tensor_scalar(tmp1[:, :], tmp2[:, :], self.scr[:, 4:5], None,
                                    ALU.is_equal)
                    v.tensor_mul(tmp1[:, :], tmp1[:, :], self.r_C[:, tt, :])
                    v.tensor_reduce(self.scr[:, 5:6], tmp1[:, :], AX.X, ALU.add)
                    v.tensor_scalar(self.scr[:, 6:7], self.scr[:, 4:5], float(CAP), None, ALU.mult)
                    v.tensor_add(self.scr[:, 6:7], self.scr[:, 6:7], self.scr[:, 5:6])
                    v.tensor_scalar(self.scr[:, 6:7], self.scr[:, 6:7], 1.0, None, ALU.subtract)
                    v.tensor_tensor(self.scr[:, 7:8], self.scr[:, 4:5], self.scr[:, 0:1], ALU.is_gt)
                    v.tensor_mul(self.g_row[:, tt, 1:2], self.scr[:, 6:7], self.scr[:, 7:8])
                    v.tensor_copy(self.g_val[:, tt, 1:2], self.scr[:, 7:8])
                self.sec("vector", f)

                def f(t, tt=tt):
                    t.transpose(self.psT[0:E, 0:128], gsl[:, :], self.b_ident[:, :])
                    t.transpose(self.psC1[0:E, 0:128], self.r_tk[:, tt, :], self.b_ident[:, :])
                self.sec("tensor", f)

                def f(v, tt=tt):
                    v.tensor_copy(r_gslf[0:E, tt*128:(tt+1)*128], self.psT[0:E, 0:128])
                    v.tensor_copy(r_g16[0:E, tt*128:(tt+1)*128], self.psC1[0:E, 0:128])
                self.sec("vector", f)

            halfc = CAP // 2
            with ExitStack() as YS:
                t_i = YS.enter_context(self.nc.sbuf_tensor("t_i", [16, CAP], I16))
                t_g = YS.enter_context(self.nc.sbuf_tensor("t_g", [16, CAP], FP16))

                def f(v):
                    v.tensor_scalar(r_gtmp[:, :], r_gslf[:, :], float(halfc), None, ALU.is_lt)
                    v.tensor_scalar(r_gslf[:, :].copy() if False else r_gslf[:, :],
                                    r_gslf[:, :], 0.0, None, ALU.add)
                    v.scalar_tensor_tensor(r_gtmp[:, :], r_gslf[:, :], 1.0, r_gtmp[:, :],
                                           ALU.add, ALU.mult)
                    v.tensor_scalar(r_gtmp[:, :], r_gtmp[:, :], 1.0, None, ALU.subtract)
                    v.tensor_copy(r_i16[0:E, :], r_gtmp[:, :])
                    v.tensor_scalar(r_gtmp[:, :], r_gslf[:, :], float(halfc), None, ALU.is_ge)
                    v.scalar_tensor_tensor(r_gtmp[:, :], r_gslf[:, :], float(1 - halfc),
                                           r_gtmp[:, :], ALU.add, ALU.mult)
                    v.tensor_scalar(r_gtmp[:, :], r_gtmp[:, :], 1.0, None, ALU.subtract)
                    v.tensor_copy(r_i16b[0:E, :], r_gtmp[:, :])
                self.sec("vector", f)

                def f(g):
                    g.load_library(libcfg.local_scatter)
                    self.sQ.t(g.local_scatter(t_i[:, 0:halfc], self.b_agp1[:, :], r_i16[:, :],
                                              channels=16, num_elems=halfc, num_idxs=TO))
                    self.sQ.t(g.local_scatter(t_i[:, halfc:CAP], self.b_agp1[:, :], r_i16b[:, :],
                                              channels=16, num_elems=CAP - halfc, num_idxs=TO))
                    self.sQ.t(g.local_scatter(t_g[:, 0:halfc], r_g16[:, :], r_i16[:, :],
                                              channels=16, num_elems=halfc, num_idxs=TO))
                    self.sQ.t(g.local_scatter(t_g[:, halfc:CAP], r_g16[:, :], r_i16b[:, :],
                                              channels=16, num_elems=CAP - halfc, num_idxs=TO))
                self.sec("gpsimd", f)

                def f(g):
                    self.sG.t(g.dma_start(out=self.tbi_in.ap(), in_=t_i[:, :]), 16)
                    self.sG.t(g.dma_start(out=self.tbg_in.ap(), in_=t_g[:, :]), 16)
                self.sec("gpsimd", f)

                def f(g):
                    self.sC.t(g.collective_compute(
                        "AllGather", ALU.bypass, replica_groups=[list(range(8))],
                        ins=[self.tbi_in.ap().opt()], outs=[self.tbi_out.ap().opt()]), 1)
                    self.sC.t(g.collective_compute(
                        "AllGather", ALU.bypass, replica_groups=[list(range(8))],
                        ins=[self.tbg_in.ap().opt()], outs=[self.tbg_out.ap().opt()]), 1)
                self.sec("gpsimd", f)

    def ph_attn(self):
        cfg, d = self.cfg, self.d
        T, TO = cfg.T, cfg.TO
        H, HD, RD = cfg.H, cfg.HD, cfg.RD
        KVCH, QRCH, SPO = cfg.KVCH, cfg.QRCH, cfg.SPO
        sP, sV, sS, sX = self.sP, self.sV, self.sS, self.sX
        sG, sD = self.sG, self.sD
        ps = self.ps
        qsc = 1.0 / math.sqrt(float(cfg.QD))

        with ExitStack() as AS:
            SB = lambda n, sh, dt: AS.enter_context(self.nc.sbuf_tensor(n, list(sh), dt))
            w_kd = SB("w_kd", (128, KVCH, cfg.D), BF16)
            w_vd = SB("w_vd", (128, KVCH, cfg.D), BF16)
            w_qd = SB("w_qd", (128, QRCH, cfg.D), BF16)
            v_g = SB("v_g", (128, T // 128, 4 * (HD + 1)), BF16)
            kt2 = SB("kt2", (128, 2, T), BF16)
            qt2 = SB("qt2", (128, 2, TO), BF16)
            sqk = SB("sqk", (128, T), BF16)
            sqq = SB("sqq", (128, TO), BF16)
            probs = SB("probs", (128, 2, 1536), BF16)
            att_acc = SB("att_acc", (128, SPO, 512), F32)
            brow = SB("brow", (1, T), F32)
            bqn = SB("bqn", (1, TO), F32)
            bqnb = SB("bqnb", (1, TO), BF16)
            att_bf = SB("att_bf", (64, 2, 512), BF16)
            att_f = SB("att_f", (64, 512), F32)

            def f(g):
                for nm, dst in [("W_kdec", w_kd), ("W_vdec", w_vd), ("W_qdec", w_qd)]:
                    self.sG.t(g.dma_start(out=dst[:],
                                          in_=d[nm].ap().rearrange("(c p) o -> p c o", p=128)), 16)
            self.sec("gpsimd", f)

            def f(v):
                v.memset(self.s_recip[:, :], 0.0)
                v.memset(kt2[96:97, 0, :], 1.0)
                v.memset(kt2[96:97, 1, :], 1.0)
                v.memset(sqk[96:97, :], 1.0)
                for l in range(4):
                    v.memset(v_g[:, :, l*(HD+1)+HD : l*(HD+1)+HD+1], 1.0)
            self.sec("vector", f)

            def emit_build(h, snap0):
                buf = h % 2
                p0, v0 = sP.v, sV.v
                items = [("k", sp8) for sp8 in range(T // 512)] + \
                        [("q", so) for so in range(SPO)]

                def f(t, h=h, v0=v0, items=items):
                    for idx, (kind, sp) in enumerate(items):
                        if idx >= 2:
                            t.wait_ge(sV.h, v0 + 2 * idx - 3)
                        pp = ps[idx % 2]
                        if kind == "k":
                            for kvc in range(KVCH):
                                inst = t.matmul(pp[0:HD, :], w_kd[:, kvc, h*HD:(h+1)*HD],
                                                self.kv_nT[:, kvc, sp*512:(sp+1)*512],
                                                start=(kvc == 0), stop=(kvc == KVCH - 1))
                        else:
                            for qc in range(QRCH):
                                inst = t.matmul(pp[0:HD, :], w_qd[:, qc, h*HD:(h+1)*HD],
                                                self.q_cT[:, qc, sp*512:(sp+1)*512],
                                                start=(qc == 0), stop=(qc == QRCH - 1))
                        sP.t(inst)
                self.sec("tensor", f, snap=snap0, raw="free")

                def f(v, buf=buf, p0=p0, items=items):
                    for idx, (kind, sp) in enumerate(items):
                        v.wait_ge(sP.h, p0 + idx + 1)
                        pp = ps[idx % 2]
                        if kind == "k":
                            sV.t(v.tensor_copy(kt2[0:HD, buf, sp*512:(sp+1)*512], pp[0:HD, :]))
                            sV.t(v.tensor_mul(sqk[0:HD, sp*512:(sp+1)*512],
                                              kt2[0:HD, buf, sp*512:(sp+1)*512],
                                              kt2[0:HD, buf, sp*512:(sp+1)*512]))
                        else:
                            sV.t(v.tensor_scalar(qt2[0:HD, buf, sp*512:(sp+1)*512], pp[0:HD, :],
                                                 qsc, None, ALU.mult))
                            sV.t(v.tensor_mul(sqq[0:HD, sp*512:(sp+1)*512],
                                              qt2[0:HD, buf, sp*512:(sp+1)*512],
                                              qt2[0:HD, buf, sp*512:(sp+1)*512]))
                self.sec("vector", f, snap=snap0, raw=True)

            def emit_rope_dma(h):
                buf, g4, l = h % 2, h // 4, h % 4

                def f(g, buf=buf, g4=g4, l=l):
                    self.sG.t(g.dma_start(
                        out=kt2[HD:HD+RD, buf, :],
                        in_=self.kr_dram.ap()[l*RD:(l+1)*RD, g4:g4+1, :]
                        .rearrange("p a b -> p (a b)")), 16)
                    self.sG.t(g.dma_start(
                        out=qt2[HD:HD+RD, buf, :],
                        in_=self.qro_dram.ap()[l*RD:(l+1)*RD, g4:g4+1, :]
                        .rearrange("p a b -> p (a b)")), 16)
                self.sec("gpsimd", f)

            def emit_rope_sq(h):
                buf = h % 2

                def f(v, buf=buf):
                    v.tensor_mul(sqk[HD:HD+RD, :], kt2[HD:HD+RD, buf, :],
                                 kt2[HD:HD+RD, buf, :])
                    v.tensor_mul(sqq[HD:HD+RD, :], qt2[HD:HD+RD, buf, :],
                                 qt2[HD:HD+RD, buf, :])
                self.sec("vector", f)

            def emit_vg(g4):
                gsn = self.snap()
                p0, v0 = sP.v, sV.v
                ntc = T // 128

                def f(t, g4=g4, v0=v0, ntc=ntc):
                    for tc in range(ntc):
                        if tc >= 2:
                            t.wait_ge(sV.h, v0 + tc - 1)
                        pp = ps[tc % 2]
                        for kvc in range(KVCH):
                            inst = t.matmul(pp[:, 0:4*HD], self.kv_nT[:, kvc, tc*128:(tc+1)*128],
                                            w_vd[:, kvc, g4*4*HD:(g4+1)*4*HD],
                                            start=(kvc == 0), stop=(kvc == KVCH - 1))
                        sP.t(inst)
                self.sec("tensor", f, snap=gsn, raw="free")

                def f(v, p0=p0, ntc=ntc):
                    for tc in range(ntc):
                        v.wait_ge(sP.h, p0 + tc + 1)
                        pp = ps[tc % 2]
                        dst = v_g[:, tc, :].rearrange("p (l d) -> p l d", l=4)[:, :, 0:HD]
                        src = pp[:, 0:4*HD].rearrange("p (l d) -> p l d", l=4)
                        sV.t(v.tensor_copy(dst, src))
                self.sec("vector", f, snap=gsn, raw=True)

            def emit_bound(h):
                gsn = self.snap()
                p0, v0 = sP.v, sV.v
                nsq = T // 512 + SPO

                def f(t, v0=v0, nsq=nsq):
                    for i in range(nsq):
                        if i >= 2:
                            t.wait_ge(sV.h, v0 + i - 1)
                        pp = self.psG3a[0:1, 0:512] if i % 2 == 0 else self.psG3b[0:1, 0:512]
                        if i < T // 512:
                            sP.t(t.matmul(pp, self.b_ocb[0:97, 0:1],
                                          sqk[0:97, i*512:(i+1)*512], start=True, stop=True))
                        else:
                            so = i - T // 512
                            sP.t(t.matmul(pp, self.b_ocb[0:96, 0:1],
                                          sqq[0:96, so*512:(so+1)*512], start=True, stop=True))
                self.sec("tensor", f, snap=gsn, raw="free")

                def f(v, p0=p0, nsq=nsq):
                    for i in range(nsq):
                        v.wait_ge(sP.h, p0 + i + 1)
                        pp = self.psG3a[0:1, 0:512] if i % 2 == 0 else self.psG3b[0:1, 0:512]
                        if i < T // 512:
                            if i == 0:
                                sV.t(v.tensor_copy(brow[:, 0:512], pp))
                            else:
                                sV.t(v.tensor_tensor(brow[:, 0:512], brow[:, 0:512], pp,
                                                     ALU.max))
                        else:
                            so = i - T // 512
                            sV.t(v.tensor_copy(bqn[:, so*512:(so+1)*512], pp))
                self.sec("vector", f, snap=gsn, raw=True)

            def emit_chain(h):
                buf = h % 2

                def f(v):
                    v.tensor_reduce(self.scr[0:1, 0:1], brow[:, 0:512], AX.X, ALU.max)
                    v.tensor_scalar(self.scr[0:1, 0:1], self.scr[0:1, 0:1], 1e-30, None,
                                    ALU.add)
                    v.tensor_scalar(bqn[:, :], bqn[:, :], 1e-30, None, ALU.add)
                    v.tensor_scalar(bqn[:, :], bqn[:, :], self.scr[0:1, 0:1], None,
                                    ALU.mult)
                self.sec("vector", f)

                def f(sc):
                    sc.activation(out=bqn[:, :], in_=bqn[:, :], func=AF.Sqrt)
                self.sec("scalar", f)

                def f(v):
                    v.tensor_scalar(bqn[:, :], bqn[:, :], -1.0, cfg.MARGIN,
                                    ALU.mult, ALU.add)
                    v.tensor_copy(bqnb[:, :], bqn[:, :])
                self.sec("vector", f)

                def f(g, buf=buf):
                    self.sG.t(g.dma_start(out=qt2[96:97, buf, :], in_=bqnb[0:1, :]), 16)
                self.sec("gpsimd", f)

            def emit_norm_g0(h):
                def f(g, h=h):
                    self.sG.t(g.dma_start(out=self.s_recip[h:h+1, :],
                                          in_=att_acc[HD:HD+1, :, :]
                                          .rearrange("p a b -> p (a b)")), 16)
                self.sec("gpsimd", f)
                return sG.v

            def emit_norm_g3(h):
                gsn = self.snap()
                v0, g0 = sV.v, sG.v

                def f(v, g0=g0):
                    for j in range(SPO):
                        if j >= 2:
                            v.wait_ge(sG.h, g0 + 16 * (j - 1))
                        sV.t(v.tensor_copy(att_bf[:, j % 2, :], att_acc[0:HD, j, :]))
                self.sec("vector", f, snap=gsn, raw=True)

                def f(g, h=h, v0=v0):
                    for j in range(SPO):
                        g.wait_ge(sV.h, v0 + j + 1)
                        self.sG.t(g.dma_start(out=self.aoT_sp.ap()[h*HD:(h+1)*HD,
                                                                   j*512:(j+1)*512],
                                              in_=att_bf[:, j % 2, :]), 16)
                self.sec("gpsimd", f, snap=gsn, raw="free")

            def emit_inner(h, snap0, v_guard, s4_guard):
                buf, l = h % 2, h % 4
                for j in range(SPO):
                    nk = cfg.KB[j]
                    ngr = (nk + 2) // 3
                    grp = [list(range(3*g, min(3*g+3, nk))) for g in range(ngr)]

                    def g_masked(g, grp=grp, nk=nk):
                        return [kc for kc in grp[g] if kc >= nk - 8]
                    scores_done = [0] * ngr
                    av_done = [0] * ngr
                    exp_done = [0] * ngr
                    mask_grp_done = [0] * ngr
                    pe_c = 0
                    for g in range(ngr):
                        pe_c += 1
                        scores_done[g] = pe_c
                        if g >= 1:
                            pe_c += 1
                            av_done[g - 1] = pe_c
                    pe_c += 1
                    av_done[ngr - 1] = pe_c
                    for g in range(ngr):
                        exp_done[g] = g + 1
                    mk_c = 0
                    for g in range(ngr):
                        if g_masked(g):
                            mk_c += 1
                        mask_grp_done[g] = mk_c

                    gsn = snap0 if j == 0 else self.snap()
                    p0, s0, x0 = sP.v, sS.v, sX.v
                    psets = [self.psG3a, self.psG3b]

                    def f(t, j=j, nk=nk, ngr=ngr, grp=grp, s0=s0, x0=x0, l=l, buf=buf,
                          av_done=av_done, exp_done=exp_done,
                          mask_grp_done=mask_grp_done, psets=psets, g_masked=g_masked):
                        if j == 0:
                            t.wait_ge(sV.h, v_guard)

                        def av_group(g):
                            t.wait_ge(sS.h, s0 + exp_done[g])
                            if g_masked(g):
                                t.wait_ge(sX.h, x0 + mask_grp_done[g])
                            for i, kc in enumerate(grp[g]):
                                inst = t.matmul(ps[6][0:HD+1, :],
                                                v_g[:, kc, l*(HD+1):(l+1)*(HD+1)],
                                                probs[:, g % 2, i*512:(i+1)*512],
                                                start=(kc == 0), stop=(kc == nk - 1))
                                if i == len(grp[g]) - 1:
                                    sP.t(inst)
                        for g in range(ngr):
                            if g >= 2:
                                t.wait_ge(sS.h, s0 + exp_done[g - 2])
                            pset = psets[g % 2]
                            for i, kc in enumerate(grp[g]):
                                inst = t.matmul(pset[:, i*512:(i+1)*512],
                                                kt2[0:97, buf, kc*128:(kc+1)*128],
                                                qt2[0:97, buf, j*512:(j+1)*512],
                                                start=True, stop=True)
                                if i == len(grp[g]) - 1:
                                    sP.t(inst)
                            if g >= 1:
                                av_group(g - 1)
                        av_group(ngr - 1)
                    self.sec("tensor", f, snap=gsn, raw="free")

                    def f(sc, ngr=ngr, grp=grp, p0=p0,
                          scores_done=scores_done, av_done=av_done, psets=psets):
                        for g in range(ngr):
                            sc.wait_ge(sP.h, p0 + scores_done[g])
                            if g >= 2:
                                sc.wait_ge(sP.h, p0 + av_done[g - 2])
                            w = len(grp[g])
                            self.sS.t(sc.activation(out=probs[:, g % 2, 0:w*512],
                                                    in_=psets[g % 2][:, 0:w*512], func=AF.Exp))
                    self.sec("scalar", f, snap=gsn, raw="free")

                    def f(v, nk=nk, ngr=ngr, grp=grp, s0=s0,
                          exp_done=exp_done, g_masked=g_masked):
                        for g in range(ngr):
                            mk = g_masked(g)
                            if not mk:
                                continue
                            v.wait_ge(sS.h, s0 + exp_done[g])
                            for kc in mk:
                                i = kc - grp[g][0]
                                mi = kc - (nk - 8)
                                inst = v.tensor_mul(probs[:, g % 2, i*512:(i+1)*512],
                                                    probs[:, g % 2, i*512:(i+1)*512],
                                                    self.b_mask[:, mi, :])
                                if kc == mk[-1]:
                                    self.sX.t(inst, 1)
                    self.sec("vector", f, snap=gsn, raw="free")

                    def f(v, j=j):
                        if j == 0 and s4_guard > 0:
                            v.wait_ge(sG.h, s4_guard)
                        v.tensor_copy(att_acc[0:HD+1, j, :], ps[6][0:HD+1, :])
                    self.sec("vector", f)

            emit_rope_dma(0)
            emit_build(0, self.snap())
            emit_rope_sq(0)
            emit_vg(0)
            emit_bound(0)
            snap_bp = self.snap()
            emit_chain(0)
            s4_guard = 0
            for h in range(H):
                snap0 = self.snap()
                if h + 1 < H:
                    emit_rope_dma(h + 1)
                if h >= 1:
                    s4_guard = emit_norm_g0(h - 1)
                if h + 1 < H:
                    emit_build(h + 1, snap_bp)
                    emit_rope_sq(h + 1)
                    emit_bound(h + 1)
                if h >= 1:
                    emit_norm_g3(h - 1)
                v_guard = sV.v
                emit_inner(h, snap0, v_guard, s4_guard)
                if (h + 1) % 4 == 0 and h + 1 < H:
                    emit_vg((h + 1) // 4)
                snap_bp = self.snap()
                if h + 1 < H:
                    emit_chain(h + 1)
            s4_guard = emit_norm_g0(H - 1)
            emit_norm_g3(H - 1)

            def f(v):
                v.reciprocal(self.s_recip[:, :], self.s_recip[:, :])
            self.sec("vector", f)

    def ph_wo(self):
        cfg, d = self.cfg, self.d
        TO, D, DCH, SPO = cfg.TO, cfg.D, cfg.DCH, cfg.SPO
        ps = self.ps
        sP, sV = self.sP, self.sV

        with ExitStack() as WS:
            SB = lambda n, sh, dt: WS.enter_context(self.nc.sbuf_tensor(n, list(sh), dt))
            w_o = SB("w_o", (128, DCH, D), BF16)
            ao = SB("ao", (128, DCH, 512), BF16)
            xx = SB("xx", (128, 4, D), F32)
            yt = SB("yt", (128, D), F32)

            def f(g):
                self.sG.t(g.dma_start(out=w_o[:],
                                      in_=d["W_o"].ap().rearrange("(c p) o -> p c o", p=128)), 16)
            self.sec("gpsimd", f)

            for j in range(SPO):
                def f(s, j=j):
                    for fc in range(DCH):
                        self.sD.t(s.dma_start(out=ao[:, fc, :],
                                              in_=self.aoT_sp.ap()[fc*128:(fc+1)*128,
                                                                   j*512:(j+1)*512]), 16)
                self.sec("sync", f)

                def f(sc, j=j):
                    self.sD.t(sc.dma_start(out=xx[:, :, :],
                                          in_=d["x_own"].ap()[j*512:(j+1)*512, :]
                                          .rearrange("(a p) b -> p a b", p=128)), 16)
                self.sec("scalar", f, raw=True)

                gsn = self.snap()
                p0n, v0n = sP.v, sV.v

                def f(t, j=j, v0n=v0n):
                    for fc in range(DCH):
                        if fc >= 2:
                            t.wait_ge(sV.h, v0n + fc - 1)
                        pp = ps[3] if fc % 2 == 0 else ps[4]
                        sP.t(t.matmul(pp[:, :], self.b_hsel[:, fc*128:(fc+1)*128],
                                      self.s_recip[:, j*512:(j+1)*512],
                                      start=True, stop=True))
                self.sec("tensor", f, snap=gsn, raw="free")

                def f(v, j=j, p0n=p0n):
                    for fc in range(DCH):
                        v.wait_ge(sP.h, p0n + fc + 1)
                        pp = ps[3] if fc % 2 == 0 else ps[4]
                        sV.t(v.tensor_mul(ao[:, fc, :], ao[:, fc, :], pp[:, :]))
                self.sec("vector", f, snap=gsn, raw=True)

                for tl in range(4):
                    def f(t, j=j, tl=tl):
                        for Fh in range(2):
                            pp = ps[Fh]
                            for fc in range(DCH):
                                t.matmul(pp[:, :], ao[:, fc, tl*128:(tl+1)*128],
                                         w_o[:, fc, Fh*512:(Fh+1)*512],
                                         start=(fc == 0), stop=(fc == DCH - 1))
                    self.sec("tensor", f)

                    def f(v, j=j, tl=tl):
                        for Fh in range(2):
                            v.tensor_add(yt[:, Fh*512:(Fh+1)*512], ps[Fh][:, :],
                                         xx[:, tl, Fh*512:(Fh+1)*512])
                    self.sec("vector", f)

                    def f(s, j=j, tl=tl):
                        r0 = (j * 4 + tl) * 128
                        self.sD.t(s.dma_start(out=self.y_mid.ap()[r0:r0+128, :], in_=yt[:, :]), 16)
                    self.sec("sync", f)

    def ph_moe(self):
        cfg, d = self.cfg, self.d
        D, I, CAP, E = cfg.D, cfg.I, cfg.CAP, cfg.E
        DCH, ICH, ISL, CAPT = cfg.DCH, cfg.ICH, cfg.ISL, cfg.CAPT
        ps = self.ps
        sP, sV, sS = self.sP, self.sV, self.sS

        with ExitStack() as MS:
            SB = lambda n, sh, dt: MS.enter_context(self.nc.sbuf_tensor(n, list(sh), dt))
            cmb = SB("cmb", (8, CAP), F32)
            cmb16 = SB("cmb16", (8, CAP), I16)
            cmg16 = SB("cmg16", (8, CAP), FP16)
            rowf = SB("rowf", (1, CAP), F32)
            rowg = SB("rowg", (1, CAP), F32)
            m_xg = SB("m_xg", (128, 2, 4, D), BF16)
            m_xgf = SB("m_xgf", (128, 2, DCH, 512), BF16)
            m_wi = SB("m_wi", (128, 4, DCH, 512), BF16)
            m_wo = SB("m_wo", (128, 4, 4, 512), BF16)
            m_h = SB("m_h", (128, ICH, 512), BF16)
            m_out = SB("m_out", (128, 2, 512), F32)

            _ns_mc = self.nc.named_scope("mComb"); _ns_mc.__enter__()
            def f(s):
                cid = self._vload(s, self.b_cmeta['sync'][0:1, 2:3], 0, 7)
                for r in range(8):
                    rr = s.snap(cid + 16 * r, min_val=0, max_val=127)
                    self.sD.t(s.dma_start(out=cmb16[r:r+1, :],
                                          in_=self.tbi_out.ap()[ts(rr, 1), :]), 16)
                    self.sD.t(s.dma_start(out=cmg16[r:r+1, :],
                                          in_=self.tbg_out.ap()[ts(rr, 1), :]), 16)
            self.sec("sync", f)

            def f(v):
                v.tensor_copy(cmb[:, :], cmb16[:, :])
            self.sec("vector", f)

            spans5 = [(o, min(512, CAP - o)) for o in range(0, CAP, 512)]
            for o5, w5 in spans5:
                def f(t, o5=o5, w5=w5):
                    t.matmul(ps[0][0:1, 0:w5], self.b_oc[0:8, 0:1], cmb[:, o5:o5+w5],
                             start=True, stop=True)
                self.sec("tensor", f)

                def f(v, o5=o5, w5=w5):
                    v.tensor_copy(rowf[:, o5:o5+w5], ps[0][0:1, 0:w5])
                self.sec("vector", f)

            def f(v):
                v.tensor_copy(cmb[:, :], cmg16[:, :])
            self.sec("vector", f)

            for o5, w5 in spans5:
                def f(t, o5=o5, w5=w5):
                    t.matmul(ps[0][0:1, 0:w5], self.b_oc[0:8, 0:1], cmb[:, o5:o5+w5],
                             start=True, stop=True)
                self.sec("tensor", f)

                def f(v, o5=o5, w5=w5):
                    v.tensor_copy(rowg[:, o5:o5+w5], ps[0][0:1, 0:w5])
                self.sec("vector", f)

            def f(v):
                v.tensor_scalar(rowf[:, :], rowf[:, :], 1.0, None, ALU.subtract)
                v.tensor_scalar(rowf[:, :], rowf[:, :], 0.0, None, ALU.max)
            self.sec("vector", f)

            gsn = self.snap()
            p0, v0 = self.sP.v, self.sV.v

            def f(t, p0=p0, v0=v0):
                for st in range(CAPT):
                    if st >= 2:
                        t.wait_ge(self.sV.h, v0 + 2 * (st - 1))
                    pp = self.psT if st % 2 == 0 else self.psC1
                    self.sP.t(t.transpose(pp[0:128, 0:1], rowf[:, st*128:(st+1)*128],
                                          self.b_ident[0:1, 0:1]))
                    self.sP.t(t.transpose(pp[0:128, 1:2], rowg[:, st*128:(st+1)*128],
                                          self.b_ident[0:1, 0:1]))
            self.sec("tensor", f, snap=gsn, raw=True)

            def f(v, p0=p0):
                for st in range(CAPT):
                    v.wait_ge(self.sP.h, p0 + 2 * (st + 1))
                    pp = self.psT if st % 2 == 0 else self.psC1
                    self.sV.t(v.tensor_copy(self.e_idx[:, st:st+1], pp[0:128, 0:1]))
                    self.sV.t(v.tensor_copy(self.e_gate[:, st:st+1], pp[0:128, 1:2]))
            self.sec("vector", f, snap=gsn, raw=True)

            _ns_mc.__exit__(None, None, None)
            NCh = len(cfg.SCH)
            assert all(c == 512 for c in cfg.SCH)
            sV, sG, sD = self.sV, self.sG, self.sD

            def f(g):
                for st in range(4):
                    self.sG.t(g.indirect_dma_start(
                        m_xg[:, 0, st, :], None, self.ag1_out.ap(),
                        IndirectOffsetOnAxis(self.e_idx[:, st:st+1], 0)), 16)
            self.sec("gpsimd", f)

            gsn = self.snap()
            p0, s0, v0, d0, g0 = sP.v, sS.v, sV.v, sD.v, sG.v
            P, S, V, G, D = {}, {}, {}, {}, {}
            pl = p0
            for ci in range(NCh):
                for i in range(32):
                    pl += 1; P[("tr", ci, i)] = pl
                for k in range(32):
                    pl += 1; P[("m1", ci, k)] = pl
                for Fh in range(2):
                    for isl in range(8):
                        pl += 1; P[("m2", ci, Fh, isl)] = pl
            sl = s0
            for ci in range(NCh):
                for k in range(32):
                    sl += 1; S[("relu", ci, k)] = sl
            vl = v0
            for ci in range(NCh):
                for i in range(32):
                    vl += 1; V[("te", ci, i)] = vl
                for Fh in range(2):
                    for st in range(4):
                        vl += 1; V[("gate", ci, Fh, st)] = vl
            gl = g0
            for ci in range(NCh):
                if ci + 1 < NCh:
                    gl += 64; G[("gath", ci + 1)] = gl
                for Fh in range(2):
                    for st in range(4):
                        gl += 16; G[("out", (ci * 2 + Fh) * 4 + st)] = gl
            dl = d0
            for ci in range(NCh):
                for isl in range(8):
                    dl += 16; D[("wi", ci, isl)] = dl
                for Fh in range(2):
                    for isl in range(8):
                        dl += 16; D[("wo", ci, Fh, isl)] = dl

            def f(t):
                for ci in range(NCh):
                    for i in range(32):
                        st, fc = divmod(i, DCH)
                        if i == 0 and ci >= 1:
                            t.wait_ge(sG.h, G[("gath", ci)])
                        if i >= 2:
                            t.wait_ge(sV.h, V[("te", ci, i - 2)])
                        dst = (self.psT[:, :] if i % 2 == 0
                               else ps[2]).bitcast(BF16)[0:128, 0:128]
                        sP.t(t.transpose(dst, m_xg[:, ci % 2, st, fc*128:(fc+1)*128],
                                         self.b_identb[:, :]))
                    for isl in range(8):
                        for ic2 in range(4):
                            k = isl * 4 + ic2
                            if k == 0:
                                t.wait_ge(sV.h, V[("te", ci, 31)])
                            if ic2 == 0:
                                t.wait_ge(sD.h, D[("wi", ci, isl)])
                            if k >= 2:
                                t.wait_ge(sS.h, S[("relu", ci, k - 2)])
                            elif ci > 0:
                                t.wait_ge(sS.h, S[("relu", ci - 1, 30 + k)])
                            pp = ps[k % 2]
                            for fc in range(DCH):
                                inst = t.matmul(pp[:, :],
                                                m_wi[:, (ci*8+isl) % 4, fc, ic2*128:(ic2+1)*128],
                                                m_xgf[:, ci % 2, fc, :],
                                                start=(fc == 0), stop=(fc == DCH - 1))
                            sP.t(inst)
                    for Fh in range(2):
                        if Fh == 1:
                            t.wait_ge(sV.h, V[("gate", ci, 0, 3)])
                        elif ci > 0:
                            t.wait_ge(sV.h, V[("gate", ci - 1, 1, 3)])
                        for isl in range(8):
                            t.wait_ge(sD.h, D[("wo", ci, Fh, isl)])
                            t.wait_ge(sS.h, S[("relu", ci, isl * 4 + 3)])
                            buf = (ci * 16 + Fh * 8 + isl) % 4
                            for st in range(4):
                                for ic2 in range(4):
                                    inst = t.matmul(ps[3 + st][:, :],
                                                    m_h[:, isl*4+ic2, st*128:(st+1)*128],
                                                    m_wo[:, buf, ic2, :],
                                                    start=(isl == 0 and ic2 == 0),
                                                    stop=(isl == 7 and ic2 == 3))
                            sP.t(inst)
            self.sec("tensor", f, snap=gsn, raw="free")

            def f(sc):
                for ci in range(NCh):
                    for k in range(32):
                        sc.wait_ge(sP.h, P[("m1", ci, k)])
                        self.sS.t(sc.activation(out=m_h[:, k, :], in_=ps[k % 2][:, :],
                                                func=AF.Relu))
            self.sec("scalar", f, snap=gsn, raw="free")

            def f(v):
                for ci in range(NCh):
                    for i in range(32):
                        st, fc = divmod(i, DCH)
                        v.wait_ge(sP.h, P[("tr", ci, i)])
                        src = (self.psT[:, :] if i % 2 == 0
                               else ps[2]).bitcast(BF16)[0:128, 0:128]
                        self.sV.t(v.tensor_copy(m_xgf[:, ci % 2, fc, st*128:(st+1)*128], src))
                    for Fh in range(2):
                        v.wait_ge(sP.h, P[("m2", ci, Fh, 7)])
                        for st in range(4):
                            q = (ci * 2 + Fh) * 4 + st
                            if q >= 2:
                                v.wait_ge(sG.h, G[("out", q - 2)])
                            self.sV.t(v.tensor_scalar(m_out[:, q % 2, :], ps[3 + st][:, :],
                                                      self.e_gate[:, ci*4+st:ci*4+st+1],
                                                      None, ALU.mult))
            self.sec("vector", f, snap=gsn, raw="free")

            def f(g):
                for ci in range(NCh):
                    if ci + 1 < NCh:
                        if ci >= 1:
                            g.wait_ge(sP.h, P[("tr", ci - 1, 31)])
                        for st in range(4):
                            c = (ci + 1) * 4 + st
                            self.sG.t(g.indirect_dma_start(
                                m_xg[:, (ci + 1) % 2, st, :], None, self.ag1_out.ap(),
                                IndirectOffsetOnAxis(self.e_idx[:, c:c+1], 0)), 16)
                    for Fh in range(2):
                        for st in range(4):
                            q = (ci * 2 + Fh) * 4 + st
                            g.wait_ge(sV.h, V[("gate", ci, Fh, st)])
                            r0 = (ci * 4 + st) * 128
                            self.sG.t(g.dma_start(
                                out=self.ag2_in.ap()[r0:r0+128, Fh*512:(Fh+1)*512],
                                in_=m_out[:, q % 2, :]), 16)
            self.sec("gpsimd", f, snap=gsn, raw="free")

            def f(s):
                for ci in range(NCh):
                    for isl in range(8):
                        tw = ci * 8 + isl
                        if tw >= 4:
                            c2, i2 = divmod(tw - 4, 8)
                            s.wait_ge(sP.h, P[("m1", c2, i2 * 4 + 3)])
                        self.sD.t(s.dma_start(out=m_wi[:, tw % 4, :, :],
                                              in_=self.wi_t.ap()[isl]), 16)
                    for Fh in range(2):
                        for isl in range(8):
                            to = ci * 16 + Fh * 8 + isl
                            if to >= 4:
                                c2, r2 = divmod(to - 4, 16)
                                Fh2, isl2 = divmod(r2, 8)
                                s.wait_ge(sP.h, P[("m2", c2, Fh2, isl2)])
                            self.sD.t(s.dma_start(out=m_wo[:, to % 4, :, :],
                                                  in_=self.wo_t.ap()[Fh * 8 + isl]), 16)
            self.sec("sync", f, snap=gsn, raw="free")

            def f(g):
                self.sC.t(g.collective_compute(
                    "AllGather", ALU.bypass, replica_groups=[list(range(8))],
                    ins=[self.ag2_in.ap().opt()], outs=[self.ag2_out.ap().opt()]), 1)
            self.sec("gpsimd", f)

    def ph_final(self):
        cfg = self.cfg
        TO, D = cfg.TO, cfg.D

        with ExitStack() as FS:
            SB = lambda n, sh, dt: FS.enter_context(self.nc.sbuf_tensor(n, list(sh), dt))
            yt = SB("ytf", (128, 2, D), F32)
            g2 = SB("g2", (128, 2, 2, D), F32)
            idxc = SB("idxc", (128, cfg.NT_O, 2), I32)

            def f(v):
                v.tensor_copy(idxc[:, :, :], self.g_row[:, :, :])
            self.sec("vector", f)

            for tt in range(cfg.NT_O):
                gsn = self.snap()
                b = tt % 2

                def f(s, tt=tt, b=b):
                    self.sD.t(s.dma_start(out=yt[:, b, :],
                                          in_=self.y_mid.ap()[tt*128:(tt+1)*128, :]), 16)
                self.sec("sync", f, snap=gsn)

                def f(g, tt=tt, b=b):
                    for k2 in range(2):
                        self.sG.t(g.indirect_dma_start(
                            g2[:, b, k2, :], None, self.ag2_out.ap(),
                            IndirectOffsetOnAxis(idxc[:, tt, k2:k2+1], 0)), 16)
                self.sec("gpsimd", f, snap=gsn)

                def f(v, tt=tt, b=b):
                    v.scalar_tensor_tensor(yt[:, b, :], g2[:, b, 0, :], self.g_val[:, tt, 0:1],
                                           yt[:, b, :], ALU.mult, ALU.add)
                    v.scalar_tensor_tensor(yt[:, b, :], g2[:, b, 1, :], self.g_val[:, tt, 1:2],
                                           yt[:, b, :], ALU.mult, ALU.add)
                self.sec("vector", f)

                def f(s, tt=tt, b=b):
                    self.sD.t(s.dma_start(out=self.y_out.ap()[tt*128:(tt+1)*128, :],
                                          in_=yt[:, b, :]), 16)
                self.sec("sync", f)

            def f(s):
                s.wait_ge(self.sG.h, self.sG.v)
                s.wait_ge(self.sD.h, self.sD.v)
            self.blk.sync(f)


_CFG = CFG(T=4096)
_NC_CACHE = {}


def _build(cfg):
    key = cfg.T
    if key not in _NC_CACHE:
        _NC_CACHE[key] = K(cfg).build()
    return _NC_CACHE[key]


def kernel(L0_attn_x, L0_attn_W_kv, L0_attn_g_kv, L0_attn_W_kdec, L0_attn_W_vdec,
           L0_attn_W_q, L0_attn_W_qdec, L0_attn_W_krope, L0_attn_W_qrope, L0_attn_W_o,
           L0_ffn_W_router, L0_ffn_b_router, L0_ffn_experts_inter, L0_ffn_experts_out,
           cfg=None):
    cfg = cfg or _CFG
    nc = _build(cfg)
    x = np.asarray(L0_attn_x, dtype=np.float32)
    B, T, D = x.shape
    assert (B, T, D) == (cfg.B, cfg.T, cfg.D)
    WrT = np.ascontiguousarray(np.asarray(L0_ffn_W_router, np.float32).T)

    in_maps = []
    for core in range(8):
        b, qh = core // 2, core % 2
        m = dict(host_consts(cfg, core))
        m["x_b"] = np.ascontiguousarray(x[b])
        xo = np.zeros((cfg.TO, cfg.D), np.float32)
        for j in range(cfg.SPO):
            g = 2 * j + qh
            xo[j*512:(j+1)*512] = x[b, g*512:(g+1)*512]
        m["x_own"] = xo
        m["W_kv"] = np.asarray(L0_attn_W_kv, np.float32)
        m["g_kv"] = np.asarray(L0_attn_g_kv, np.float32)
        m["W_kdec"] = np.asarray(L0_attn_W_kdec, np.float32)
        m["W_vdec"] = np.asarray(L0_attn_W_vdec, np.float32)
        m["W_q"] = np.asarray(L0_attn_W_q, np.float32)
        m["W_qdec"] = np.asarray(L0_attn_W_qdec, np.float32)
        m["W_krope"] = np.asarray(L0_attn_W_krope, np.float32)
        m["W_qrope"] = np.asarray(L0_attn_W_qrope, np.float32)
        m["W_o"] = np.asarray(L0_attn_W_o, np.float32)
        m["Wr_T"] = WrT
        m["b_r"] = np.asarray(L0_ffn_b_router, np.float32)
        m["Wi"] = np.ascontiguousarray(np.asarray(L0_ffn_experts_inter, np.float32)[core])
        m["Wo"] = np.ascontiguousarray(np.asarray(L0_ffn_experts_out, np.float32)[core])
        in_maps.append(m)

    res = run_bass_kernel_spmd(nc, in_maps, core_ids=list(range(8)))

    out = np.zeros((B, T, D), np.float32)
    for core in range(8):
        b, qh = core // 2, core % 2
        yc = res.results[core]["y"]
        for j in range(cfg.SPO):
            g = 2 * j + qh
            out[b, g*512:(g+1)*512, :] = yc[j*512:(j+1)*512, :]
    return out
